# version 07
# baseline (speedup 1.0000x reference)
import contextlib
import math
import numpy as np
import concourse.bass as bass
import concourse.mybir as mybir

F32 = mybir.dt.float32
BF16 = mybir.dt.bfloat16
I32 = mybir.dt.int32
ALU = mybir.AluOpType
AF = mybir.ActivationFunctionType
AX = mybir.AxisListType

H, DH, D, DQ = 16, 64, 1024, 3072
TCH = 512
NT = TCH // 128
KD = D // 128
NTL = 8
NRR = 64
NGR = 4
EPS_LN = 1e-5
EPS_RMS = 1e-6
EPS_ACT = 1e-5


def build_program(n_chunks, n_warm, debug_outputs=()):
    nc = bass.Bass("TRN2", target_bir_lowering=False, debug=False, detect_race_conditions=False)
    NC = n_chunks
    T_ALL = NC * TCH
    T_OUT = (NC - n_warm) * TCH

    x_in = nc.dram_tensor("x", [T_ALL, D], F32, kind="ExternalInput").ap()
    kscale_in = nc.dram_tensor("kscale", [128, NC], F32, kind="ExternalInput").ap()
    wq_in = nc.dram_tensor("wq", [D, DQ], F32, kind="ExternalInput").ap()
    wqb_in = nc.dram_tensor("wqb", [128, 24], F32, kind="ExternalInput").ap()
    wo_in = nc.dram_tensor("wo", [D, D], F32, kind="ExternalInput").ap()
    ob_in = nc.dram_tensor("ob", [128, 8], F32, kind="ExternalInput").ap()
    wf_in = nc.dram_tensor("wf", [D, D], I32, kind="ExternalInput").ap()
    fb_in = nc.dram_tensor("fb", [128, 8], F32, kind="ExternalInput").ap()
    ws_in = nc.dram_tensor("ws", [128, 1], F32, kind="ExternalInput").ap()
    lam_in = nc.dram_tensor("lam", [128, 512], F32, kind="ExternalInput").ap()
    gam_in = nc.dram_tensor("gam", [128, 512], F32, kind="ExternalInput").ap()
    wst_in = nc.dram_tensor("wst", [128, H * 64], F32, kind="ExternalInput").ap()
    esel_in = nc.dram_tensor("esel", [128, 32 * 128], F32, kind="ExternalInput").ap()
    dsel_in = nc.dram_tensor("dsel", [128, 128], F32, kind="ExternalInput").ap()
    idn_in = nc.dram_tensor("idn", [128, 128], F32, kind="ExternalInput").ap()
    out_dram = nc.dram_tensor("out", [T_OUT, D], F32, kind="ExternalOutput").ap()
    waste = nc.dram_tensor("waste", [128, D], F32).ap()
    dbg = {}
    for name, shape in (("xn", [T_ALL, D]), ("qkv", [DQ, T_ALL]), ("y", [D, T_ALL]),
                        ("xq", [D, T_ALL]), ("s", [D, T_ALL])):
        if name in debug_outputs:
            dbg[name] = nc.dram_tensor("dbg_" + name, shape, BF16, kind="ExternalOutput").ap()

    with contextlib.ExitStack() as _st:
        def sb(name, shape, dt):
            return _st.enter_context(nc.sbuf_tensor(name, shape, dt))
        def psa(name):
            return _st.enter_context(nc.psum_tensor(name, [128, 512], F32))
        def sem(name):
            return _st.enter_context(nc.semaphore(name))

        wq_sb = sb("wq_sb", [128, KD * DQ], BF16)
        wo_sb = sb("wo_sb", [128, KD * D], BF16)
        wf_sb = sb("wf_sb", [128, KD * D], BF16)
        wqb_sb = sb("wqb_sb", [128, 24], F32)
        ob_sb = sb("ob_sb", [128, 8], F32)
        fb_sb = sb("fb_sb", [128, 8], F32)
        ws_sb = sb("ws_sb", [128, 1], F32)
        rr_sb = sb("rr_sb", [128, 512], F32)
        gam_sb = sb("gam_sb", [128, 512], F32)
        wst_sb = sb("wst_sb", [128, H * 64], BF16)
        esel_sb = sb("esel_sb", [128, 32 * 128], BF16)
        dsel_sb = sb("dsel_sb", [128, 128], BF16)
        idn_sb = sb("idn_sb", [128, 128], BF16)
        kscale_sb = sb("kscale_sb", [128, NC], F32)
        eps_sb = sb("eps_sb", [128, 2], F32)
        scr_sb = sb("scr_sb", [1, 4], F32)
        xstage = sb("xstage", [128, NT * D], F32)
        cols = sb("cols", [128, 32], F32)
        xns = sb("xns", [128, 2 * D], F32)
        tmb = sb("tmb", [128, NT * D], BF16)
        ffm = sb("ffm", [128, KD * TCH], BF16)
        qkv_fm = sb("qkv_fm", [128, 3 * KD * TCH], BF16)
        kdup = sb("kdup", [128, 2 * TCH], BF16)
        qdup = sb("qdup", [128, 2 * TCH], BF16)
        gv = sb("gv", [128, NTL * TCH], BF16)
        uz = sb("uz", [128, NTL * TCH], BF16)
        scol = sb("scol", [128, 512], F32)
        ybuf = sb("ybuf", [128, KD * TCH], BF16)
        sgbuf = sb("sgbuf", [128, KD * TCH], BF16)
        gbuf = tmb
        lam_sb = sgbuf.ap().bitcast(F32)[:, 0:512]
        outstage = sb("outstage", [128, 2 * D], F32)
        pb0, pb1 = psa("pb0"), psa("pb1")
        py0, py1 = psa("py0"), psa("py1")
        pv0, pv1, pv2, pv3 = psa("pv0"), psa("pv1"), psa("pv2"), psa("pv3")
        sX, sW, sWc = sem("sX"), sem("sW"), sem("sWc")
        sW2, sW3 = sem("sW2"), sem("sW3")
        sDUPb, sOUTb = sem("sDUPb"), sem("sOUTb")
        sST, sSQ, sLN, sQF = sem("sST"), sem("sSQ"), sem("sLN"), sem("sQF")
        sTPm, sTPe, sQm, sQe = sem("sTPm"), sem("sTPe"), sem("sQm"), sem("sQe")
        sDUP = sem("sDUP")
        sVR, sGV, sU, sSC = sem("sVR"), sem("sGV"), sem("sU"), sem("sSC")
        sZ, sRD, sYE = sem("sZ"), sem("sRD"), sem("sYE")
        sFM, sSIL, sOM, sSG = sem("sFM"), sem("sSIL"), sem("sOM"), sem("sSG")
        sSGT, sRES, sOUT, sDBG = sem("sSGT"), sem("sRES"), sem("sOUT"), sem("sDBG")
        sGD = sem("sGD")
        sGD2 = sem("sGD2")
        block = _st.enter_context(nc.Block())

        pv = [pv0, pv1, pv2, pv3]
        py = [py0, py1]
        pb = [pb0, pb1]
        pbT = [pb0.ap().bitcast(BF16), pb1.ap().bitcast(BF16)]
        wstg = qkv_fm.ap().bitcast(F32)

        wloads = []
        for k in range(KD):
            wloads.append((wq_sb, k * DQ, wq_in[k * 128:(k + 1) * 128, :], 128, DQ, "f32"))
        for k in range(KD):
            wloads.append((wo_sb, k * D, wo_in[k * 128:(k + 1) * 128, :], 128, D, "f32"))
        for k in range(KD):
            wloads.append((wf_sb, k * D, wf_in[k * 128:(k + 1) * 128, :], 128, D, "i32"))
        wloads.append((wst_sb, 0, wst_in[:, :], 128, H * 64, "f32"))
        wloads.append((esel_sb, 0, esel_in[:, :], 128, 32 * 128, "f32"))
        wloads.append((dsel_sb, 0, dsel_in[:, :], 128, 128, "f32"))
        wloads.append((idn_sb, 0, idn_in[:, :], 128, 128, "f32"))
        NWL = len(wloads)

        def vtile_ap(h):
            m = 16 + h // 2
            return qkv_fm[(h % 2) * 64:(h % 2) * 64 + 64, m * TCH:(m + 1) * TCH]

        def qtile_ap(h):
            m = h // 2
            return qkv_fm[(h % 2) * 64:(h % 2) * 64 + 64, m * TCH:(m + 1) * TCH]

        def ktile_ap(h):
            m = 8 + h // 2
            return qkv_fm[(h % 2) * 64:(h % 2) * 64 + 64, m * TCH:(m + 1) * TCH]

        @block.sync
        def _(s):
            for i, (dst, off, src, rows, colsn, conv) in enumerate(wloads):
                if i >= 2:
                    s.wait_ge(sWc, i - 1)
                half = 2048 if colsn > DQ else (i % 2) * DQ
                dstage = wstg[0:rows, half:half + colsn]
                if conv == "i32":
                    dstage = dstage.bitcast(I32)
                s.dma_start(out=dstage, in_=src).then_inc(sW if i % 2 == 0 else sW2, 16)
            s.dma_start(out=wqb_sb[:, :], in_=wqb_in).then_inc(sW3, 16)
            s.dma_start(out=ob_sb[:, :], in_=ob_in).then_inc(sW3, 16)
            s.dma_start(out=fb_sb[:, :], in_=fb_in).then_inc(sW3, 16)
            s.dma_start(out=ws_sb[:, :], in_=ws_in).then_inc(sW3, 16)
            s.dma_start(out=lam_sb, in_=lam_in).then_inc(sW3, 16)
            s.dma_start(out=gam_sb[:, :], in_=gam_in).then_inc(sW3, 16)
            s.dma_start(out=kscale_sb[:, :], in_=kscale_in).then_inc(sW3, 16)
            for c in range(NC):
                if c >= 1:
                    s.wait_ge(sRES, 4 * c)
                s.dma_start(out=xstage[:, :], in_=x_in[c * TCH:(c + 1) * TCH, :]
                            .rearrange("(n p) d -> p n d", p=128)).then_inc(sX, 16)
                if "xn" in dbg:
                    s.wait_ge(sLN, c + 1)
                    s.dma_start(out=dbg["xn"][c * TCH:(c + 1) * TCH, :]
                                .rearrange("(n p) d -> p n d", p=128), in_=tmb[:, :]).then_inc(sDBG, 16)
                for h in range(H):
                    gh = 16 * c + h
                    s.wait_ge(sQe, 24 * c + 16 + h // 2 + 1)
                    if gh >= 2:
                        s.wait_ge(sU, 16 * (gh - 1))
                        s.wait_ge(sZ, 16 * (gh - 1))
                    slot = (h % 2) * TCH
                    dsem = sDUP if h % 2 == 0 else sDUPb
                    s.dma_start(out=kdup[0:64, slot:slot + TCH], in_=ktile_ap(h)).then_inc(dsem, 16)
                    s.dma_start(out=kdup[64:128, slot:slot + TCH], in_=ktile_ap(h)).then_inc(dsem, 16)
                    s.dma_start(out=qdup[0:64, slot:slot + TCH], in_=qtile_ap(h)).then_inc(dsem, 16)
                    s.dma_start(out=qdup[64:128, slot:slot + TCH], in_=qtile_ap(h)).then_inc(dsem, 16)
                if "qkv" in dbg:
                    s.wait_ge(sQe, 24 * (c + 1))
                    s.dma_start(out=dbg["qkv"][:, c * TCH:(c + 1) * TCH]
                                .rearrange("(m p) t -> p m t", p=128), in_=qkv_fm[:, :]).then_inc(sDBG, 16)
                if "y" in dbg:
                    s.wait_ge(sYE, 8 * (c + 1))
                    s.dma_start(out=dbg["y"][:, c * TCH:(c + 1) * TCH]
                                .rearrange("(m p) t -> p m t", p=128), in_=ybuf[:, :]).then_inc(sDBG, 16)
                if "xq" in dbg:
                    s.wait_ge(sTPe, 16 * c + 16)
                    s.dma_start(out=dbg["xq"][:, c * TCH:(c + 1) * TCH]
                                .rearrange("(m p) t -> p m t", p=128), in_=ffm[:, :]).then_inc(sDBG, 16)
                if "s" in dbg:
                    s.wait_ge(sSG, 8 * (c + 1))
                    s.dma_start(out=dbg["s"][:, c * TCH:(c + 1) * TCH]
                                .rearrange("(m p) t -> p m t", p=128), in_=sgbuf[:, :]).then_inc(sDBG, 16)
                for tt in range(NT):
                    s.wait_ge(sRES, 4 * c + tt + 1)
                    osem = sOUT if tt % 2 == 0 else sOUTb
                    if c >= n_warm:
                        co = c - n_warm
                        s.dma_start(out=out_dram[co * TCH + tt * 128: co * TCH + (tt + 1) * 128, :],
                                    in_=outstage[:, (tt % 2) * D:(tt % 2 + 1) * D]).then_inc(osem, 16)
                    else:
                        s.dma_start(out=waste[:, :],
                                    in_=outstage[:, (tt % 2) * D:(tt % 2 + 1) * D]).then_inc(osem, 16)
            s.wait_ge(sOUT, 16 * NC * 2)
            s.wait_ge(sOUTb, 16 * NC * 2)

        @block.vector
        def _(v):
            for i, (dst, off, src, rows, colsn, conv) in enumerate(wloads):
                v.wait_ge(sW if i % 2 == 0 else sW2, 16 * (i // 2 + 1))
                half = 2048 if colsn > DQ else (i % 2) * DQ
                src_ap = wstg[0:rows, half:half + colsn]
                if conv == "i32":
                    src_ap = src_ap.bitcast(I32)
                ins = v.tensor_copy(dst[0:rows, off:off + colsn], src_ap)
                if i >= NWL - 2:
                    ins.then_inc(sGD, 1)
                    v.wait_ge(sGD, i - (NWL - 2) + 1)
                    v.engine_nop().then_inc(sWc, 1)
                else:
                    ins.then_inc(sWc, 1)
            v.memset(scol[:, :], 0.0)
            v.memset(cols[:, :], 0.0)
            v.memset(eps_sb[:, 0:1], EPS_LN)
            v.memset(eps_sb[:, 1:2], EPS_RMS)
            v.wait_ge(sW3, 16 * 7)

            GDBASE = [0]
            GDTOT = [2]
            for c in range(NC):
                GDBASE[0] = GDTOT[0]
                v.wait_ge(sX, 16 * (c + 1))
                if dbg and c >= 1:
                    v.wait_ge(sDBG, 16 * len(dbg) * c)
                gd = [0]
                def seal(ins):
                    gd[0] += 1
                    ins.then_inc(sGD, 1)
                    v.wait_ge(sGD, GDBASE[0] + gd[0])
                def seal_sig(ins, sigsem, inc=1):
                    seal(ins)
                    v.engine_nop().then_inc(sigsem, inc)
                v.wait_ge(sOUT, 16 * 2 * c)
                v.wait_ge(sOUTb, 16 * 2 * c)
                for tt in range(NT):
                    xs = xstage[:, tt * D:(tt + 1) * D]
                    seal(v.reduce_sum(cols[:, 0 + tt:1 + tt], xs, AX.X))
                seal(v.tensor_scalar(cols[:, 0:NT], cols[:, 0:NT], 1.0 / D, None, ALU.mult))
                for tt in range(NT):
                    xs = xstage[:, tt * D:(tt + 1) * D]
                    xslot = xns[:, (tt % 2) * D:(tt % 2 + 1) * D]
                    oslot = outstage[:, (tt % 2) * D:(tt % 2 + 1) * D]
                    v.tensor_scalar(xslot, xs, cols[:, 0 + tt:1 + tt], None, ALU.subtract)
                    seal(v.tensor_tensor(oslot, xslot, xslot, ALU.mult))
                    seal(v.reduce_sum(cols[:, 4 + tt:5 + tt], oslot, AX.X))
                seal_sig(v.tensor_scalar(cols[:, 4:4 + NT], cols[:, 4:4 + NT], 1.0 / D, None, ALU.mult), sST)
                v.wait_ge(sSQ, 2 * c + 1)
                seal(v.reciprocal(cols[:, 12:12 + NT], cols[:, 8:8 + NT]))
                for tt in range(NT):
                    xs = xstage[:, tt * D:(tt + 1) * D]
                    xslot = xns[:, (tt % 2) * D:(tt % 2 + 1) * D]
                    v.tensor_scalar(xslot, xs, cols[:, 0 + tt:1 + tt],
                                    cols[:, 12 + tt:13 + tt], ALU.subtract, ALU.mult)
                    ins = v.tensor_copy(tmb[:, tt * D:(tt + 1) * D], xslot)
                    if tt == NT - 1:
                        seal_sig(ins, sLN)
                    v.reduce_max(cols[:, 16 + tt:17 + tt], xslot, AX.X,
                                 apply_absolute_value=True)
                for g8 in range(KD):
                    v.wait_ge(sTPm, 16 * c + g8 + 1)
                    ins = v.tensor_copy(ffm[:, g8 * TCH:(g8 + 1) * TCH],
                                        pbT[g8 % 2][:, 0:512])
                    if g8 == KD - 1:
                        seal_sig(ins, sTPe)
                    else:
                        ins.then_inc(sTPe, 1)
                if c >= 1:
                    v.wait_ge(sDUP, 512 * c)
                    v.wait_ge(sDUPb, 512 * c)
                for m in range(24):
                    v.wait_ge(sQm, 24 * c + m + 1)
                    bank = pb[m % 2]
                    if 8 <= m < 16:
                        ins = v.tensor_scalar(qkv_fm[:, m * TCH:(m + 1) * TCH], bank[:, :],
                                              wqb_sb[:, m:m + 1], kscale_sb[:, c:c + 1],
                                              ALU.add, ALU.mult)
                    else:
                        ins = v.tensor_scalar(qkv_fm[:, m * TCH:(m + 1) * TCH], bank[:, :],
                                              wqb_sb[:, m:m + 1], None, ALU.add)
                    if m >= 16:
                        seal_sig(ins, sQe)
                    else:
                        ins.then_inc(sQe, 1)
                seal(v.tensor_tensor(cols[:, 20:20 + NT], cols[:, 12:12 + NT],
                                     cols[:, 12:12 + NT], ALU.mult))
                seal_sig(v.tensor_tensor(cols[:, 20:20 + NT], cols[:, 20:20 + NT],
                                         cols[:, 4:4 + NT], ALU.mult), sST)
                v.wait_ge(sSQ, 2 * c + 2)
                seal(v.reciprocal(cols[:, 28:28 + NT], cols[:, 8:8 + NT]))
                seal(v.tensor_tensor(cols[:, 16:16 + NT], cols[:, 16:16 + NT],
                                     cols[:, 28:28 + NT], ALU.mult))
                seal(v.tensor_scalar(cols[:, 16:16 + NT], cols[:, 16:16 + NT], EPS_ACT, None, ALU.max))
                seal(v.reciprocal(cols[:, 24:24 + NT], cols[:, 16:16 + NT]))
                seal(v.tensor_scalar(cols[:, 24:24 + NT], cols[:, 24:24 + NT], 127.0, None, ALU.mult))
                seal(v.tensor_tensor(cols[:, 28:28 + NT], cols[:, 28:28 + NT],
                                     cols[:, 12:12 + NT], ALU.mult))
                seal(v.tensor_tensor(cols[:, 28:28 + NT], cols[:, 28:28 + NT],
                                     cols[:, 24:24 + NT], ALU.mult))
                seal(v.tensor_scalar(cols[:, 16:16 + NT], cols[:, 16:16 + NT], ws_sb[:, 0:1], None, ALU.mult))
                seal(v.tensor_scalar(cols[:, 16:16 + NT], cols[:, 16:16 + NT], 1.0 / 127.0, None, ALU.mult))
                v.wait_ge(sTPm, 16 * c + 8)
                if "xn" in dbg:
                    v.wait_ge(sDBG, 16 * (len(dbg) * c + 1))
                for tt in range(NT):
                    xs = xstage[:, tt * D:(tt + 1) * D]
                    xslot = xns[:, (tt % 2) * D:(tt % 2 + 1) * D]
                    oslot = outstage[:, (tt % 2) * D:(tt % 2 + 1) * D]
                    v.tensor_scalar(xslot, xs, cols[:, 0 + tt:1 + tt],
                                    cols[:, 28 + tt:29 + tt], ALU.subtract, ALU.mult)
                    v.tensor_scalar(xslot, xslot, 128.5, None, ALU.add)
                    for kb in range(7, -1, -1):
                        v.tensor_scalar(oslot, xslot, float(1 << kb), float(1 << kb),
                                        ALU.is_ge, ALU.mult)
                        v.tensor_tensor(xslot, xslot, oslot, ALU.subtract)
                    v.tensor_scalar(oslot, xs, cols[:, 0 + tt:1 + tt],
                                    cols[:, 28 + tt:29 + tt], ALU.subtract, ALU.mult)
                    v.tensor_tensor(xslot, oslot, xslot, ALU.subtract)
                    v.tensor_scalar(xslot, xslot, 0.5, -128.0, ALU.add, ALU.max)
                    v.tensor_scalar(xslot, xslot, 127.0, cols[:, 16 + tt:17 + tt],
                                    ALU.min, ALU.mult)
                    ins = v.tensor_copy(tmb[:, tt * D:(tt + 1) * D], xslot)
                    if tt == NT - 1:
                        seal_sig(ins, sQF)
                for g8 in range(KD):
                    v.wait_ge(sTPm, 16 * c + 8 + g8 + 1)
                    ins = v.tensor_copy(ffm[:, g8 * TCH:(g8 + 1) * TCH],
                                        pbT[(8 + g8) % 2][:, 0:512])
                    if g8 == KD - 1:
                        seal_sig(ins, sTPe)
                    else:
                        ins.then_inc(sTPe, 1)
                for rr in range(NRR):
                    h = rr // 4
                    qq = rr % 4
                    qsl = qdup[:, (h % 2) * TCH:(h % 2 + 1) * TCH]
                    ksl = kdup[:, (h % 2) * TCH:(h % 2 + 1) * TCH]
                    for g in range(NGR):
                        GG = 256 * c + 4 * rr + g
                        v.wait_ge(sGV, GG + 1)
                        if g == 0 and rr % 4 == 0:
                            v.wait_ge(sDUP if h % 2 == 0 else sDUPb, 64 * (8 * c + h // 2 + 1))
                        for t2 in range(2):
                            k = g * 2 + t2
                            sl = slice(k * TCH, (k + 1) * TCH)
                            ins = v.tensor_tensor(uz[:, sl], gv[:, sl], ksl, ALU.mult)
                            if t2 == 1:
                                ins.then_inc(sU, 1)
                        for t2 in range(2):
                            k = g * 2 + t2
                            m = qq * 8 + k
                            ct = h * 32 + m
                            sl = slice(k * TCH, (k + 1) * TCH)
                            v.tensor_tensor_scan(gv[:, sl],
                                                 rr_sb[:, ct:ct + 1].broadcast_to([128, TCH]),
                                                 uz[:, sl], scol[:, ct:ct + 1],
                                                 ALU.mult, ALU.add)
                        if GG >= 4:
                            v.wait_ge(sRD, GG - 3)
                        for t2 in range(2):
                            k = g * 2 + t2
                            sl = slice(k * TCH, (k + 1) * TCH)
                            ins = v.tensor_tensor(uz[:, sl], gv[:, sl], qsl, ALU.mult)
                            if t2 == 1:
                                ins.then_inc(sZ, 1)
                    if rr >= 9 and (rr - 9) % 8 == 0:
                        p = (rr - 9) // 8
                        v.wait_ge(sRD, 256 * c + 32 * (p + 1))
                        seal_sig(v.tensor_copy(ybuf[:, p * TCH:(p + 1) * TCH], py[p % 2][:, :]), sYE)
                p = 7
                v.wait_ge(sRD, 256 * c + 32 * (p + 1))
                seal_sig(v.tensor_copy(ybuf[:, p * TCH:(p + 1) * TCH], py[p % 2][:, :]), sYE)
                for m in range(8):
                    v.wait_ge(sOM, 8 * c + m + 1)
                    v.wait_ge(sSIL, 8 * c + m + 1)
                    seal_sig(v.scalar_tensor_tensor(sgbuf[:, m * TCH:(m + 1) * TCH], pb[m % 2][:, :],
                                                    ob_sb[:, m:m + 1], gbuf[:, m * TCH:(m + 1) * TCH],
                                                    ALU.add, ALU.mult), sSG)
                for tt in range(NT):
                    v.wait_ge(sSGT, 4 * c + tt + 1)
                    if 2 * c + tt // 2 >= 1:
                        v.wait_ge(sOUT if tt % 2 == 0 else sOUTb, 16 * (2 * c + tt // 2))
                    oslot = outstage[:, (tt % 2) * D:(tt % 2 + 1) * D]
                    v.tensor_tensor(oslot[:, 0:512], pbT[0][:, 0:512],
                                    xstage[:, tt * D:tt * D + 512], ALU.add)
                    seal_sig(v.tensor_tensor(oslot[:, 512:1024], pbT[1][:, 0:512],
                                             xstage[:, tt * D + 512:(tt + 1) * D], ALU.add), sRES)
                GDTOT[0] = GDBASE[0] + gd[0]

        @block.scalar
        def _(sc):
            gd2 = [0]
            def seal_sig2(ins, sigsem):
                gd2[0] += 1
                ins.then_inc(sGD2, 1)
                sc.wait_ge(sGD2, gd2[0])
                sc.mul(scr_sb[0:1, 0:1], scr_sb[0:1, 0:1], 1.0).then_inc(sigsem, 1)
            sc.wait_ge(sW3, 16 * 7)
            sc.copy(scr_sb[0:1, 0:1], eps_sb[0:1, 0:1])
            sc.activation(rr_sb[:, :], lam_sb, AF.Sigmoid, scale=-1.0)
            for c in range(NC):
                sc.wait_ge(sST, 2 * c + 1)
                seal_sig2(sc.activation(cols[:, 8:8 + NT], cols[:, 4:4 + NT], AF.Sqrt,
                                        bias=eps_sb[:, 0:1]), sSQ)
                sc.wait_ge(sST, 2 * c + 2)
                seal_sig2(sc.activation(cols[:, 8:8 + NT], cols[:, 20:20 + NT], AF.Sqrt,
                                        bias=eps_sb[:, 1:2]), sSQ)
                for m in range(8):
                    sc.wait_ge(sFM, 8 * c + m + 1)
                    seal_sig2(sc.activation(gbuf[:, m * TCH:(m + 1) * TCH], pb[m % 2][:, :],
                                            AF.Silu, bias=fb_sb[:, m:m + 1]), sSIL)
                for rr in range(NRR):
                    h = rr // 4
                    qq = rr % 4
                    for g in range(NGR):
                        GG = 256 * c + 4 * rr + g
                        R = c * NRR + rr
                        sc.wait_ge(sVR, GG + 1)
                        if GG >= 4:
                            sc.wait_ge(sZ, GG - 3)
                            rp = (R - 1) % NRR
                            ct0 = (rp // 4) * 32 + (rp % 4) * 8 + 2 * g
                            k0 = 2 * g
                            sc.activation(scol[:, ct0:ct0 + 2],
                                          gv[:, k0 * TCH + TCH - 1:(k0 + 2) * TCH:TCH],
                                          AF.Copy)
                        for t2 in range(2):
                            k = g * 2 + t2
                            m = qq * 8 + k
                            ct = h * 32 + m
                            bank = pv[(g % 2) * 2 + t2]
                            ins = sc.activation(gv[:, k * TCH:(k + 1) * TCH], bank[:, :],
                                                AF.Copy, scale=gam_sb[:, ct:ct + 1])
                            if t2 == 1:
                                ins.then_inc(sGV, 1)

        @block.tensor
        def _(t):
            t.wait_ge(sWc, NWL)
            for c in range(NC):
                t.wait_ge(sLN, c + 1)
                if c >= 1:
                    t.wait_ge(sRES, 4 * c)
                for g8 in range(KD):
                    if g8 >= 2:
                        t.wait_ge(sTPe, 16 * c + g8 - 1)
                    bank = pbT[g8 % 2]
                    for tt in range(NT):
                        ins = t.transpose(bank[:, tt * 128:(tt + 1) * 128],
                                          tmb[:, tt * D + g8 * 128: tt * D + (g8 + 1) * 128],
                                          idn_sb[:, :])
                        if tt == NT - 1:
                            ins.then_inc(sTPm, 1)
                t.wait_ge(sTPe, 16 * c + 8)
                for m in range(24):
                    if m >= 2:
                        t.wait_ge(sQe, 24 * c + m - 1)
                    bank = pb[m % 2]
                    for k in range(KD):
                        ins = t.matmul(bank[:, :], wq_sb[:, k * DQ + m * 128: k * DQ + (m + 1) * 128],
                                       ffm[:, k * TCH:(k + 1) * TCH],
                                       start=(k == 0), stop=(k == KD - 1))
                        if k == KD - 1:
                            ins.then_inc(sQm, 1)
                t.wait_ge(sQF, c + 1)
                for g8 in range(KD):
                    if g8 >= 2:
                        t.wait_ge(sTPe, 16 * c + 8 + g8 - 1)
                    bank = pbT[(8 + g8) % 2]
                    for tt in range(NT):
                        ins = t.transpose(bank[:, tt * 128:(tt + 1) * 128],
                                          tmb[:, tt * D + g8 * 128: tt * D + (g8 + 1) * 128],
                                          idn_sb[:, :])
                        if tt == NT - 1:
                            ins.then_inc(sTPm, 1)

                def emit_vrep(rr):
                    h = rr // 4
                    qq = rr % 4
                    if qq == 0:
                        t.wait_ge(sQe, 24 * c + 16 + h // 2 + 1)
                    for g in range(NGR):
                        GG = 256 * c + 4 * rr + g
                        if GG >= 2:
                            t.wait_ge(sGV, GG - 1)
                        for t2 in range(2):
                            k = g * 2 + t2
                            m = qq * 8 + k
                            bank = pv[(g % 2) * 2 + t2]
                            eb = (h % 2) * 64
                            ins = t.matmul(bank[:, :], esel_sb[eb:eb + 64, m * 128:(m + 1) * 128],
                                           vtile_ap(h), start=True, stop=True)
                            if t2 == 1:
                                ins.then_inc(sVR, 1)

                def emit_reduce(rr):
                    h = rr // 4
                    qq = rr % 4
                    p = h // 2
                    rowbase = (h % 2) * 64
                    bank = py[p % 2]
                    if qq == 0:
                        gp = 8 * c + p
                        if h % 2 == 0 and gp >= 2:
                            t.wait_ge(sYE, gp - 1)
                        wb = (h % 2) * 64
                        t.matmul(bank[rowbase:rowbase + 64, :],
                                 wst_sb[wb:wb + 64, h * 64:(h + 1) * 64], qtile_ap(h),
                                 start=True, stop=False, skip_group_check=True)
                    for g in range(NGR):
                        GG = 256 * c + 4 * rr + g
                        t.wait_ge(sZ, GG + 1)
                        for t2 in range(2):
                            k = g * 2 + t2
                            m = qq * 8 + k
                            last = (qq == 3 and k == 7)
                            ins = t.matmul(bank[rowbase:rowbase + 64, :],
                                           dsel_sb[:, 64 - 2 * m:128 - 2 * m],
                                           uz[:, k * TCH:(k + 1) * TCH],
                                           start=False, stop=last, skip_group_check=True)
                            if t2 == 1:
                                ins.then_inc(sRD, 1)

                t.wait_ge(sTPe, 16 * c + 16)
                for m in range(8):
                    if m == 0 and c >= 1:
                        t.wait_ge(sSG, 8 * c)
                    if m >= 2:
                        t.wait_ge(sSIL, 8 * c + m - 1)
                    bank = pb[m % 2]
                    for k in range(KD):
                        ins = t.matmul(bank[:, :], wf_sb[:, k * D + m * 128: k * D + (m + 1) * 128],
                                       ffm[:, k * TCH:(k + 1) * TCH],
                                       start=(k == 0), stop=(k == KD - 1))
                        if k == KD - 1:
                            ins.then_inc(sFM, 1)
                emit_vrep(0)
                for rr in range(1, NRR):
                    emit_vrep(rr)
                    emit_reduce(rr - 1)
                emit_reduce(NRR - 1)
                for m in range(8):
                    t.wait_ge(sSIL, 8 * c + m + 1)
                    bank = pb[m % 2]
                    for k in range(KD):
                        t.wait_ge(sYE, 8 * c + k + 1)
                        ins = t.matmul(bank[:, :], wo_sb[:, k * D + m * 128: k * D + (m + 1) * 128],
                                       ybuf[:, k * TCH:(k + 1) * TCH],
                                       start=(k == 0), stop=(k == KD - 1))
                        if k == KD - 1:
                            ins.then_inc(sOM, 1)
                t.wait_ge(sSG, 8 * (c + 1))
                for tt in range(NT):
                    if tt >= 1:
                        t.wait_ge(sRES, 4 * c + tt)
                    for o8 in range(KD):
                        bank = pbT[o8 // 4]
                        ins = t.transpose(bank[:, (o8 % 4) * 128:(o8 % 4 + 1) * 128],
                                          sgbuf[:, o8 * TCH + tt * 128: o8 * TCH + (tt + 1) * 128],
                                          idn_sb[:, :])
                        if o8 == KD - 1:
                            ins.then_inc(sSGT, 1)

    return nc


def host_weights(inputs):
    W = {}
    W["wq"] = np.ascontiguousarray(np.asarray(inputs["L0_attn_Wqkv_w"], np.float32).T)
    W["wqb"] = np.ascontiguousarray(np.asarray(inputs["L0_attn_Wqkv_b"], np.float32).reshape(24, 128).T)
    W["wo"] = np.ascontiguousarray(np.asarray(inputs["L0_attn_out_w"], np.float32).T)
    W["ob"] = np.ascontiguousarray(np.asarray(inputs["L0_attn_out_b"], np.float32).reshape(8, 128).T)
    W["wf"] = np.ascontiguousarray(np.asarray(inputs["L0_ffn_w_int8"], np.int32).T)
    W["fb"] = np.ascontiguousarray(np.asarray(inputs["L0_ffn_bias"], np.float32).reshape(8, 128).T)
    W["ws"] = np.ascontiguousarray(np.broadcast_to(
        np.asarray(inputs["L0_ffn_weight_scale"], np.float32).reshape(1, 1), (128, 1)))
    lam = np.asarray(inputs["L0_attn_Lambda_raw"], np.float32)
    gam = np.asarray(inputs["L0_attn_Gamma"], np.float32)
    W["lam"] = np.ascontiguousarray(lam.reshape(H, 32, 2, 64).transpose(2, 3, 0, 1).reshape(128, 512))
    W["gam"] = np.ascontiguousarray(gam.reshape(H, 32, 2, 64).transpose(2, 3, 0, 1).reshape(128, 512))
    wst = np.asarray(inputs["L0_attn_W_static"], np.float32)
    wstj = wst.transpose(2, 0, 1).reshape(64, H * 64)
    W["wst"] = np.ascontiguousarray(np.concatenate([wstj, wstj], 0))
    esel = np.zeros((64, 32 * 128), np.float32)
    for m in range(32):
        esel[2 * m, m * 128:m * 128 + 64] = 1.0
        esel[2 * m + 1, m * 128 + 64:m * 128 + 128] = 1.0
    W["esel"] = np.concatenate([esel, esel], 0)
    dsel = np.zeros((128, 128), np.float32)
    for p2 in range(128):
        dsel[p2, 64 + p2 // 64] = 1.0
    W["dsel"] = dsel
    W["idn"] = np.eye(128, dtype=np.float32)
    return W


N_WARM = 2
N_CHUNKS = 10
_CACHE = {}


def _get_nc():
    if "nc" not in _CACHE:
        _CACHE["nc"] = build_program(N_CHUNKS, N_WARM)
    return _CACHE["nc"]


def kernel(L0_attn_x, L0_attn_Wqkv_w, L0_attn_Wqkv_b, L0_attn_out_w, L0_attn_out_b,
           L0_attn_W_static, L0_attn_Lambda_raw, L0_attn_Gamma, L0_ffn_w_int8,
           L0_ffn_weight_scale, L0_ffn_bias):
    from concourse.bass_utils import run_bass_kernel_spmd
    inputs = dict(L0_attn_Wqkv_w=L0_attn_Wqkv_w, L0_attn_Wqkv_b=L0_attn_Wqkv_b,
                  L0_attn_out_w=L0_attn_out_w, L0_attn_out_b=L0_attn_out_b,
                  L0_attn_W_static=L0_attn_W_static, L0_attn_Lambda_raw=L0_attn_Lambda_raw,
                  L0_attn_Gamma=L0_attn_Gamma, L0_ffn_w_int8=L0_ffn_w_int8,
                  L0_ffn_weight_scale=L0_ffn_weight_scale, L0_ffn_bias=L0_ffn_bias)
    W = host_weights(inputs)
    x = np.asarray(L0_attn_x, np.float32)
    B, L, _ = x.shape
    TQ = L // 4
    WARM = N_WARM * TCH
    in_maps = []
    for c in range(8):
        b, q = c // 4, c % 4
        ks = np.full((128, N_CHUNKS), 0.125, np.float32)
        if q == 0:
            warm = x[b, 0:WARM]
            ks[:, :N_WARM] = 0.0
        else:
            warm = x[b, q * TQ - WARM: q * TQ]
        xc = np.ascontiguousarray(np.concatenate([warm, x[b, q * TQ:(q + 1) * TQ]], 0))
        in_maps.append(dict(x=xc, kscale=ks, **W))
    nc = _get_nc()
    res = run_bass_kernel_spmd(nc, in_maps, core_ids=list(range(8)))
    parts = [res.results[c]["out"] for c in range(8)]
    out = np.empty((B, L, D), np.float32)
    for c in range(8):
        b, q = c // 4, c % 4
        out[b, q * TQ:(q + 1) * TQ] = np.asarray(parts[c], np.float32)
    return out


# version 08
# speedup vs baseline: 1.0037x; 1.0037x over previous
import contextlib
import math
import numpy as np
import concourse.bass as bass
import concourse.mybir as mybir

F32 = mybir.dt.float32
BF16 = mybir.dt.bfloat16
I32 = mybir.dt.int32
ALU = mybir.AluOpType
AF = mybir.ActivationFunctionType
AX = mybir.AxisListType

H, DH, D, DQ = 16, 64, 1024, 3072
TCH = 512
NT = TCH // 128
KD = D // 128
NTL = 8
NRR = 64
NGR = 4
EPS_LN = 1e-5
EPS_RMS = 1e-6
EPS_ACT = 1e-5


def build_program(n_chunks, n_warm, debug_outputs=()):
    nc = bass.Bass("TRN2", target_bir_lowering=False, debug=False, detect_race_conditions=False)
    NC = n_chunks
    T_ALL = NC * TCH
    T_OUT = (NC - n_warm) * TCH

    x_in = nc.dram_tensor("x", [T_ALL, D], F32, kind="ExternalInput").ap()
    kscale_in = nc.dram_tensor("kscale", [128, NC], F32, kind="ExternalInput").ap()
    wq_in = nc.dram_tensor("wq", [D, DQ], F32, kind="ExternalInput").ap()
    wqb_in = nc.dram_tensor("wqb", [128, 24], F32, kind="ExternalInput").ap()
    wo_in = nc.dram_tensor("wo", [D, D], F32, kind="ExternalInput").ap()
    ob_in = nc.dram_tensor("ob", [128, 8], F32, kind="ExternalInput").ap()
    wf_in = nc.dram_tensor("wf", [D, D], I32, kind="ExternalInput").ap()
    fb_in = nc.dram_tensor("fb", [128, 8], F32, kind="ExternalInput").ap()
    ws_in = nc.dram_tensor("ws", [128, 1], F32, kind="ExternalInput").ap()
    lam_in = nc.dram_tensor("lam", [128, 512], F32, kind="ExternalInput").ap()
    gam_in = nc.dram_tensor("gam", [128, 512], F32, kind="ExternalInput").ap()
    wst_in = nc.dram_tensor("wst", [128, H * 64], F32, kind="ExternalInput").ap()
    esel_in = nc.dram_tensor("esel", [128, 32 * 128], F32, kind="ExternalInput").ap()
    dsel_in = nc.dram_tensor("dsel", [128, 128], F32, kind="ExternalInput").ap()
    idn_in = nc.dram_tensor("idn", [128, 128], F32, kind="ExternalInput").ap()
    out_dram = nc.dram_tensor("out", [T_OUT, D], F32, kind="ExternalOutput").ap()
    waste = nc.dram_tensor("waste", [128, D], F32).ap()
    dbg = {}
    for name, shape in (("xn", [T_ALL, D]), ("qkv", [DQ, T_ALL]), ("y", [D, T_ALL]),
                        ("xq", [D, T_ALL]), ("s", [D, T_ALL])):
        if name in debug_outputs:
            dbg[name] = nc.dram_tensor("dbg_" + name, shape, BF16, kind="ExternalOutput").ap()

    with contextlib.ExitStack() as _st:
        def sb(name, shape, dt):
            return _st.enter_context(nc.sbuf_tensor(name, shape, dt))
        def psa(name):
            return _st.enter_context(nc.psum_tensor(name, [128, 512], F32))
        def sem(name):
            return _st.enter_context(nc.semaphore(name))

        wq_sb = sb("wq_sb", [128, KD * DQ], BF16)
        wo_sb = sb("wo_sb", [128, KD * D], BF16)
        wf_sb = sb("wf_sb", [128, KD * D], BF16)
        wqb_sb = sb("wqb_sb", [128, 24], F32)
        ob_sb = sb("ob_sb", [128, 8], F32)
        fb_sb = sb("fb_sb", [128, 8], F32)
        ws_sb = sb("ws_sb", [128, 1], F32)
        rr_sb = sb("rr_sb", [128, 512], F32)
        gam_sb = sb("gam_sb", [128, 512], F32)
        wst_sb = sb("wst_sb", [128, H * 64], BF16)
        esel_sb = sb("esel_sb", [128, 32 * 128], BF16)
        dsel_sb = sb("dsel_sb", [128, 128], BF16)
        idn_sb = sb("idn_sb", [128, 128], BF16)
        kscale_sb = sb("kscale_sb", [128, NC], F32)
        eps_sb = sb("eps_sb", [128, 2], F32)
        scr_sb = sb("scr_sb", [1, 4], F32)
        xstage = sb("xstage", [128, NT * D], F32)
        stats = sb("stats", [128, NT * 12], F32)
        cols = sb("cols", [128, 32], F32)
        xns = sb("xns", [128, 2 * D], F32)
        tmb = sb("tmb", [128, NT * D], BF16)
        ffm = sb("ffm", [128, KD * TCH], BF16)
        qkv_fm = sb("qkv_fm", [128, 3 * KD * TCH], BF16)
        kdup = sb("kdup", [128, 2 * TCH], BF16)
        qdup = sb("qdup", [128, 2 * TCH], BF16)
        gv = sb("gv", [128, NTL * TCH], BF16)
        uz = sb("uz", [128, NTL * TCH], BF16)
        scol = sb("scol", [128, 512], F32)
        ybuf = sb("ybuf", [128, KD * TCH], BF16)
        gbuf = sb("gbuf", [128, 2 * TCH], BF16)
        sgbuf = sb("sgbuf", [128, KD * TCH], BF16)
        lam_sb = sgbuf.ap().bitcast(F32)[:, 0:512]
        outstage = sb("outstage", [128, 2 * D], F32)
        pb0, pb1 = psa("pb0"), psa("pb1")
        py0, py1 = psa("py0"), psa("py1")
        pv0, pv1, pv2, pv3 = psa("pv0"), psa("pv1"), psa("pv2"), psa("pv3")
        sX, sW, sWc = sem("sX"), sem("sW"), sem("sWc")
        sW2, sW3 = sem("sW2"), sem("sW3")
        sDUPb, sOUTb = sem("sDUPb"), sem("sOUTb")
        sST, sSQ, sLN, sQF = sem("sST"), sem("sSQ"), sem("sLN"), sem("sQF")
        sTPm, sTPe, sQm, sQe = sem("sTPm"), sem("sTPe"), sem("sQm"), sem("sQe")
        sDUP = sem("sDUP")
        sVR, sGV, sU, sSC = sem("sVR"), sem("sGV"), sem("sU"), sem("sSC")
        sZ, sRD, sYE = sem("sZ"), sem("sRD"), sem("sYE")
        sFM, sSIL, sOM, sSG = sem("sFM"), sem("sSIL"), sem("sOM"), sem("sSG")
        sSGT, sRES, sOUT, sDBG = sem("sSGT"), sem("sRES"), sem("sOUT"), sem("sDBG")
        sGD = sem("sGD")
        sGD2 = sem("sGD2")
        block = _st.enter_context(nc.Block())

        pv = [pv0, pv1, pv2, pv3]
        py = [py0, py1]
        pb = [pb0, pb1]
        pbT = [pb0.ap().bitcast(BF16), pb1.ap().bitcast(BF16)]
        wstg = qkv_fm.ap().bitcast(F32)

        wloads = []
        for k in range(KD):
            wloads.append((wq_sb, k * DQ, wq_in[k * 128:(k + 1) * 128, :], 128, DQ, "f32"))
        for k in range(KD):
            wloads.append((wo_sb, k * D, wo_in[k * 128:(k + 1) * 128, :], 128, D, "f32"))
        for k in range(KD):
            wloads.append((wf_sb, k * D, wf_in[k * 128:(k + 1) * 128, :], 128, D, "i32"))
        wloads.append((wst_sb, 0, wst_in[:, :], 128, H * 64, "f32"))
        wloads.append((esel_sb, 0, esel_in[:, :], 128, 32 * 128, "f32"))
        wloads.append((dsel_sb, 0, dsel_in[:, :], 128, 128, "f32"))
        wloads.append((idn_sb, 0, idn_in[:, :], 128, 128, "f32"))
        NWL = len(wloads)

        def vtile_ap(h):
            m = 16 + h // 2
            return qkv_fm[(h % 2) * 64:(h % 2) * 64 + 64, m * TCH:(m + 1) * TCH]

        def qtile_ap(h):
            m = h // 2
            return qkv_fm[(h % 2) * 64:(h % 2) * 64 + 64, m * TCH:(m + 1) * TCH]

        def ktile_ap(h):
            m = 8 + h // 2
            return qkv_fm[(h % 2) * 64:(h % 2) * 64 + 64, m * TCH:(m + 1) * TCH]

        @block.sync
        def _(s):
            for i, (dst, off, src, rows, colsn, conv) in enumerate(wloads):
                if i >= 2:
                    s.wait_ge(sWc, i - 1)
                half = 2048 if colsn > DQ else (i % 2) * DQ
                dstage = wstg[0:rows, half:half + colsn]
                if conv == "i32":
                    dstage = dstage.bitcast(I32)
                s.dma_start(out=dstage, in_=src).then_inc(sW if i % 2 == 0 else sW2, 16)
            s.dma_start(out=wqb_sb[:, :], in_=wqb_in).then_inc(sW3, 16)
            s.dma_start(out=ob_sb[:, :], in_=ob_in).then_inc(sW3, 16)
            s.dma_start(out=fb_sb[:, :], in_=fb_in).then_inc(sW3, 16)
            s.dma_start(out=ws_sb[:, :], in_=ws_in).then_inc(sW3, 16)
            s.dma_start(out=lam_sb, in_=lam_in).then_inc(sW3, 16)
            s.dma_start(out=gam_sb[:, :], in_=gam_in).then_inc(sW3, 16)
            s.dma_start(out=kscale_sb[:, :], in_=kscale_in).then_inc(sW3, 16)
            for c in range(NC):
                if "xn" in dbg:
                    s.wait_ge(sLN, c + 1)
                    s.dma_start(out=dbg["xn"][c * TCH:(c + 1) * TCH, :]
                                .rearrange("(n p) d -> p n d", p=128), in_=tmb[:, :]).then_inc(sDBG, 16)
                for h in range(H):
                    gh = 16 * c + h
                    s.wait_ge(sQe, 24 * c + 16 + h // 2 + 1)
                    if gh >= 2:
                        s.wait_ge(sU, 16 * (gh - 1))
                        s.wait_ge(sZ, 16 * (gh - 1))
                    slot = (h % 2) * TCH
                    dsem = sDUP if h % 2 == 0 else sDUPb
                    s.dma_start(out=kdup[0:64, slot:slot + TCH], in_=ktile_ap(h)).then_inc(dsem, 16)
                    s.dma_start(out=kdup[64:128, slot:slot + TCH], in_=ktile_ap(h)).then_inc(dsem, 16)
                    s.dma_start(out=qdup[0:64, slot:slot + TCH], in_=qtile_ap(h)).then_inc(dsem, 16)
                    s.dma_start(out=qdup[64:128, slot:slot + TCH], in_=qtile_ap(h)).then_inc(dsem, 16)
                if "qkv" in dbg:
                    s.wait_ge(sQe, 24 * (c + 1))
                    s.dma_start(out=dbg["qkv"][:, c * TCH:(c + 1) * TCH]
                                .rearrange("(m p) t -> p m t", p=128), in_=qkv_fm[:, :]).then_inc(sDBG, 16)
                if "y" in dbg:
                    s.wait_ge(sYE, 8 * (c + 1))
                    s.dma_start(out=dbg["y"][:, c * TCH:(c + 1) * TCH]
                                .rearrange("(m p) t -> p m t", p=128), in_=ybuf[:, :]).then_inc(sDBG, 16)
                if "xq" in dbg:
                    s.wait_ge(sTPe, 16 * c + 16)
                    s.dma_start(out=dbg["xq"][:, c * TCH:(c + 1) * TCH]
                                .rearrange("(m p) t -> p m t", p=128), in_=ffm[:, :]).then_inc(sDBG, 16)
                if "s" in dbg:
                    s.wait_ge(sSG, 8 * (c + 1))
                    s.dma_start(out=dbg["s"][:, c * TCH:(c + 1) * TCH]
                                .rearrange("(m p) t -> p m t", p=128), in_=sgbuf[:, :]).then_inc(sDBG, 16)
                for tt in range(NT):
                    s.wait_ge(sRES, 4 * c + tt + 1)
                    osem = sOUT if tt % 2 == 0 else sOUTb
                    if c >= n_warm:
                        co = c - n_warm
                        s.dma_start(out=out_dram[co * TCH + tt * 128: co * TCH + (tt + 1) * 128, :],
                                    in_=outstage[:, (tt % 2) * D:(tt % 2 + 1) * D]).then_inc(osem, 16)
                    else:
                        s.dma_start(out=waste[:, :],
                                    in_=outstage[:, (tt % 2) * D:(tt % 2 + 1) * D]).then_inc(osem, 16)
            s.wait_ge(sOUT, 16 * NC * 2)
            s.wait_ge(sOUTb, 16 * NC * 2)

        @block.vector
        def _(v):
            for i, (dst, off, src, rows, colsn, conv) in enumerate(wloads):
                v.wait_ge(sW if i % 2 == 0 else sW2, 16 * (i // 2 + 1))
                half = 2048 if colsn > DQ else (i % 2) * DQ
                src_ap = wstg[0:rows, half:half + colsn]
                if conv == "i32":
                    src_ap = src_ap.bitcast(I32)
                ins = v.tensor_copy(dst[0:rows, off:off + colsn], src_ap)
                if i >= NWL - 2:
                    ins.then_inc(sGD, 1)
                    v.wait_ge(sGD, i - (NWL - 2) + 1)
                    v.engine_nop().then_inc(sWc, 1)
                else:
                    ins.then_inc(sWc, 1)
            v.memset(scol[:, :], 0.0)
            v.memset(cols[:, :], 0.0)
            v.memset(eps_sb[:, 0:1], EPS_LN)
            v.memset(eps_sb[:, 1:2], EPS_RMS)
            v.wait_ge(sW3, 16 * 7)

            GDBASE = [0]
            GDTOT = [2]
            for c in range(NC):
                GDBASE[0] = GDTOT[0]
                pass
                if dbg and c >= 1:
                    v.wait_ge(sDBG, 16 * len(dbg) * c)
                gd = [0]
                def seal(ins):
                    gd[0] += 1
                    ins.then_inc(sGD, 1)
                    v.wait_ge(sGD, GDBASE[0] + gd[0])
                def seal_sig(ins, sigsem, inc=1):
                    seal(ins)
                    v.engine_nop().then_inc(sigsem, inc)
                v.wait_ge(sOUT, 16 * 2 * c)
                v.wait_ge(sOUTb, 16 * 2 * c)
                for tt in range(NT):
                    xs = xstage[:, tt * D:(tt + 1) * D]
                    v.wait_ge(sX, 64 * c + 16 * (tt + 1))
                    seal(v.reduce_sum(cols[:, 0 + tt:1 + tt], xs, AX.X))
                seal(v.tensor_scalar(cols[:, 0:NT], cols[:, 0:NT], 1.0 / D, None, ALU.mult))
                for tt in range(NT):
                    xs = xstage[:, tt * D:(tt + 1) * D]
                    xslot = xns[:, (tt % 2) * D:(tt % 2 + 1) * D]
                    oslot = outstage[:, (tt % 2) * D:(tt % 2 + 1) * D]
                    v.tensor_scalar(xslot, xs, cols[:, 0 + tt:1 + tt], None, ALU.subtract)
                    seal(v.tensor_tensor(oslot, xslot, xslot, ALU.mult))
                    seal(v.reduce_sum(cols[:, 4 + tt:5 + tt], oslot, AX.X))
                seal_sig(v.tensor_scalar(cols[:, 4:4 + NT], cols[:, 4:4 + NT], 1.0 / D, None, ALU.mult), sST)
                v.wait_ge(sSQ, 2 * c + 1)
                seal(v.reciprocal(cols[:, 12:12 + NT], cols[:, 8:8 + NT]))
                for tt in range(NT):
                    xs = xstage[:, tt * D:(tt + 1) * D]
                    xslot = xns[:, (tt % 2) * D:(tt % 2 + 1) * D]
                    v.tensor_scalar(xslot, xs, cols[:, 0 + tt:1 + tt],
                                    cols[:, 12 + tt:13 + tt], ALU.subtract, ALU.mult)
                    ins = v.tensor_copy(tmb[:, tt * D:(tt + 1) * D], xslot)
                    if tt == NT - 1:
                        seal_sig(ins, sLN)
                    v.reduce_max(cols[:, 16 + tt:17 + tt], xslot, AX.X,
                                 apply_absolute_value=True)
                for g8 in range(KD):
                    v.wait_ge(sTPm, 16 * c + g8 + 1)
                    ins = v.tensor_copy(ffm[:, g8 * TCH:(g8 + 1) * TCH],
                                        pbT[g8 % 2][:, 0:512])
                    if g8 == KD - 1:
                        seal_sig(ins, sTPe)
                    else:
                        ins.then_inc(sTPe, 1)
                if c >= 1:
                    v.wait_ge(sDUP, 512 * c)
                    v.wait_ge(sDUPb, 512 * c)
                for m in range(24):
                    v.wait_ge(sQm, 24 * c + m + 1)
                    bank = pb[m % 2]
                    if 8 <= m < 16:
                        ins = v.tensor_scalar(qkv_fm[:, m * TCH:(m + 1) * TCH], bank[:, :],
                                              wqb_sb[:, m:m + 1], kscale_sb[:, c:c + 1],
                                              ALU.add, ALU.mult)
                    else:
                        ins = v.tensor_scalar(qkv_fm[:, m * TCH:(m + 1) * TCH], bank[:, :],
                                              wqb_sb[:, m:m + 1], None, ALU.add)
                    if m >= 16:
                        seal_sig(ins, sQe)
                    else:
                        ins.then_inc(sQe, 1)
                seal(v.tensor_tensor(cols[:, 20:20 + NT], cols[:, 12:12 + NT],
                                     cols[:, 12:12 + NT], ALU.mult))
                seal_sig(v.tensor_tensor(cols[:, 20:20 + NT], cols[:, 20:20 + NT],
                                         cols[:, 4:4 + NT], ALU.mult), sST)
                v.wait_ge(sSQ, 2 * c + 2)
                seal(v.reciprocal(cols[:, 28:28 + NT], cols[:, 8:8 + NT]))
                seal(v.tensor_tensor(cols[:, 16:16 + NT], cols[:, 16:16 + NT],
                                     cols[:, 28:28 + NT], ALU.mult))
                seal(v.tensor_scalar(cols[:, 16:16 + NT], cols[:, 16:16 + NT], EPS_ACT, None, ALU.max))
                seal(v.reciprocal(cols[:, 24:24 + NT], cols[:, 16:16 + NT]))
                seal(v.tensor_scalar(cols[:, 24:24 + NT], cols[:, 24:24 + NT], 127.0, None, ALU.mult))
                seal(v.tensor_tensor(cols[:, 28:28 + NT], cols[:, 28:28 + NT],
                                     cols[:, 12:12 + NT], ALU.mult))
                seal(v.tensor_tensor(cols[:, 28:28 + NT], cols[:, 28:28 + NT],
                                     cols[:, 24:24 + NT], ALU.mult))
                seal(v.tensor_scalar(cols[:, 16:16 + NT], cols[:, 16:16 + NT], ws_sb[:, 0:1], None, ALU.mult))
                seal(v.tensor_scalar(cols[:, 16:16 + NT], cols[:, 16:16 + NT], 1.0 / 127.0, None, ALU.mult))
                v.wait_ge(sTPm, 16 * c + 8)
                if "xn" in dbg:
                    v.wait_ge(sDBG, 16 * (len(dbg) * c + 1))
                for tt in range(NT):
                    xs = xstage[:, tt * D:(tt + 1) * D]
                    xslot = xns[:, (tt % 2) * D:(tt % 2 + 1) * D]
                    oslot = outstage[:, (tt % 2) * D:(tt % 2 + 1) * D]
                    v.tensor_scalar(xslot, xs, cols[:, 0 + tt:1 + tt],
                                    cols[:, 28 + tt:29 + tt], ALU.subtract, ALU.mult)
                    v.tensor_scalar(xslot, xslot, 128.5, None, ALU.add)
                    for kb in range(7, -1, -1):
                        v.tensor_scalar(oslot, xslot, float(1 << kb), float(1 << kb),
                                        ALU.is_ge, ALU.mult)
                        v.tensor_tensor(xslot, xslot, oslot, ALU.subtract)
                    v.tensor_scalar(oslot, xs, cols[:, 0 + tt:1 + tt],
                                    cols[:, 28 + tt:29 + tt], ALU.subtract, ALU.mult)
                    v.tensor_tensor(xslot, oslot, xslot, ALU.subtract)
                    v.tensor_scalar(xslot, xslot, 0.5, -128.0, ALU.add, ALU.max)
                    v.tensor_scalar(xslot, xslot, 127.0, cols[:, 16 + tt:17 + tt],
                                    ALU.min, ALU.mult)
                    ins = v.tensor_copy(tmb[:, tt * D:(tt + 1) * D], xslot)
                    if tt == NT - 1:
                        seal_sig(ins, sQF)
                for g8 in range(KD):
                    v.wait_ge(sTPm, 16 * c + 8 + g8 + 1)
                    ins = v.tensor_copy(ffm[:, g8 * TCH:(g8 + 1) * TCH],
                                        pbT[(8 + g8) % 2][:, 0:512])
                    if g8 == KD - 1:
                        seal_sig(ins, sTPe)
                    else:
                        ins.then_inc(sTPe, 1)
                for rr in range(NRR):
                    h = rr // 4
                    qq = rr % 4
                    qsl = qdup[:, (h % 2) * TCH:(h % 2 + 1) * TCH]
                    ksl = kdup[:, (h % 2) * TCH:(h % 2 + 1) * TCH]
                    for g in range(NGR):
                        GG = 256 * c + 4 * rr + g
                        v.wait_ge(sGV, GG + 1)
                        if g == 0 and rr % 4 == 0:
                            v.wait_ge(sDUP if h % 2 == 0 else sDUPb, 64 * (8 * c + h // 2 + 1))
                        for t2 in range(2):
                            k = g * 2 + t2
                            sl = slice(k * TCH, (k + 1) * TCH)
                            ins = v.tensor_tensor(uz[:, sl], gv[:, sl], ksl, ALU.mult)
                            if t2 == 1:
                                ins.then_inc(sU, 1)
                        for t2 in range(2):
                            k = g * 2 + t2
                            m = qq * 8 + k
                            ct = h * 32 + m
                            sl = slice(k * TCH, (k + 1) * TCH)
                            v.tensor_tensor_scan(gv[:, sl],
                                                 rr_sb[:, ct:ct + 1].broadcast_to([128, TCH]),
                                                 uz[:, sl], scol[:, ct:ct + 1],
                                                 ALU.mult, ALU.add)
                        if GG >= 4:
                            v.wait_ge(sRD, GG - 3)
                        for t2 in range(2):
                            k = g * 2 + t2
                            sl = slice(k * TCH, (k + 1) * TCH)
                            ins = v.tensor_tensor(uz[:, sl], gv[:, sl], qsl, ALU.mult)
                            if t2 == 1:
                                ins.then_inc(sZ, 1)
                    if rr >= 9 and (rr - 9) % 8 == 0:
                        p = (rr - 9) // 8
                        v.wait_ge(sRD, 256 * c + 32 * (p + 1))
                        seal_sig(v.tensor_copy(ybuf[:, p * TCH:(p + 1) * TCH], py[p % 2][:, :]), sYE)
                p = 7
                v.wait_ge(sRD, 256 * c + 32 * (p + 1))
                seal_sig(v.tensor_copy(ybuf[:, p * TCH:(p + 1) * TCH], py[p % 2][:, :]), sYE)
                for m in range(8):
                    v.wait_ge(sOM, 8 * c + m + 1)
                    v.wait_ge(sSIL, 8 * c + m + 1)
                    seal_sig(v.scalar_tensor_tensor(sgbuf[:, m * TCH:(m + 1) * TCH], pb[m % 2][:, :],
                                                    ob_sb[:, m:m + 1], gbuf[:, (m % 2) * TCH:(m % 2 + 1) * TCH],
                                                    ALU.add, ALU.mult), sSG)
                for tt in range(NT):
                    v.wait_ge(sSGT, 4 * c + tt + 1)
                    if 2 * c + tt // 2 >= 1:
                        v.wait_ge(sOUT if tt % 2 == 0 else sOUTb, 16 * (2 * c + tt // 2))
                    oslot = outstage[:, (tt % 2) * D:(tt % 2 + 1) * D]
                    v.tensor_tensor(oslot[:, 0:512], pbT[0][:, 0:512],
                                    xstage[:, tt * D:tt * D + 512], ALU.add)
                    seal_sig(v.tensor_tensor(oslot[:, 512:1024], pbT[1][:, 0:512],
                                             xstage[:, tt * D + 512:(tt + 1) * D], ALU.add), sRES)
                GDTOT[0] = GDBASE[0] + gd[0]

        @block.scalar
        def _(sc):
            gd2 = [0]
            def seal_sig2(ins, sigsem):
                gd2[0] += 1
                ins.then_inc(sGD2, 1)
                sc.wait_ge(sGD2, gd2[0])
                sc.mul(scr_sb[0:1, 0:1], scr_sb[0:1, 0:1], 1.0).then_inc(sigsem, 1)
            sc.wait_ge(sW3, 16 * 7)
            sc.copy(scr_sb[0:1, 0:1], eps_sb[0:1, 0:1])
            sc.activation(rr_sb[:, :], lam_sb, AF.Sigmoid, scale=-1.0)
            for c in range(NC):
                sc.wait_ge(sST, 2 * c + 1)
                seal_sig2(sc.activation(cols[:, 8:8 + NT], cols[:, 4:4 + NT], AF.Sqrt,
                                        bias=eps_sb[:, 0:1]), sSQ)
                sc.wait_ge(sST, 2 * c + 2)
                seal_sig2(sc.activation(cols[:, 8:8 + NT], cols[:, 20:20 + NT], AF.Sqrt,
                                        bias=eps_sb[:, 1:2]), sSQ)
                for rr in range(NRR):
                    h = rr // 4
                    qq = rr % 4
                    for g in range(NGR):
                        GG = 256 * c + 4 * rr + g
                        R = c * NRR + rr
                        sc.wait_ge(sVR, GG + 1)
                        if GG >= 4:
                            sc.wait_ge(sZ, GG - 3)
                            rp = (R - 1) % NRR
                            ct0 = (rp // 4) * 32 + (rp % 4) * 8 + 2 * g
                            k0 = 2 * g
                            sc.activation(scol[:, ct0:ct0 + 2],
                                          gv[:, k0 * TCH + TCH - 1:(k0 + 2) * TCH:TCH],
                                          AF.Copy)
                        for t2 in range(2):
                            k = g * 2 + t2
                            m = qq * 8 + k
                            ct = h * 32 + m
                            bank = pv[(g % 2) * 2 + t2]
                            ins = sc.activation(gv[:, k * TCH:(k + 1) * TCH], bank[:, :],
                                                AF.Copy, scale=gam_sb[:, ct:ct + 1])
                            if t2 == 1:
                                ins.then_inc(sGV, 1)
                for m in range(8):
                    sc.wait_ge(sFM, 8 * c + m + 1)
                    if 8 * c + m >= 2:
                        sc.wait_ge(sSG, 8 * c + m - 1)
                    seal_sig2(sc.activation(gbuf[:, (m % 2) * TCH:(m % 2 + 1) * TCH], pb[m % 2][:, :],
                                            AF.Silu, bias=fb_sb[:, m:m + 1]), sSIL)


        @block.gpsimd
        def _(g_):
            for c in range(NC):
                for tt in range(NT):
                    if c >= 1:
                        g_.wait_ge(sRES, 4 * (c - 1) + tt + 1)
                    g_.dma_start(out=xstage[:, tt * D:(tt + 1) * D],
                                 in_=x_in[c * TCH + tt * 128: c * TCH + (tt + 1) * 128, :]
                                 ).then_inc(sX, 16)

        @block.tensor
        def _(t):
            t.wait_ge(sWc, NWL)
            for c in range(NC):
                t.wait_ge(sLN, c + 1)
                if c >= 1:
                    t.wait_ge(sRES, 4 * c)
                for g8 in range(KD):
                    if g8 >= 2:
                        t.wait_ge(sTPe, 16 * c + g8 - 1)
                    bank = pbT[g8 % 2]
                    for tt in range(NT):
                        ins = t.transpose(bank[:, tt * 128:(tt + 1) * 128],
                                          tmb[:, tt * D + g8 * 128: tt * D + (g8 + 1) * 128],
                                          idn_sb[:, :])
                        if tt == NT - 1:
                            ins.then_inc(sTPm, 1)
                t.wait_ge(sTPe, 16 * c + 8)
                for m in range(24):
                    if m >= 2:
                        t.wait_ge(sQe, 24 * c + m - 1)
                    bank = pb[m % 2]
                    for k in range(KD):
                        ins = t.matmul(bank[:, :], wq_sb[:, k * DQ + m * 128: k * DQ + (m + 1) * 128],
                                       ffm[:, k * TCH:(k + 1) * TCH],
                                       start=(k == 0), stop=(k == KD - 1))
                        if k == KD - 1:
                            ins.then_inc(sQm, 1)
                t.wait_ge(sQF, c + 1)
                for g8 in range(KD):
                    if g8 >= 2:
                        t.wait_ge(sTPe, 16 * c + 8 + g8 - 1)
                    bank = pbT[(8 + g8) % 2]
                    for tt in range(NT):
                        ins = t.transpose(bank[:, tt * 128:(tt + 1) * 128],
                                          tmb[:, tt * D + g8 * 128: tt * D + (g8 + 1) * 128],
                                          idn_sb[:, :])
                        if tt == NT - 1:
                            ins.then_inc(sTPm, 1)

                def emit_vrep(rr):
                    h = rr // 4
                    qq = rr % 4
                    if qq == 0:
                        t.wait_ge(sQe, 24 * c + 16 + h // 2 + 1)
                    for g in range(NGR):
                        GG = 256 * c + 4 * rr + g
                        if GG >= 2:
                            t.wait_ge(sGV, GG - 1)
                        for t2 in range(2):
                            k = g * 2 + t2
                            m = qq * 8 + k
                            bank = pv[(g % 2) * 2 + t2]
                            eb = (h % 2) * 64
                            ins = t.matmul(bank[:, :], esel_sb[eb:eb + 64, m * 128:(m + 1) * 128],
                                           vtile_ap(h), start=True, stop=True)
                            if t2 == 1:
                                ins.then_inc(sVR, 1)

                def emit_reduce(rr):
                    h = rr // 4
                    qq = rr % 4
                    p = h // 2
                    rowbase = (h % 2) * 64
                    bank = py[p % 2]
                    if qq == 0:
                        gp = 8 * c + p
                        if h % 2 == 0 and gp >= 2:
                            t.wait_ge(sYE, gp - 1)
                        wb = (h % 2) * 64
                        t.matmul(bank[rowbase:rowbase + 64, :],
                                 wst_sb[wb:wb + 64, h * 64:(h + 1) * 64], qtile_ap(h),
                                 start=True, stop=False, skip_group_check=True)
                    for g in range(NGR):
                        GG = 256 * c + 4 * rr + g
                        t.wait_ge(sZ, GG + 1)
                        for t2 in range(2):
                            k = g * 2 + t2
                            m = qq * 8 + k
                            last = (qq == 3 and k == 7)
                            ins = t.matmul(bank[rowbase:rowbase + 64, :],
                                           dsel_sb[:, 64 - 2 * m:128 - 2 * m],
                                           uz[:, k * TCH:(k + 1) * TCH],
                                           start=False, stop=last, skip_group_check=True)
                            if t2 == 1:
                                ins.then_inc(sRD, 1)

                emit_vrep(0)
                for rr in range(1, NRR):
                    emit_vrep(rr)
                    emit_reduce(rr - 1)
                emit_reduce(NRR - 1)
                t.wait_ge(sTPe, 16 * c + 16)
                for pair in range(4):
                    for m in (2 * pair, 2 * pair + 1):
                        if m >= 2:
                            t.wait_ge(sSG, 8 * c + m - 1)
                        bank = pb[m % 2]
                        for k in range(KD):
                            ins = t.matmul(bank[:, :], wf_sb[:, k * D + m * 128: k * D + (m + 1) * 128],
                                           ffm[:, k * TCH:(k + 1) * TCH],
                                           start=(k == 0), stop=(k == KD - 1))
                            if k == KD - 1:
                                ins.then_inc(sFM, 1)
                    if pair == 0:
                        t.wait_ge(sYE, 8 * (c + 1))
                    for m in (2 * pair, 2 * pair + 1):
                        t.wait_ge(sSIL, 8 * c + m + 1)
                        bank = pb[m % 2]
                        for k in range(KD):
                            ins = t.matmul(bank[:, :], wo_sb[:, k * D + m * 128: k * D + (m + 1) * 128],
                                           ybuf[:, k * TCH:(k + 1) * TCH],
                                           start=(k == 0), stop=(k == KD - 1))
                            if k == KD - 1:
                                ins.then_inc(sOM, 1)
                t.wait_ge(sSG, 8 * (c + 1))
                for tt in range(NT):
                    if tt >= 1:
                        t.wait_ge(sRES, 4 * c + tt)
                    for o8 in range(KD):
                        bank = pbT[o8 // 4]
                        ins = t.transpose(bank[:, (o8 % 4) * 128:(o8 % 4 + 1) * 128],
                                          sgbuf[:, o8 * TCH + tt * 128: o8 * TCH + (tt + 1) * 128],
                                          idn_sb[:, :])
                        if o8 == KD - 1:
                            ins.then_inc(sSGT, 1)

    return nc


def host_weights(inputs):
    W = {}
    W["wq"] = np.ascontiguousarray(np.asarray(inputs["L0_attn_Wqkv_w"], np.float32).T)
    W["wqb"] = np.ascontiguousarray(np.asarray(inputs["L0_attn_Wqkv_b"], np.float32).reshape(24, 128).T)
    W["wo"] = np.ascontiguousarray(np.asarray(inputs["L0_attn_out_w"], np.float32).T)
    W["ob"] = np.ascontiguousarray(np.asarray(inputs["L0_attn_out_b"], np.float32).reshape(8, 128).T)
    W["wf"] = np.ascontiguousarray(np.asarray(inputs["L0_ffn_w_int8"], np.int32).T)
    W["fb"] = np.ascontiguousarray(np.asarray(inputs["L0_ffn_bias"], np.float32).reshape(8, 128).T)
    W["ws"] = np.ascontiguousarray(np.broadcast_to(
        np.asarray(inputs["L0_ffn_weight_scale"], np.float32).reshape(1, 1), (128, 1)))
    lam = np.asarray(inputs["L0_attn_Lambda_raw"], np.float32)
    gam = np.asarray(inputs["L0_attn_Gamma"], np.float32)
    W["lam"] = np.ascontiguousarray(lam.reshape(H, 32, 2, 64).transpose(2, 3, 0, 1).reshape(128, 512))
    W["gam"] = np.ascontiguousarray(gam.reshape(H, 32, 2, 64).transpose(2, 3, 0, 1).reshape(128, 512))
    wst = np.asarray(inputs["L0_attn_W_static"], np.float32)
    wstj = wst.transpose(2, 0, 1).reshape(64, H * 64)
    W["wst"] = np.ascontiguousarray(np.concatenate([wstj, wstj], 0))
    esel = np.zeros((64, 32 * 128), np.float32)
    for m in range(32):
        esel[2 * m, m * 128:m * 128 + 64] = 1.0
        esel[2 * m + 1, m * 128 + 64:m * 128 + 128] = 1.0
    W["esel"] = np.concatenate([esel, esel], 0)
    dsel = np.zeros((128, 128), np.float32)
    for p2 in range(128):
        dsel[p2, 64 + p2 // 64] = 1.0
    W["dsel"] = dsel
    W["idn"] = np.eye(128, dtype=np.float32)
    return W


N_WARM = 2
N_CHUNKS = 10
_CACHE = {}


def _get_nc():
    if "nc" not in _CACHE:
        _CACHE["nc"] = build_program(N_CHUNKS, N_WARM)
    return _CACHE["nc"]


def kernel(L0_attn_x, L0_attn_Wqkv_w, L0_attn_Wqkv_b, L0_attn_out_w, L0_attn_out_b,
           L0_attn_W_static, L0_attn_Lambda_raw, L0_attn_Gamma, L0_ffn_w_int8,
           L0_ffn_weight_scale, L0_ffn_bias):
    from concourse.bass_utils import run_bass_kernel_spmd
    inputs = dict(L0_attn_Wqkv_w=L0_attn_Wqkv_w, L0_attn_Wqkv_b=L0_attn_Wqkv_b,
                  L0_attn_out_w=L0_attn_out_w, L0_attn_out_b=L0_attn_out_b,
                  L0_attn_W_static=L0_attn_W_static, L0_attn_Lambda_raw=L0_attn_Lambda_raw,
                  L0_attn_Gamma=L0_attn_Gamma, L0_ffn_w_int8=L0_ffn_w_int8,
                  L0_ffn_weight_scale=L0_ffn_weight_scale, L0_ffn_bias=L0_ffn_bias)
    W = host_weights(inputs)
    x = np.asarray(L0_attn_x, np.float32)
    B, L, _ = x.shape
    TQ = L // 4
    WARM = N_WARM * TCH
    in_maps = []
    for c in range(8):
        b, q = c // 4, c % 4
        ks = np.full((128, N_CHUNKS), 0.125, np.float32)
        if q == 0:
            warm = x[b, 0:WARM]
            ks[:, :N_WARM] = 0.0
        else:
            warm = x[b, q * TQ - WARM: q * TQ]
        xc = np.ascontiguousarray(np.concatenate([warm, x[b, q * TQ:(q + 1) * TQ]], 0))
        in_maps.append(dict(x=xc, kscale=ks, **W))
    nc = _get_nc()
    res = run_bass_kernel_spmd(nc, in_maps, core_ids=list(range(8)))
    parts = [res.results[c]["out"] for c in range(8)]
    out = np.empty((B, L, D), np.float32)
    for c in range(8):
        b, q = c // 4, c % 4
        out[b, q * TQ:(q + 1) * TQ] = np.asarray(parts[c], np.float32)
    return out


# version 10
# speedup vs baseline: 1.0038x; 1.0000x over previous
import contextlib
import math
import numpy as np
import concourse.bass as bass
import concourse.mybir as mybir

F32 = mybir.dt.float32
BF16 = mybir.dt.bfloat16
I32 = mybir.dt.int32
ALU = mybir.AluOpType
AF = mybir.ActivationFunctionType
AX = mybir.AxisListType

H, DH, D, DQ = 16, 64, 1024, 3072
TCH = 512
NT = TCH // 128
KD = D // 128
NTL = 8
NRR = 64
NGR = 4
EPS_LN = 1e-5
EPS_RMS = 1e-6
EPS_ACT = 1e-5


def build_program(n_chunks, n_warm, debug_outputs=()):
    nc = bass.Bass("TRN2", target_bir_lowering=False, debug=False, detect_race_conditions=False)
    NC = n_chunks
    T_ALL = NC * TCH
    T_OUT = (NC - n_warm) * TCH

    x_in = nc.dram_tensor("x", [T_ALL, D], F32, kind="ExternalInput").ap()
    kscale_in = nc.dram_tensor("kscale", [128, NC], F32, kind="ExternalInput").ap()
    wq_in = nc.dram_tensor("wq", [D, DQ], F32, kind="ExternalInput").ap()
    wqb_in = nc.dram_tensor("wqb", [128, 24], F32, kind="ExternalInput").ap()
    wo_in = nc.dram_tensor("wo", [D, D], F32, kind="ExternalInput").ap()
    ob_in = nc.dram_tensor("ob", [128, 8], F32, kind="ExternalInput").ap()
    wf_in = nc.dram_tensor("wf", [D, D], I32, kind="ExternalInput").ap()
    fb_in = nc.dram_tensor("fb", [128, 8], F32, kind="ExternalInput").ap()
    ws_in = nc.dram_tensor("ws", [128, 1], F32, kind="ExternalInput").ap()
    lam_in = nc.dram_tensor("lam", [128, 512], F32, kind="ExternalInput").ap()
    gam_in = nc.dram_tensor("gam", [128, 512], F32, kind="ExternalInput").ap()
    wst_in = nc.dram_tensor("wst", [128, H * 64], F32, kind="ExternalInput").ap()
    esel_in = nc.dram_tensor("esel", [128, 32 * 128], F32, kind="ExternalInput").ap()
    dsel_in = nc.dram_tensor("dsel", [128, 128], F32, kind="ExternalInput").ap()
    idn_in = nc.dram_tensor("idn", [128, 128], F32, kind="ExternalInput").ap()
    out_dram = nc.dram_tensor("out", [T_OUT, D], F32, kind="ExternalOutput").ap()
    waste = nc.dram_tensor("waste", [128, D], F32).ap()
    dbg = {}
    for name, shape in (("xn", [T_ALL, D]), ("qkv", [DQ, T_ALL]), ("y", [D, T_ALL]),
                        ("xq", [D, T_ALL]), ("s", [D, T_ALL])):
        if name in debug_outputs:
            dbg[name] = nc.dram_tensor("dbg_" + name, shape, BF16, kind="ExternalOutput").ap()

    with contextlib.ExitStack() as _st:
        def sb(name, shape, dt):
            return _st.enter_context(nc.sbuf_tensor(name, shape, dt))
        def psa(name):
            return _st.enter_context(nc.psum_tensor(name, [128, 512], F32))
        def sem(name):
            return _st.enter_context(nc.semaphore(name))

        wq_sb = sb("wq_sb", [128, KD * DQ], BF16)
        wo_sb = sb("wo_sb", [128, KD * D], BF16)
        wf_sb = sb("wf_sb", [128, KD * D], BF16)
        wqb_sb = sb("wqb_sb", [128, 24], F32)
        ob_sb = sb("ob_sb", [128, 8], F32)
        fb_sb = sb("fb_sb", [128, 8], F32)
        ws_sb = sb("ws_sb", [128, 1], F32)
        rr_sb = sb("rr_sb", [128, 512], F32)
        gam_sb = sb("gam_sb", [128, 512], F32)
        wst_sb = sb("wst_sb", [128, H * 64], BF16)
        esel_sb = sb("esel_sb", [128, 32 * 128], BF16)
        dsel_sb = sb("dsel_sb", [128, 128], BF16)
        idn_sb = sb("idn_sb", [128, 128], BF16)
        kscale_sb = sb("kscale_sb", [128, NC], F32)
        eps_sb = sb("eps_sb", [128, 2], F32)
        scr_sb = sb("scr_sb", [1, 4], F32)
        xstage = sb("xstage", [128, NT * D], F32)
        stats = sb("stats", [128, NT * 12], F32)
        cols = sb("cols", [128, 32], F32)
        xns = sb("xns", [128, 2 * D], F32)
        tmb = sb("tmb", [128, NT * D], BF16)
        ffm = sb("ffm", [128, KD * TCH], BF16)
        qkv_fm = sb("qkv_fm", [128, 3 * KD * TCH], BF16)
        kdup = sb("kdup", [128, 2 * TCH], BF16)
        qdup = sb("qdup", [128, 2 * TCH], BF16)
        gv = sb("gv", [128, NTL * TCH], BF16)
        uz = sb("uz", [128, NTL * TCH], BF16)
        scol = sb("scol", [128, 512], F32)
        stg32 = sb("stg32", [128, 512], I32)
        ybuf = sb("ybuf", [128, KD * TCH], BF16)
        gbuf = tmb
        sgbuf = sb("sgbuf", [128, KD * TCH], BF16)
        outstage = sb("outstage", [128, 2 * D], F32)
        pt0 = _st.enter_context(nc.psum_tensor("pt0", [128, 1024], BF16))
        pt1 = _st.enter_context(nc.psum_tensor("pt1", [128, 1024], BF16))
        py0, py1 = psa("py0"), psa("py1")
        pv0, pv1, pv2, pv3 = psa("pv0"), psa("pv1"), psa("pv2"), psa("pv3")
        sX, sW, sWc = sem("sX"), sem("sW"), sem("sWc")
        sW2, sW3 = sem("sW2"), sem("sW3")
        sWI, sLAM = sem("sWI"), sem("sLAM")
        sWS = [sem("sWS%d" % k) for k in range(8)]
        sDUPb, sOUTb = sem("sDUPb"), sem("sOUTb")
        sST, sSQ, sLN, sQF = sem("sST"), sem("sSQ"), sem("sLN"), sem("sQF")
        sTPm, sTPe, sQm, sQe = sem("sTPm"), sem("sTPe"), sem("sQm"), sem("sQe")
        sDUP = sem("sDUP")
        sVR, sGV, sU, sSC = sem("sVR"), sem("sGV"), sem("sU"), sem("sSC")
        sZ, sRD, sYE = sem("sZ"), sem("sRD"), sem("sYE")
        sFM, sSIL, sOM, sSG = sem("sFM"), sem("sSIL"), sem("sOM"), sem("sSG")
        sSGT, sRES, sOUT, sDBG = sem("sSGT"), sem("sRES"), sem("sOUT"), sem("sDBG")
        sGD = sem("sGD")
        sGD2 = sem("sGD2")
        block = _st.enter_context(nc.Block())

        pv = [pv0, pv1, pv2, pv3]
        py = [py0, py1]
        pt = [pt0, pt1]

        floads = []
        for k in range(KD):
            for j in range(6):
                floads.append((wq_sb, k * DQ + j * 512,
                               wq_in[k * 128:(k + 1) * 128, j * 512:(j + 1) * 512], 512))
        for k in range(KD):
            for j in range(2):
                floads.append((wo_sb, k * D + j * 512,
                               wo_in[k * 128:(k + 1) * 128, j * 512:(j + 1) * 512], 512))
        for j in range(2):
            floads.append((wst_sb, j * 512, wst_in[:, j * 512:(j + 1) * 512], 512))
        for j in range(8):
            floads.append((esel_sb, j * 512, esel_in[:, j * 512:(j + 1) * 512], 512))
        floads.append((dsel_sb, 0, dsel_in[:, :], 128))
        floads.append((idn_sb, 0, idn_in[:, :], 128))
        NINT = 2 * KD
        NCONV = NINT + len(floads)

        def vtile_ap(h):
            m = 16 + h // 2
            return qkv_fm[(h % 2) * 64:(h % 2) * 64 + 64, m * TCH:(m + 1) * TCH]

        def qtile_ap(h):
            m = h // 2
            return qkv_fm[(h % 2) * 64:(h % 2) * 64 + 64, m * TCH:(m + 1) * TCH]

        def ktile_ap(h):
            m = 8 + h // 2
            return qkv_fm[(h % 2) * 64:(h % 2) * 64 + 64, m * TCH:(m + 1) * TCH]

        @block.sync
        def _(s):
            for j in range(NINT):
                if j >= 1:
                    s.wait_ge(sWc, j)
                k, jj = j // 2, j % 2
                s.dma_start(out=stg32[:, :],
                            in_=wf_in[k * 128:(k + 1) * 128, jj * 512:(jj + 1) * 512]
                            ).then_inc(sWI, 16)
            for i, (dst, off, src, w) in enumerate(floads):
                if i >= 8:
                    s.wait_ge(sWc, NINT + i - 7)
                slot = (i % 8) * 512
                s.dma_start(out=xstage[:, slot:slot + w],
                            in_=src).then_inc(sWS[i % 8], 16)
            s.dma_start(out=wqb_sb[:, :], in_=wqb_in).then_inc(sW3, 16)
            s.dma_start(out=ob_sb[:, :], in_=ob_in).then_inc(sW3, 16)
            s.dma_start(out=fb_sb[:, :], in_=fb_in).then_inc(sW3, 16)
            s.dma_start(out=ws_sb[:, :], in_=ws_in).then_inc(sW3, 16)
            s.dma_start(out=scol[:, :], in_=lam_in).then_inc(sW3, 16)
            s.dma_start(out=gam_sb[:, :], in_=gam_in).then_inc(sW3, 16)
            s.dma_start(out=kscale_sb[:, :], in_=kscale_in).then_inc(sW3, 16)
            for c in range(NC):
                if "xn" in dbg:
                    s.wait_ge(sLN, c + 1)
                    s.dma_start(out=dbg["xn"][c * TCH:(c + 1) * TCH, :]
                                .rearrange("(n p) d -> p n d", p=128), in_=tmb[:, :]).then_inc(sDBG, 16)
                for h in range(H):
                    gh = 16 * c + h
                    s.wait_ge(sQe, 24 * c + 16 + h // 2 + 1)
                    if gh >= 2:
                        s.wait_ge(sU, 16 * (gh - 1))
                        s.wait_ge(sZ, 16 * (gh - 1))
                    slot = (h % 2) * TCH
                    dsem = sDUP if h % 2 == 0 else sDUPb
                    s.dma_start(out=kdup[0:64, slot:slot + TCH], in_=ktile_ap(h)).then_inc(dsem, 16)
                    s.dma_start(out=kdup[64:128, slot:slot + TCH], in_=ktile_ap(h)).then_inc(dsem, 16)
                    s.dma_start(out=qdup[0:64, slot:slot + TCH], in_=qtile_ap(h)).then_inc(dsem, 16)
                    s.dma_start(out=qdup[64:128, slot:slot + TCH], in_=qtile_ap(h)).then_inc(dsem, 16)
                if "qkv" in dbg:
                    s.wait_ge(sQe, 24 * (c + 1))
                    s.dma_start(out=dbg["qkv"][:, c * TCH:(c + 1) * TCH]
                                .rearrange("(m p) t -> p m t", p=128), in_=qkv_fm[:, :]).then_inc(sDBG, 16)
                if "y" in dbg:
                    s.wait_ge(sYE, 8 * (c + 1))
                    s.dma_start(out=dbg["y"][:, c * TCH:(c + 1) * TCH]
                                .rearrange("(m p) t -> p m t", p=128), in_=ybuf[:, :]).then_inc(sDBG, 16)
                if "xq" in dbg:
                    s.wait_ge(sTPe, 16 * c + 16)
                    s.dma_start(out=dbg["xq"][:, c * TCH:(c + 1) * TCH]
                                .rearrange("(m p) t -> p m t", p=128), in_=ffm[:, :]).then_inc(sDBG, 16)
                if "s" in dbg:
                    s.wait_ge(sSG, 8 * (c + 1))
                    s.dma_start(out=dbg["s"][:, c * TCH:(c + 1) * TCH]
                                .rearrange("(m p) t -> p m t", p=128), in_=sgbuf[:, :]).then_inc(sDBG, 16)
                for tt in range(NT):
                    s.wait_ge(sRES, 4 * c + tt + 1)
                    osem = sOUT if tt % 2 == 0 else sOUTb
                    if c >= n_warm:
                        co = c - n_warm
                        s.dma_start(out=out_dram[co * TCH + tt * 128: co * TCH + (tt + 1) * 128, :],
                                    in_=outstage[:, (tt % 2) * D:(tt % 2 + 1) * D]).then_inc(osem, 16)
                    else:
                        s.dma_start(out=waste[:, :],
                                    in_=outstage[:, (tt % 2) * D:(tt % 2 + 1) * D]).then_inc(osem, 16)
            s.wait_ge(sOUT, 16 * NC * 2)
            s.wait_ge(sOUTb, 16 * NC * 2)

        @block.vector
        def _(v):
            v.memset(cols[:, :], 0.0)
            v.memset(eps_sb[:, 0:1], EPS_LN)
            v.memset(eps_sb[:, 1:2], EPS_RMS)
            for j in range(NINT):
                v.wait_ge(sWI, 16 * (j + 1))
                k, jj = j // 2, j % 2
                v.tensor_copy(wf_sb[:, k * D + jj * 512:k * D + (jj + 1) * 512],
                              stg32[:, :]).then_inc(sWc, 1)
            nseal = 0
            for i, (dst, off, src, w) in enumerate(floads):
                v.wait_ge(sWS[i % 8], 16 * (i // 8 + 1))
                slot = (i % 8) * 512
                ins = v.tensor_copy(dst[:, off:off + w], xstage[:, slot:slot + w])
                if w == 128:
                    nseal += 1
                    ins.then_inc(sGD, 1)
                    v.wait_ge(sGD, nseal)
                    v.engine_nop().then_inc(sWc, 1)
                else:
                    ins.then_inc(sWc, 1)
            v.wait_ge(sLAM, 1)
            v.memset(scol[:, :], 0.0)
            v.wait_ge(sW3, 16 * 7)

            GDBASE = [0]
            GDTOT = [nseal]
            for c in range(NC):
                GDBASE[0] = GDTOT[0]
                pass
                if dbg and c >= 1:
                    v.wait_ge(sDBG, 16 * len(dbg) * c)
                gd = [0]
                def seal(ins):
                    gd[0] += 1
                    ins.then_inc(sGD, 1)
                    v.wait_ge(sGD, GDBASE[0] + gd[0])
                def seal_sig(ins, sigsem, inc=1):
                    seal(ins)
                    v.engine_nop().then_inc(sigsem, inc)
                v.wait_ge(sOUT, 16 * 2 * c)
                v.wait_ge(sOUTb, 16 * 2 * c)
                for tt in range(NT):
                    xs = xstage[:, tt * D:(tt + 1) * D]
                    v.wait_ge(sX, 64 * c + 16 * (tt + 1))
                    seal(v.reduce_sum(cols[:, 0 + tt:1 + tt], xs, AX.X))
                seal(v.tensor_scalar(cols[:, 0:NT], cols[:, 0:NT], 1.0 / D, None, ALU.mult))
                for tt in range(NT):
                    xs = xstage[:, tt * D:(tt + 1) * D]
                    xslot = xns[:, (tt % 2) * D:(tt % 2 + 1) * D]
                    oslot = outstage[:, (tt % 2) * D:(tt % 2 + 1) * D]
                    v.tensor_scalar(xslot, xs, cols[:, 0 + tt:1 + tt], None, ALU.subtract)
                    seal(v.tensor_tensor(oslot, xslot, xslot, ALU.mult))
                    seal(v.reduce_sum(cols[:, 4 + tt:5 + tt], oslot, AX.X))
                seal_sig(v.tensor_scalar(cols[:, 4:4 + NT], cols[:, 4:4 + NT], 1.0 / D, None, ALU.mult), sST)
                v.wait_ge(sSQ, 2 * c + 1)
                seal(v.reciprocal(cols[:, 12:12 + NT], cols[:, 8:8 + NT]))
                for tt in range(NT):
                    xs = xstage[:, tt * D:(tt + 1) * D]
                    xslot = xns[:, (tt % 2) * D:(tt % 2 + 1) * D]
                    v.tensor_scalar(xslot, xs, cols[:, 0 + tt:1 + tt],
                                    cols[:, 12 + tt:13 + tt], ALU.subtract, ALU.mult)
                    ins = v.tensor_copy(tmb[:, tt * D:(tt + 1) * D], xslot)
                    if tt == NT - 1:
                        seal_sig(ins, sLN)
                    v.reduce_max(cols[:, 16 + tt:17 + tt], xslot, AX.X,
                                 apply_absolute_value=True)
                for g8 in range(KD):
                    v.wait_ge(sTPm, 16 * c + g8 + 1)
                    ins = v.tensor_copy(ffm[:, g8 * TCH:(g8 + 1) * TCH],
                                        pt[g8 % 2][:, 0:512])
                    if g8 == KD - 1:
                        seal_sig(ins, sTPe)
                    else:
                        ins.then_inc(sTPe, 1)
                if c >= 1:
                    v.wait_ge(sDUP, 512 * c)
                    v.wait_ge(sDUPb, 512 * c)
                for m in range(24):
                    v.wait_ge(sQm, 24 * c + m + 1)
                    bank = py[m % 2]
                    if 8 <= m < 16:
                        ins = v.tensor_scalar(qkv_fm[:, m * TCH:(m + 1) * TCH], bank[:, :],
                                              wqb_sb[:, m:m + 1], kscale_sb[:, c:c + 1],
                                              ALU.add, ALU.mult)
                    else:
                        ins = v.tensor_scalar(qkv_fm[:, m * TCH:(m + 1) * TCH], bank[:, :],
                                              wqb_sb[:, m:m + 1], None, ALU.add)
                    if m >= 16:
                        seal_sig(ins, sQe)
                    else:
                        ins.then_inc(sQe, 1)
                seal(v.tensor_tensor(cols[:, 20:20 + NT], cols[:, 12:12 + NT],
                                     cols[:, 12:12 + NT], ALU.mult))
                seal_sig(v.tensor_tensor(cols[:, 20:20 + NT], cols[:, 20:20 + NT],
                                         cols[:, 4:4 + NT], ALU.mult), sST)
                v.wait_ge(sSQ, 2 * c + 2)
                seal(v.reciprocal(cols[:, 28:28 + NT], cols[:, 8:8 + NT]))
                seal(v.tensor_tensor(cols[:, 16:16 + NT], cols[:, 16:16 + NT],
                                     cols[:, 28:28 + NT], ALU.mult))
                seal(v.tensor_scalar(cols[:, 16:16 + NT], cols[:, 16:16 + NT], EPS_ACT, None, ALU.max))
                seal(v.reciprocal(cols[:, 24:24 + NT], cols[:, 16:16 + NT]))
                seal(v.tensor_scalar(cols[:, 24:24 + NT], cols[:, 24:24 + NT], 127.0, None, ALU.mult))
                seal(v.tensor_tensor(cols[:, 28:28 + NT], cols[:, 28:28 + NT],
                                     cols[:, 12:12 + NT], ALU.mult))
                seal(v.tensor_tensor(cols[:, 28:28 + NT], cols[:, 28:28 + NT],
                                     cols[:, 24:24 + NT], ALU.mult))
                seal(v.tensor_scalar(cols[:, 16:16 + NT], cols[:, 16:16 + NT], ws_sb[:, 0:1], None, ALU.mult))
                seal(v.tensor_scalar(cols[:, 16:16 + NT], cols[:, 16:16 + NT], 1.0 / 127.0, None, ALU.mult))
                v.wait_ge(sTPm, 16 * c + 8)
                if "xn" in dbg:
                    v.wait_ge(sDBG, 16 * (len(dbg) * c + 1))
                for tt in range(NT):
                    xs = xstage[:, tt * D:(tt + 1) * D]
                    xslot = xns[:, (tt % 2) * D:(tt % 2 + 1) * D]
                    oslot = outstage[:, (tt % 2) * D:(tt % 2 + 1) * D]
                    v.tensor_scalar(xslot, xs, cols[:, 0 + tt:1 + tt],
                                    cols[:, 28 + tt:29 + tt], ALU.subtract, ALU.mult)
                    v.tensor_scalar(xslot, xslot, 128.5, None, ALU.add)
                    for kb in range(7, -1, -1):
                        v.tensor_scalar(oslot, xslot, float(1 << kb), float(1 << kb),
                                        ALU.is_ge, ALU.mult)
                        v.tensor_tensor(xslot, xslot, oslot, ALU.subtract)
                    v.tensor_scalar(oslot, xs, cols[:, 0 + tt:1 + tt],
                                    cols[:, 28 + tt:29 + tt], ALU.subtract, ALU.mult)
                    v.tensor_tensor(xslot, oslot, xslot, ALU.subtract)
                    v.tensor_scalar(xslot, xslot, 0.5, -128.0, ALU.add, ALU.max)
                    v.tensor_scalar(xslot, xslot, 127.0, cols[:, 16 + tt:17 + tt],
                                    ALU.min, ALU.mult)
                    ins = v.tensor_copy(tmb[:, tt * D:(tt + 1) * D], xslot)
                    if tt == NT - 1:
                        seal_sig(ins, sQF)
                for g8 in range(KD):
                    v.wait_ge(sTPm, 16 * c + 8 + g8 + 1)
                    ins = v.tensor_copy(ffm[:, g8 * TCH:(g8 + 1) * TCH],
                                        pt[g8 % 2][:, 0:512])
                    if g8 == KD - 1:
                        seal_sig(ins, sTPe)
                    else:
                        ins.then_inc(sTPe, 1)
                for rr in range(NRR):
                    h = rr // 4
                    qq = rr % 4
                    qsl = qdup[:, (h % 2) * TCH:(h % 2 + 1) * TCH]
                    ksl = kdup[:, (h % 2) * TCH:(h % 2 + 1) * TCH]
                    for g in range(NGR):
                        GG = 256 * c + 4 * rr + g
                        v.wait_ge(sGV, GG + 1)
                        if g == 0 and rr % 4 == 0:
                            v.wait_ge(sDUP if h % 2 == 0 else sDUPb, 64 * (8 * c + h // 2 + 1))
                        for t2 in range(2):
                            k = g * 2 + t2
                            sl = slice(k * TCH, (k + 1) * TCH)
                            ins = v.tensor_tensor(uz[:, sl], gv[:, sl], ksl, ALU.mult)
                            if t2 == 1:
                                ins.then_inc(sU, 1)
                        for t2 in range(2):
                            k = g * 2 + t2
                            m = qq * 8 + k
                            ct = h * 32 + m
                            sl = slice(k * TCH, (k + 1) * TCH)
                            v.tensor_tensor_scan(gv[:, sl],
                                                 rr_sb[:, ct:ct + 1].broadcast_to([128, TCH]),
                                                 uz[:, sl], scol[:, ct:ct + 1],
                                                 ALU.mult, ALU.add)
                        if GG >= 4:
                            v.wait_ge(sRD, GG - 3)
                        for t2 in range(2):
                            k = g * 2 + t2
                            sl = slice(k * TCH, (k + 1) * TCH)
                            ins = v.tensor_tensor(uz[:, sl], gv[:, sl], qsl, ALU.mult)
                            if t2 == 1:
                                ins.then_inc(sZ, 1)
                    if rr >= 9 and (rr - 9) % 8 == 0:
                        p = (rr - 9) // 8
                        v.wait_ge(sRD, 256 * c + 32 * (p + 1))
                        seal_sig(v.tensor_copy(ybuf[:, p * TCH:(p + 1) * TCH], py[p % 2][:, :]), sYE)
                p = 7
                v.wait_ge(sRD, 256 * c + 32 * (p + 1))
                seal_sig(v.tensor_copy(ybuf[:, p * TCH:(p + 1) * TCH], py[p % 2][:, :]), sYE)
                for m in range(8):
                    v.wait_ge(sOM, 8 * c + m + 1)
                    v.wait_ge(sSIL, 8 * c + m + 1)
                    seal_sig(v.scalar_tensor_tensor(sgbuf[:, m * TCH:(m + 1) * TCH], py[m % 2][:, :],
                                                    ob_sb[:, m:m + 1], gbuf[:, (m % 2) * TCH:(m % 2 + 1) * TCH],
                                                    ALU.add, ALU.mult), sSG)
                for tt in range(NT):
                    v.wait_ge(sSGT, 4 * c + tt + 1)
                    if 2 * c + tt // 2 >= 1:
                        v.wait_ge(sOUT if tt % 2 == 0 else sOUTb, 16 * (2 * c + tt // 2))
                    oslot = outstage[:, (tt % 2) * D:(tt % 2 + 1) * D]
                    v.tensor_tensor(oslot[:, 0:512], pt0[:, 0:512],
                                    xstage[:, tt * D:tt * D + 512], ALU.add)
                    seal_sig(v.tensor_tensor(oslot[:, 512:1024], pt1[:, 0:512],
                                             xstage[:, tt * D + 512:(tt + 1) * D], ALU.add), sRES)
                GDTOT[0] = GDBASE[0] + gd[0]

        @block.scalar
        def _(sc):
            gd2 = [0]
            def seal_sig2(ins, sigsem):
                gd2[0] += 1
                ins.then_inc(sGD2, 1)
                sc.wait_ge(sGD2, gd2[0])
                sc.mul(scr_sb[0:1, 0:1], scr_sb[0:1, 0:1], 1.0).then_inc(sigsem, 1)
            sc.wait_ge(sW3, 16 * 7)
            sc.copy(scr_sb[0:1, 0:1], eps_sb[0:1, 0:1])
            sc.activation(rr_sb[:, :], scol[:, :], AF.Sigmoid, scale=-1.0).then_inc(sLAM, 1)
            for c in range(NC):
                sc.wait_ge(sST, 2 * c + 1)
                seal_sig2(sc.activation(cols[:, 8:8 + NT], cols[:, 4:4 + NT], AF.Sqrt,
                                        bias=eps_sb[:, 0:1]), sSQ)
                sc.wait_ge(sST, 2 * c + 2)
                seal_sig2(sc.activation(cols[:, 8:8 + NT], cols[:, 20:20 + NT], AF.Sqrt,
                                        bias=eps_sb[:, 1:2]), sSQ)
                for rr in range(NRR):
                    h = rr // 4
                    qq = rr % 4
                    for g in range(NGR):
                        GG = 256 * c + 4 * rr + g
                        R = c * NRR + rr
                        sc.wait_ge(sVR, GG + 1)
                        if GG >= 4:
                            sc.wait_ge(sZ, GG - 3)
                            rp = (R - 1) % NRR
                            ct0 = (rp // 4) * 32 + (rp % 4) * 8 + 2 * g
                            k0 = 2 * g
                            sc.activation(scol[:, ct0:ct0 + 2],
                                          gv[:, k0 * TCH + TCH - 1:(k0 + 2) * TCH:TCH],
                                          AF.Copy)
                        for t2 in range(2):
                            k = g * 2 + t2
                            m = qq * 8 + k
                            ct = h * 32 + m
                            bank = pv[(g % 2) * 2 + t2]
                            ins = sc.activation(gv[:, k * TCH:(k + 1) * TCH], bank[:, :],
                                                AF.Copy, scale=gam_sb[:, ct:ct + 1])
                            if t2 == 1:
                                ins.then_inc(sGV, 1)
                for m in range(8):
                    sc.wait_ge(sFM, 8 * c + m + 1)
                    if 8 * c + m >= 2:
                        sc.wait_ge(sSG, 8 * c + m - 1)
                    seal_sig2(sc.activation(gbuf[:, (m % 2) * TCH:(m % 2 + 1) * TCH], py[m % 2][:, :],
                                            AF.Silu, bias=fb_sb[:, m:m + 1]), sSIL)


        @block.gpsimd
        def _(g_):
            for c in range(NC):
                for tt in range(NT):
                    if c == 0 and tt == 0:
                        g_.wait_ge(sWc, NCONV)
                    if c >= 1:
                        g_.wait_ge(sRES, 4 * (c - 1) + tt + 1)
                    g_.dma_start(out=xstage[:, tt * D:(tt + 1) * D],
                                 in_=x_in[c * TCH + tt * 128: c * TCH + (tt + 1) * 128, :]
                                 ).then_inc(sX, 16)

        @block.tensor
        def _(t):
            t.wait_ge(sWc, NCONV)
            for c in range(NC):
                t.wait_ge(sLN, c + 1)
                if c >= 1:
                    t.wait_ge(sRES, 4 * c)
                for g8 in range(KD):
                    if g8 >= 2:
                        t.wait_ge(sTPe, 16 * c + g8 - 1)
                    bank = pt[g8 % 2]
                    for tt in range(NT):
                        ins = t.transpose(bank[:, tt * 128:(tt + 1) * 128],
                                          tmb[:, tt * D + g8 * 128: tt * D + (g8 + 1) * 128],
                                          idn_sb[:, :])
                        if tt == NT - 1:
                            ins.then_inc(sTPm, 1)
                t.wait_ge(sTPe, 16 * c + 8)
                for m in range(24):
                    if m >= 2:
                        t.wait_ge(sQe, 24 * c + m - 1)
                    bank = py[m % 2]
                    for k in range(KD):
                        ins = t.matmul(bank[:, :], wq_sb[:, k * DQ + m * 128: k * DQ + (m + 1) * 128],
                                       ffm[:, k * TCH:(k + 1) * TCH],
                                       start=(k == 0), stop=(k == KD - 1))
                        if k == KD - 1:
                            ins.then_inc(sQm, 1)
                t.wait_ge(sQF, c + 1)
                for g8 in range(KD):
                    if g8 >= 2:
                        t.wait_ge(sTPe, 16 * c + 8 + g8 - 1)
                    bank = pt[g8 % 2]
                    for tt in range(NT):
                        ins = t.transpose(bank[:, tt * 128:(tt + 1) * 128],
                                          tmb[:, tt * D + g8 * 128: tt * D + (g8 + 1) * 128],
                                          idn_sb[:, :])
                        if tt == NT - 1:
                            ins.then_inc(sTPm, 1)

                def emit_vrep(rr):
                    h = rr // 4
                    qq = rr % 4
                    if qq == 0:
                        t.wait_ge(sQe, 24 * c + 16 + h // 2 + 1)
                    for g in range(NGR):
                        GG = 256 * c + 4 * rr + g
                        if GG >= 2:
                            t.wait_ge(sGV, GG - 1)
                        for t2 in range(2):
                            k = g * 2 + t2
                            m = qq * 8 + k
                            bank = pv[(g % 2) * 2 + t2]
                            eb = (h % 2) * 64
                            ins = t.matmul(bank[:, :], esel_sb[eb:eb + 64, m * 128:(m + 1) * 128],
                                           vtile_ap(h), start=True, stop=True)
                            if t2 == 1:
                                ins.then_inc(sVR, 1)

                def emit_reduce(rr):
                    h = rr // 4
                    qq = rr % 4
                    p = h // 2
                    rowbase = (h % 2) * 64
                    bank = py[p % 2]
                    if qq == 0:
                        gp = 8 * c + p
                        if h % 2 == 0 and gp >= 2:
                            t.wait_ge(sYE, gp - 1)
                        if h == 0:
                            t.wait_ge(sQe, 24 * c + 23)
                        elif h == 2:
                            t.wait_ge(sQe, 24 * c + 24)
                        wb = (h % 2) * 64
                        t.matmul(bank[rowbase:rowbase + 64, :],
                                 wst_sb[wb:wb + 64, h * 64:(h + 1) * 64], qtile_ap(h),
                                 start=True, stop=False, skip_group_check=True)
                    for g in range(NGR):
                        GG = 256 * c + 4 * rr + g
                        t.wait_ge(sZ, GG + 1)
                        for t2 in range(2):
                            k = g * 2 + t2
                            m = qq * 8 + k
                            last = (qq == 3 and k == 7)
                            ins = t.matmul(bank[rowbase:rowbase + 64, :],
                                           dsel_sb[:, 64 - 2 * m:128 - 2 * m],
                                           uz[:, k * TCH:(k + 1) * TCH],
                                           start=False, stop=last, skip_group_check=True)
                            if t2 == 1:
                                ins.then_inc(sRD, 1)

                emit_vrep(0)
                for rr in range(1, NRR):
                    emit_vrep(rr)
                    emit_reduce(rr - 1)
                emit_reduce(NRR - 1)
                t.wait_ge(sTPe, 16 * c + 16)
                for pair in range(4):
                    for m in (2 * pair, 2 * pair + 1):
                        if pair == 0:
                            t.wait_ge(sYE, 8 * c + 7 + m)
                        if m >= 2:
                            t.wait_ge(sSG, 8 * c + m - 1)
                        bank = py[m % 2]
                        for k in range(KD):
                            ins = t.matmul(bank[:, :], wf_sb[:, k * D + m * 128: k * D + (m + 1) * 128],
                                           ffm[:, k * TCH:(k + 1) * TCH],
                                           start=(k == 0), stop=(k == KD - 1))
                            if k == KD - 1:
                                ins.then_inc(sFM, 1)
                    if pair == 0:
                        t.wait_ge(sYE, 8 * (c + 1))
                    for m in (2 * pair, 2 * pair + 1):
                        t.wait_ge(sSIL, 8 * c + m + 1)
                        bank = py[m % 2]
                        for k in range(KD):
                            ins = t.matmul(bank[:, :], wo_sb[:, k * D + m * 128: k * D + (m + 1) * 128],
                                           ybuf[:, k * TCH:(k + 1) * TCH],
                                           start=(k == 0), stop=(k == KD - 1))
                            if k == KD - 1:
                                ins.then_inc(sOM, 1)
                t.wait_ge(sSG, 8 * (c + 1))
                for tt in range(NT):
                    if tt >= 1:
                        t.wait_ge(sRES, 4 * c + tt)
                    for o8 in range(KD):
                        bank = pt[o8 // 4]
                        ins = t.transpose(bank[:, (o8 % 4) * 128:(o8 % 4 + 1) * 128],
                                          sgbuf[:, o8 * TCH + tt * 128: o8 * TCH + (tt + 1) * 128],
                                          idn_sb[:, :])
                        if o8 == KD - 1:
                            ins.then_inc(sSGT, 1)

    return nc


def host_weights(inputs):
    W = {}
    W["wq"] = np.ascontiguousarray(np.asarray(inputs["L0_attn_Wqkv_w"], np.float32).T)
    W["wqb"] = np.ascontiguousarray(np.asarray(inputs["L0_attn_Wqkv_b"], np.float32).reshape(24, 128).T)
    W["wo"] = np.ascontiguousarray(np.asarray(inputs["L0_attn_out_w"], np.float32).T)
    W["ob"] = np.ascontiguousarray(np.asarray(inputs["L0_attn_out_b"], np.float32).reshape(8, 128).T)
    W["wf"] = np.ascontiguousarray(np.asarray(inputs["L0_ffn_w_int8"], np.int32).T)
    W["fb"] = np.ascontiguousarray(np.asarray(inputs["L0_ffn_bias"], np.float32).reshape(8, 128).T)
    W["ws"] = np.ascontiguousarray(np.broadcast_to(
        np.asarray(inputs["L0_ffn_weight_scale"], np.float32).reshape(1, 1), (128, 1)))
    lam = np.asarray(inputs["L0_attn_Lambda_raw"], np.float32)
    gam = np.asarray(inputs["L0_attn_Gamma"], np.float32)
    W["lam"] = np.ascontiguousarray(lam.reshape(H, 32, 2, 64).transpose(2, 3, 0, 1).reshape(128, 512))
    W["gam"] = np.ascontiguousarray(gam.reshape(H, 32, 2, 64).transpose(2, 3, 0, 1).reshape(128, 512))
    wst = np.asarray(inputs["L0_attn_W_static"], np.float32)
    wstj = wst.transpose(2, 0, 1).reshape(64, H * 64)
    W["wst"] = np.ascontiguousarray(np.concatenate([wstj, wstj], 0))
    esel = np.zeros((64, 32 * 128), np.float32)
    for m in range(32):
        esel[2 * m, m * 128:m * 128 + 64] = 1.0
        esel[2 * m + 1, m * 128 + 64:m * 128 + 128] = 1.0
    W["esel"] = np.concatenate([esel, esel], 0)
    dsel = np.zeros((128, 128), np.float32)
    for p2 in range(128):
        dsel[p2, 64 + p2 // 64] = 1.0
    W["dsel"] = dsel
    W["idn"] = np.eye(128, dtype=np.float32)
    return W


N_WARM = 2
N_CHUNKS = 10
_CACHE = {}


def _get_nc():
    if "nc" not in _CACHE:
        _CACHE["nc"] = build_program(N_CHUNKS, N_WARM)
    return _CACHE["nc"]


def kernel(L0_attn_x, L0_attn_Wqkv_w, L0_attn_Wqkv_b, L0_attn_out_w, L0_attn_out_b,
           L0_attn_W_static, L0_attn_Lambda_raw, L0_attn_Gamma, L0_ffn_w_int8,
           L0_ffn_weight_scale, L0_ffn_bias):
    from concourse.bass_utils import run_bass_kernel_spmd
    inputs = dict(L0_attn_Wqkv_w=L0_attn_Wqkv_w, L0_attn_Wqkv_b=L0_attn_Wqkv_b,
                  L0_attn_out_w=L0_attn_out_w, L0_attn_out_b=L0_attn_out_b,
                  L0_attn_W_static=L0_attn_W_static, L0_attn_Lambda_raw=L0_attn_Lambda_raw,
                  L0_attn_Gamma=L0_attn_Gamma, L0_ffn_w_int8=L0_ffn_w_int8,
                  L0_ffn_weight_scale=L0_ffn_weight_scale, L0_ffn_bias=L0_ffn_bias)
    W = host_weights(inputs)
    x = np.asarray(L0_attn_x, np.float32)
    B, L, _ = x.shape
    TQ = L // 4
    WARM = N_WARM * TCH
    in_maps = []
    for c in range(8):
        b, q = c // 4, c % 4
        ks = np.full((128, N_CHUNKS), 0.125, np.float32)
        if q == 0:
            warm = x[b, 0:WARM]
            ks[:, :N_WARM] = 0.0
        else:
            warm = x[b, q * TQ - WARM: q * TQ]
        xc = np.ascontiguousarray(np.concatenate([warm, x[b, q * TQ:(q + 1) * TQ]], 0))
        in_maps.append(dict(x=xc, kscale=ks, **W))
    nc = _get_nc()
    res = run_bass_kernel_spmd(nc, in_maps, core_ids=list(range(8)))
    parts = [res.results[c]["out"] for c in range(8)]
    out = np.empty((B, L, D), np.float32)
    for c in range(8):
        b, q = c // 4, c % 4
        out[b, q * TQ:(q + 1) * TQ] = np.asarray(parts[c], np.float32)
    return out


# version 11
# speedup vs baseline: 1.0590x; 1.0550x over previous
import contextlib
import math
import numpy as np
import concourse.bass as bass
import concourse.mybir as mybir

F32 = mybir.dt.float32
BF16 = mybir.dt.bfloat16
I32 = mybir.dt.int32
ALU = mybir.AluOpType
AF = mybir.ActivationFunctionType
AX = mybir.AxisListType

H, DH, D, DQ = 16, 64, 1024, 3072
TCH = 512
NT = TCH // 128
KD = D // 128
NTL = 8
NRR = 64
NGR = 4
EPS_LN = 1e-5
EPS_RMS = 1e-6
EPS_ACT = 1e-5


def build_program(n_chunks, n_warm, debug_outputs=()):
    nc = bass.Bass("TRN2", target_bir_lowering=False, debug=False, detect_race_conditions=False)
    NC = n_chunks
    T_ALL = NC * TCH
    T_OUT = (NC - n_warm) * TCH

    def TPB(c):
        return 8 * c + 8 * max(c - n_warm, 0)

    x_in = nc.dram_tensor("x", [T_ALL, D], F32, kind="ExternalInput").ap()
    kscale_in = nc.dram_tensor("kscale", [128, NC], F32, kind="ExternalInput").ap()
    wq_in = nc.dram_tensor("wq", [D, DQ], F32, kind="ExternalInput").ap()
    wqb_in = nc.dram_tensor("wqb", [128, 24], F32, kind="ExternalInput").ap()
    wo_in = nc.dram_tensor("wo", [D, D], F32, kind="ExternalInput").ap()
    ob_in = nc.dram_tensor("ob", [128, 8], F32, kind="ExternalInput").ap()
    wf_in = nc.dram_tensor("wf", [D, D], I32, kind="ExternalInput").ap()
    fb_in = nc.dram_tensor("fb", [128, 8], F32, kind="ExternalInput").ap()
    ws_in = nc.dram_tensor("ws", [128, 1], F32, kind="ExternalInput").ap()
    lam_in = nc.dram_tensor("lam", [128, 512], F32, kind="ExternalInput").ap()
    gam_in = nc.dram_tensor("gam", [128, 512], F32, kind="ExternalInput").ap()
    wst_in = nc.dram_tensor("wst", [128, H * 64], F32, kind="ExternalInput").ap()
    esel_in = nc.dram_tensor("esel", [128, 32 * 128], F32, kind="ExternalInput").ap()
    dsel_in = nc.dram_tensor("dsel", [128, 128], F32, kind="ExternalInput").ap()
    idn_in = nc.dram_tensor("idn", [128, 128], F32, kind="ExternalInput").ap()
    out_dram = nc.dram_tensor("out", [T_OUT, D], F32, kind="ExternalOutput").ap()
    dbg = {}
    for name, shape in (("xn", [T_ALL, D]), ("qkv", [DQ, T_ALL]), ("y", [D, T_ALL]),
                        ("xq", [D, T_ALL]), ("s", [D, T_ALL])):
        if name in debug_outputs:
            dbg[name] = nc.dram_tensor("dbg_" + name, shape, BF16, kind="ExternalOutput").ap()

    with contextlib.ExitStack() as _st:
        def sb(name, shape, dt):
            return _st.enter_context(nc.sbuf_tensor(name, shape, dt))
        def psa(name):
            return _st.enter_context(nc.psum_tensor(name, [128, 512], F32))
        def sem(name):
            return _st.enter_context(nc.semaphore(name))

        wq_sb = sb("wq_sb", [128, KD * DQ], BF16)
        wo_sb = sb("wo_sb", [128, KD * D], BF16)
        wf_sb = sb("wf_sb", [128, KD * D], BF16)
        wqb_sb = sb("wqb_sb", [128, 24], F32)
        ob_sb = sb("ob_sb", [128, 8], F32)
        fb_sb = sb("fb_sb", [128, 8], F32)
        ws_sb = sb("ws_sb", [128, 1], F32)
        rr_sb = sb("rr_sb", [128, 512], F32)
        gam_sb = sb("gam_sb", [128, 512], F32)
        wst_sb = sb("wst_sb", [128, H * 64], BF16)
        esel_sb = sb("esel_sb", [128, 32 * 128], BF16)
        dsel_sb = sb("dsel_sb", [128, 128], BF16)
        idn_sb = sb("idn_sb", [128, 128], BF16)
        kscale_sb = sb("kscale_sb", [128, NC], F32)
        eps_sb = sb("eps_sb", [128, 2], F32)
        scr_sb = sb("scr_sb", [1, 4], F32)
        xstage = sb("xstage", [128, NT * D], F32)
        stats = sb("stats", [128, NT * 12], F32)
        cols = sb("cols", [128, 32], F32)
        xns = sb("xns", [128, 2 * D], F32)
        tmb = sb("tmb", [128, NT * D], BF16)
        ffm = sb("ffm", [128, KD * TCH], BF16)
        qkv_fm = sb("qkv_fm", [128, 3 * KD * TCH], BF16)
        kdup = sb("kdup", [128, 2 * TCH], BF16)
        qdup = sb("qdup", [128, 2 * TCH], BF16)
        gv = sb("gv", [128, NTL * TCH], BF16)
        uz = sb("uz", [128, NTL * TCH], BF16)
        scol = sb("scol", [128, 512], F32)
        stg32 = sb("stg32", [128, 512], I32)
        ybuf = sb("ybuf", [128, KD * TCH], BF16)
        gbuf = tmb
        sgbuf = sb("sgbuf", [128, KD * TCH], BF16)
        outstage = sb("outstage", [128, 2 * D], F32)
        pt0 = _st.enter_context(nc.psum_tensor("pt0", [128, 1024], BF16))
        pt1 = _st.enter_context(nc.psum_tensor("pt1", [128, 1024], BF16))
        py0, py1 = psa("py0"), psa("py1")
        pv0, pv1, pv2, pv3 = psa("pv0"), psa("pv1"), psa("pv2"), psa("pv3")
        sX, sW, sWc = sem("sX"), sem("sW"), sem("sWc")
        sW2, sW3 = sem("sW2"), sem("sW3")
        sWI, sLAM = sem("sWI"), sem("sLAM")
        sWS = [sem("sWS%d" % k) for k in range(8)]
        sXT = [sem("sXT%d" % k) for k in range(NT)]
        sDUPb, sOUTb = sem("sDUPb"), sem("sOUTb")
        sST, sSQ, sLN, sQF = sem("sST"), sem("sSQ"), sem("sLN"), sem("sQF")
        sTPm, sTPe, sQm, sQe = sem("sTPm"), sem("sTPe"), sem("sQm"), sem("sQe")
        sDUP = sem("sDUP")
        sVR, sGV, sU, sSC = sem("sVR"), sem("sGV"), sem("sU"), sem("sSC")
        sZ, sRD, sYE = sem("sZ"), sem("sRD"), sem("sYE")
        sFM, sSIL, sOM, sSG = sem("sFM"), sem("sSIL"), sem("sOM"), sem("sSG")
        sSGT, sRES, sOUT, sDBG = sem("sSGT"), sem("sRES"), sem("sOUT"), sem("sDBG")
        sGD = sem("sGD")
        sGD2 = sem("sGD2")
        block = _st.enter_context(nc.Block())

        pv = [pv0, pv1, pv2, pv3]
        py = [py0, py1]
        pt = [pt0, pt1]

        floads = []
        for k in range(KD):
            for j in range(6):
                floads.append((wq_sb, k * DQ + j * 512,
                               wq_in[k * 128:(k + 1) * 128, j * 512:(j + 1) * 512], 512))
        for k in range(KD):
            for j in range(2):
                floads.append((wo_sb, k * D + j * 512,
                               wo_in[k * 128:(k + 1) * 128, j * 512:(j + 1) * 512], 512))
        for j in range(2):
            floads.append((wst_sb, j * 512, wst_in[:, j * 512:(j + 1) * 512], 512))
        for j in range(8):
            floads.append((esel_sb, j * 512, esel_in[:, j * 512:(j + 1) * 512], 512))
        floads.append((dsel_sb, 0, dsel_in[:, :], 128))
        floads.append((idn_sb, 0, idn_in[:, :], 128))
        NINT = 2 * KD
        NCONV = NINT + len(floads)

        def vtile_ap(h):
            m = 16 + h // 2
            return qkv_fm[(h % 2) * 64:(h % 2) * 64 + 64, m * TCH:(m + 1) * TCH]

        def qtile_ap(h):
            m = h // 2
            return qkv_fm[(h % 2) * 64:(h % 2) * 64 + 64, m * TCH:(m + 1) * TCH]

        def ktile_ap(h):
            m = 8 + h // 2
            return qkv_fm[(h % 2) * 64:(h % 2) * 64 + 64, m * TCH:(m + 1) * TCH]

        @block.sync
        def _(s):
            for j in range(NINT):
                if j >= 1:
                    s.wait_ge(sWc, j)
                k, jj = j // 2, j % 2
                s.dma_start(out=stg32[:, :],
                            in_=wf_in[k * 128:(k + 1) * 128, jj * 512:(jj + 1) * 512]
                            ).then_inc(sWI, 16)
            for i, (dst, off, src, w) in enumerate(floads):
                if i >= 8:
                    s.wait_ge(sWc, NINT + i - 7)
                slot = (i % 8) * 512
                s.dma_start(out=xstage[:, slot:slot + w],
                            in_=src).then_inc(sWS[i % 8], 16)
            s.dma_start(out=wqb_sb[:, :], in_=wqb_in).then_inc(sW3, 16)
            s.dma_start(out=ob_sb[:, :], in_=ob_in).then_inc(sW3, 16)
            s.dma_start(out=fb_sb[:, :], in_=fb_in).then_inc(sW3, 16)
            s.dma_start(out=ws_sb[:, :], in_=ws_in).then_inc(sW3, 16)
            s.dma_start(out=scol[:, :], in_=lam_in).then_inc(sW3, 16)
            s.dma_start(out=gam_sb[:, :], in_=gam_in).then_inc(sW3, 16)
            s.dma_start(out=kscale_sb[:, :], in_=kscale_in).then_inc(sW3, 16)
            for c in range(NC):
                if "xn" in dbg:
                    s.wait_ge(sLN, c + 1)
                    s.dma_start(out=dbg["xn"][c * TCH:(c + 1) * TCH, :]
                                .rearrange("(n p) d -> p n d", p=128), in_=tmb[:, :]).then_inc(sDBG, 16)
                for h in range(H):
                    gh = 16 * c + h
                    s.wait_ge(sQe, 24 * c + 16 + h // 2 + 1)
                    if gh >= 2:
                        s.wait_ge(sU, 16 * (gh - 1))
                        s.wait_ge(sZ, 16 * (gh - 1))
                    slot = (h % 2) * TCH
                    dsem = sDUP if h % 2 == 0 else sDUPb
                    s.dma_start(out=kdup[0:64, slot:slot + TCH], in_=ktile_ap(h)).then_inc(dsem, 16)
                    s.dma_start(out=kdup[64:128, slot:slot + TCH], in_=ktile_ap(h)).then_inc(dsem, 16)
                    s.dma_start(out=qdup[0:64, slot:slot + TCH], in_=qtile_ap(h)).then_inc(dsem, 16)
                    s.dma_start(out=qdup[64:128, slot:slot + TCH], in_=qtile_ap(h)).then_inc(dsem, 16)
                if "qkv" in dbg:
                    s.wait_ge(sQe, 24 * (c + 1))
                    s.dma_start(out=dbg["qkv"][:, c * TCH:(c + 1) * TCH]
                                .rearrange("(m p) t -> p m t", p=128), in_=qkv_fm[:, :]).then_inc(sDBG, 16)
                if "y" in dbg and c >= n_warm:
                    s.wait_ge(sYE, 8 * (c - n_warm + 1))
                    s.dma_start(out=dbg["y"][:, c * TCH:(c + 1) * TCH]
                                .rearrange("(m p) t -> p m t", p=128), in_=ybuf[:, :]).then_inc(sDBG, 16)
                if "xq" in dbg and c >= n_warm:
                    s.wait_ge(sTPe, TPB(c) + 16)
                    s.dma_start(out=dbg["xq"][:, c * TCH:(c + 1) * TCH]
                                .rearrange("(m p) t -> p m t", p=128), in_=ffm[:, :]).then_inc(sDBG, 16)
                if "s" in dbg and c >= n_warm:
                    s.wait_ge(sSG, 8 * (c - n_warm + 1))
                    s.dma_start(out=dbg["s"][:, c * TCH:(c + 1) * TCH]
                                .rearrange("(m p) t -> p m t", p=128), in_=sgbuf[:, :]).then_inc(sDBG, 16)
                if c >= n_warm:
                    co = c - n_warm
                    for tt in range(NT):
                        s.wait_ge(sRES, 4 * co + tt + 1)
                        osem = sOUT if tt % 2 == 0 else sOUTb
                        s.dma_start(out=out_dram[co * TCH + tt * 128: co * TCH + (tt + 1) * 128, :],
                                    in_=outstage[:, (tt % 2) * D:(tt % 2 + 1) * D]).then_inc(osem, 16)
            s.wait_ge(sOUT, 16 * (NC - n_warm) * 2)
            s.wait_ge(sOUTb, 16 * (NC - n_warm) * 2)

        @block.vector
        def _(v):
            v.memset(cols[:, :], 0.0)
            v.memset(eps_sb[:, 0:1], EPS_LN)
            v.memset(eps_sb[:, 1:2], EPS_RMS)
            for j in range(NINT):
                v.wait_ge(sWI, 16 * (j + 1))
                k, jj = j // 2, j % 2
                v.tensor_copy(wf_sb[:, k * D + jj * 512:k * D + (jj + 1) * 512],
                              stg32[:, :]).then_inc(sWc, 1)
            nseal = 0
            for i, (dst, off, src, w) in enumerate(floads):
                v.wait_ge(sWS[i % 8], 16 * (i // 8 + 1))
                slot = (i % 8) * 512
                ins = v.tensor_copy(dst[:, off:off + w], xstage[:, slot:slot + w])
                if w == 128:
                    nseal += 1
                    ins.then_inc(sGD, 1)
                    v.wait_ge(sGD, nseal)
                    v.engine_nop().then_inc(sWc, 1)
                else:
                    ins.then_inc(sWc, 1)
            v.wait_ge(sLAM, 1)
            v.memset(scol[:, :], 0.0)
            v.wait_ge(sW3, 16 * 7)

            GDBASE = [0]
            GDTOT = [nseal]
            for c in range(NC):
                GDBASE[0] = GDTOT[0]
                pass
                if dbg and c >= 1:
                    v.wait_ge(sDBG, 16 * len(dbg) * c)
                gd = [0]
                def seal(ins):
                    gd[0] += 1
                    ins.then_inc(sGD, 1)
                    v.wait_ge(sGD, GDBASE[0] + gd[0])
                def seal_sig(ins, sigsem, inc=1):
                    seal(ins)
                    v.engine_nop().then_inc(sigsem, inc)
                v.wait_ge(sOUT, 16 * 2 * max(c - n_warm, 0))
                v.wait_ge(sOUTb, 16 * 2 * max(c - n_warm, 0))
                for tt in range(NT):
                    xs = xstage[:, tt * D:(tt + 1) * D]
                    v.wait_ge(sXT[tt], 16 * (c + 1))
                    seal(v.reduce_sum(cols[:, 0 + tt:1 + tt], xs, AX.X))
                seal(v.tensor_scalar(cols[:, 0:NT], cols[:, 0:NT], 1.0 / D, None, ALU.mult))
                for tt in range(NT):
                    xs = xstage[:, tt * D:(tt + 1) * D]
                    xslot = xns[:, (tt % 2) * D:(tt % 2 + 1) * D]
                    oslot = outstage[:, (tt % 2) * D:(tt % 2 + 1) * D]
                    v.tensor_scalar(xslot, xs, cols[:, 0 + tt:1 + tt], None, ALU.subtract)
                    seal(v.tensor_tensor(oslot, xslot, xslot, ALU.mult))
                    seal(v.reduce_sum(cols[:, 4 + tt:5 + tt], oslot, AX.X))
                seal_sig(v.tensor_scalar(cols[:, 4:4 + NT], cols[:, 4:4 + NT], 1.0 / D, None, ALU.mult), sST)
                v.wait_ge(sSQ, 2 * c + 1)
                seal(v.reciprocal(cols[:, 12:12 + NT], cols[:, 8:8 + NT]))
                for tt in range(NT):
                    xs = xstage[:, tt * D:(tt + 1) * D]
                    xslot = xns[:, (tt % 2) * D:(tt % 2 + 1) * D]
                    v.tensor_scalar(xslot, xs, cols[:, 0 + tt:1 + tt],
                                    cols[:, 12 + tt:13 + tt], ALU.subtract, ALU.mult)
                    ins = v.tensor_copy(tmb[:, tt * D:(tt + 1) * D], xslot)
                    if tt == NT - 1:
                        seal_sig(ins, sLN)
                    v.reduce_max(cols[:, 16 + tt:17 + tt], xslot, AX.X,
                                 apply_absolute_value=True)
                for g8 in range(KD):
                    v.wait_ge(sTPm, TPB(c) + g8 + 1)
                    ins = v.tensor_copy(ffm[:, g8 * TCH:(g8 + 1) * TCH],
                                        pt[g8 % 2][:, 0:512])
                    if g8 == KD - 1:
                        seal_sig(ins, sTPe)
                    else:
                        ins.then_inc(sTPe, 1)
                if c >= 1:
                    v.wait_ge(sDUP, 512 * c)
                    v.wait_ge(sDUPb, 512 * c)
                for m in range(24):
                    v.wait_ge(sQm, 24 * c + m + 1)
                    bank = py[m % 2]
                    if 8 <= m < 16:
                        ins = v.tensor_scalar(qkv_fm[:, m * TCH:(m + 1) * TCH], bank[:, :],
                                              wqb_sb[:, m:m + 1], kscale_sb[:, c:c + 1],
                                              ALU.add, ALU.mult)
                    else:
                        ins = v.tensor_scalar(qkv_fm[:, m * TCH:(m + 1) * TCH], bank[:, :],
                                              wqb_sb[:, m:m + 1], None, ALU.add)
                    if m >= 16:
                        seal_sig(ins, sQe)
                    else:
                        ins.then_inc(sQe, 1)
                seal(v.tensor_tensor(cols[:, 20:20 + NT], cols[:, 12:12 + NT],
                                     cols[:, 12:12 + NT], ALU.mult))
                seal_sig(v.tensor_tensor(cols[:, 20:20 + NT], cols[:, 20:20 + NT],
                                         cols[:, 4:4 + NT], ALU.mult), sST)
                v.wait_ge(sSQ, 2 * c + 2)
                seal(v.reciprocal(cols[:, 28:28 + NT], cols[:, 8:8 + NT]))
                seal(v.tensor_tensor(cols[:, 16:16 + NT], cols[:, 16:16 + NT],
                                     cols[:, 28:28 + NT], ALU.mult))
                seal(v.tensor_scalar(cols[:, 16:16 + NT], cols[:, 16:16 + NT], EPS_ACT, None, ALU.max))
                seal(v.reciprocal(cols[:, 24:24 + NT], cols[:, 16:16 + NT]))
                seal(v.tensor_scalar(cols[:, 24:24 + NT], cols[:, 24:24 + NT], 127.0, None, ALU.mult))
                seal(v.tensor_tensor(cols[:, 28:28 + NT], cols[:, 28:28 + NT],
                                     cols[:, 12:12 + NT], ALU.mult))
                seal(v.tensor_tensor(cols[:, 28:28 + NT], cols[:, 28:28 + NT],
                                     cols[:, 24:24 + NT], ALU.mult))
                seal(v.tensor_scalar(cols[:, 16:16 + NT], cols[:, 16:16 + NT], ws_sb[:, 0:1], None, ALU.mult))
                seal(v.tensor_scalar(cols[:, 16:16 + NT], cols[:, 16:16 + NT], 1.0 / 127.0, None, ALU.mult))
                if "xn" in dbg:
                    v.wait_ge(sDBG, 16 * (len(dbg) * c + 1))
                if c >= n_warm:
                    v.wait_ge(sTPm, TPB(c) + 8)
                    for tt in range(NT):
                        xs = xstage[:, tt * D:(tt + 1) * D]
                        xslot = xns[:, (tt % 2) * D:(tt % 2 + 1) * D]
                        oslot = outstage[:, (tt % 2) * D:(tt % 2 + 1) * D]
                        v.tensor_scalar(xslot, xs, cols[:, 0 + tt:1 + tt],
                                        cols[:, 28 + tt:29 + tt], ALU.subtract, ALU.mult)
                        v.tensor_scalar(xslot, xslot, 128.5, None, ALU.add)
                        for kb in range(7, -1, -1):
                            v.tensor_scalar(oslot, xslot, float(1 << kb), float(1 << kb),
                                            ALU.is_ge, ALU.mult)
                            v.tensor_tensor(xslot, xslot, oslot, ALU.subtract)
                        v.tensor_scalar(oslot, xs, cols[:, 0 + tt:1 + tt],
                                        cols[:, 28 + tt:29 + tt], ALU.subtract, ALU.mult)
                        v.tensor_tensor(xslot, oslot, xslot, ALU.subtract)
                        v.tensor_scalar(xslot, xslot, 0.5, -128.0, ALU.add, ALU.max)
                        v.tensor_scalar(xslot, xslot, 127.0, cols[:, 16 + tt:17 + tt],
                                        ALU.min, ALU.mult)
                        ins = v.tensor_copy(tmb[:, tt * D:(tt + 1) * D], xslot)
                        if tt == NT - 1:
                            seal_sig(ins, sQF)
                    for g8 in range(KD):
                        v.wait_ge(sTPm, TPB(c) + 8 + g8 + 1)
                        ins = v.tensor_copy(ffm[:, g8 * TCH:(g8 + 1) * TCH],
                                            pt[g8 % 2][:, 0:512])
                        if g8 == KD - 1:
                            seal_sig(ins, sTPe)
                        else:
                            ins.then_inc(sTPe, 1)
                for rr in range(NRR):
                    h = rr // 4
                    qq = rr % 4
                    qsl = qdup[:, (h % 2) * TCH:(h % 2 + 1) * TCH]
                    ksl = kdup[:, (h % 2) * TCH:(h % 2 + 1) * TCH]
                    for g in range(NGR):
                        GG = 256 * c + 4 * rr + g
                        v.wait_ge(sGV, GG + 1)
                        if g == 0 and rr % 4 == 0:
                            v.wait_ge(sDUP if h % 2 == 0 else sDUPb, 64 * (8 * c + h // 2 + 1))
                        for t2 in range(2):
                            k = g * 2 + t2
                            sl = slice(k * TCH, (k + 1) * TCH)
                            ins = v.tensor_tensor(uz[:, sl], gv[:, sl], ksl, ALU.mult)
                            if t2 == 1:
                                ins.then_inc(sU, 1)
                        for t2 in range(2):
                            k = g * 2 + t2
                            m = qq * 8 + k
                            ct = h * 32 + m
                            sl = slice(k * TCH, (k + 1) * TCH)
                            ins = v.tensor_tensor_scan(gv[:, sl],
                                                       rr_sb[:, ct:ct + 1].broadcast_to([128, TCH]),
                                                       uz[:, sl], scol[:, ct:ct + 1],
                                                       ALU.mult, ALU.add)
                            if c < n_warm and t2 == 1:
                                ins.then_inc(sZ, 1)
                        if c >= n_warm:
                            GGr = 256 * (c - n_warm) + 4 * rr + g
                            if GGr >= 4:
                                v.wait_ge(sRD, GGr - 3)
                            for t2 in range(2):
                                k = g * 2 + t2
                                sl = slice(k * TCH, (k + 1) * TCH)
                                ins = v.tensor_tensor(uz[:, sl], gv[:, sl], qsl, ALU.mult)
                                if t2 == 1:
                                    ins.then_inc(sZ, 1)
                    if c >= n_warm and rr >= 9 and (rr - 9) % 8 == 0:
                        p = (rr - 9) // 8
                        v.wait_ge(sRD, 256 * (c - n_warm) + 32 * (p + 1))
                        seal_sig(v.tensor_copy(ybuf[:, p * TCH:(p + 1) * TCH], py[p % 2][:, :]), sYE)
                if c >= n_warm:
                    co = c - n_warm
                    p = 7
                    v.wait_ge(sRD, 256 * co + 32 * (p + 1))
                    seal_sig(v.tensor_copy(ybuf[:, p * TCH:(p + 1) * TCH], py[p % 2][:, :]), sYE)
                    for m in range(8):
                        v.wait_ge(sOM, 8 * co + m + 1)
                        v.wait_ge(sSIL, 8 * co + m + 1)
                        seal_sig(v.scalar_tensor_tensor(sgbuf[:, m * TCH:(m + 1) * TCH], py[m % 2][:, :],
                                                        ob_sb[:, m:m + 1], gbuf[:, (m % 2) * TCH:(m % 2 + 1) * TCH],
                                                        ALU.add, ALU.mult), sSG)
                    for tt in range(NT):
                        v.wait_ge(sSGT, 4 * co + tt + 1)
                        if 2 * co + tt // 2 >= 1:
                            v.wait_ge(sOUT if tt % 2 == 0 else sOUTb, 16 * (2 * co + tt // 2))
                        oslot = outstage[:, (tt % 2) * D:(tt % 2 + 1) * D]
                        v.tensor_tensor(oslot[:, 0:512], pt0[:, 0:512],
                                        xstage[:, tt * D:tt * D + 512], ALU.add)
                        seal_sig(v.tensor_tensor(oslot[:, 512:1024], pt1[:, 0:512],
                                                 xstage[:, tt * D + 512:(tt + 1) * D], ALU.add), sRES)
                GDTOT[0] = GDBASE[0] + gd[0]

        @block.scalar
        def _(sc):
            gd2 = [0]
            def seal_sig2(ins, sigsem):
                gd2[0] += 1
                ins.then_inc(sGD2, 1)
                sc.wait_ge(sGD2, gd2[0])
                sc.mul(scr_sb[0:1, 0:1], scr_sb[0:1, 0:1], 1.0).then_inc(sigsem, 1)
            sc.wait_ge(sW3, 16 * 7)
            sc.copy(scr_sb[0:1, 0:1], eps_sb[0:1, 0:1])
            sc.activation(rr_sb[:, :], scol[:, :], AF.Sigmoid, scale=-1.0).then_inc(sLAM, 1)
            for c in range(NC):
                sc.wait_ge(sST, 2 * c + 1)
                seal_sig2(sc.activation(cols[:, 8:8 + NT], cols[:, 4:4 + NT], AF.Sqrt,
                                        bias=eps_sb[:, 0:1]), sSQ)
                sc.wait_ge(sST, 2 * c + 2)
                seal_sig2(sc.activation(cols[:, 8:8 + NT], cols[:, 20:20 + NT], AF.Sqrt,
                                        bias=eps_sb[:, 1:2]), sSQ)
                for rr in range(NRR):
                    h = rr // 4
                    qq = rr % 4
                    for g in range(NGR):
                        GG = 256 * c + 4 * rr + g
                        R = c * NRR + rr
                        sc.wait_ge(sVR, GG + 1)
                        if GG >= 4:
                            sc.wait_ge(sZ, GG - 3)
                            rp = (R - 1) % NRR
                            ct0 = (rp // 4) * 32 + (rp % 4) * 8 + 2 * g
                            k0 = 2 * g
                            sc.activation(scol[:, ct0:ct0 + 2],
                                          gv[:, k0 * TCH + TCH - 1:(k0 + 2) * TCH:TCH],
                                          AF.Copy)
                        for t2 in range(2):
                            k = g * 2 + t2
                            m = qq * 8 + k
                            ct = h * 32 + m
                            bank = pv[(g % 2) * 2 + t2]
                            ins = sc.activation(gv[:, k * TCH:(k + 1) * TCH], bank[:, :],
                                                AF.Copy, scale=gam_sb[:, ct:ct + 1])
                            if t2 == 1:
                                ins.then_inc(sGV, 1)
                if c >= n_warm:
                    co = c - n_warm
                    for m in range(8):
                        sc.wait_ge(sFM, 8 * co + m + 1)
                        if 8 * co + m >= 2:
                            sc.wait_ge(sSG, 8 * co + m - 1)
                        seal_sig2(sc.activation(gbuf[:, (m % 2) * TCH:(m % 2 + 1) * TCH], py[m % 2][:, :],
                                                AF.Silu, bias=fb_sb[:, m:m + 1]), sSIL)


        @block.gpsimd
        def _(g_):
            for c in range(NC):
                for tt in range(NT):
                    if c == 0 and tt == 0:
                        g_.wait_ge(sWc, NCONV)
                    if 1 <= c <= n_warm:
                        if tt == 0:
                            g_.wait_ge(sLN, c)
                    elif c >= 1:
                        g_.wait_ge(sRES, 4 * (c - 1 - n_warm) + tt + 1)
                    g_.dma_start(out=xstage[:, tt * D:(tt + 1) * D],
                                 in_=x_in[c * TCH + tt * 128: c * TCH + (tt + 1) * 128, :]
                                 ).then_inc(sXT[tt], 16)

        @block.tensor
        def _(t):
            t.wait_ge(sWc, NCONV)
            for c in range(NC):
                t.wait_ge(sLN, c + 1)
                if c > n_warm:
                    t.wait_ge(sRES, 4 * (c - n_warm))
                for g8 in range(KD):
                    if g8 >= 2:
                        t.wait_ge(sTPe, TPB(c) + g8 - 1)
                    bank = pt[g8 % 2]
                    for tt in range(NT):
                        ins = t.transpose(bank[:, tt * 128:(tt + 1) * 128],
                                          tmb[:, tt * D + g8 * 128: tt * D + (g8 + 1) * 128],
                                          idn_sb[:, :])
                        if tt == NT - 1:
                            ins.then_inc(sTPm, 1)
                t.wait_ge(sTPe, TPB(c) + 8)
                for m in range(24):
                    if m >= 2:
                        t.wait_ge(sQe, 24 * c + m - 1)
                    bank = py[m % 2]
                    for k in range(KD):
                        ins = t.matmul(bank[:, :], wq_sb[:, k * DQ + m * 128: k * DQ + (m + 1) * 128],
                                       ffm[:, k * TCH:(k + 1) * TCH],
                                       start=(k == 0), stop=(k == KD - 1))
                        if k == KD - 1:
                            ins.then_inc(sQm, 1)
                if c >= n_warm:
                    t.wait_ge(sQF, c - n_warm + 1)
                    for g8 in range(KD):
                        if g8 >= 2:
                            t.wait_ge(sTPe, TPB(c) + 8 + g8 - 1)
                        bank = pt[g8 % 2]
                        for tt in range(NT):
                            ins = t.transpose(bank[:, tt * 128:(tt + 1) * 128],
                                              tmb[:, tt * D + g8 * 128: tt * D + (g8 + 1) * 128],
                                              idn_sb[:, :])
                            if tt == NT - 1:
                                ins.then_inc(sTPm, 1)

                def emit_vrep(rr):
                    h = rr // 4
                    qq = rr % 4
                    if qq == 0:
                        t.wait_ge(sQe, 24 * c + 16 + h // 2 + 1)
                    for g in range(NGR):
                        GG = 256 * c + 4 * rr + g
                        if GG >= 2:
                            t.wait_ge(sGV, GG - 1)
                        for t2 in range(2):
                            k = g * 2 + t2
                            m = qq * 8 + k
                            bank = pv[(g % 2) * 2 + t2]
                            eb = (h % 2) * 64
                            ins = t.matmul(bank[:, :], esel_sb[eb:eb + 64, m * 128:(m + 1) * 128],
                                           vtile_ap(h), start=True, stop=True)
                            if t2 == 1:
                                ins.then_inc(sVR, 1)

                def emit_reduce(rr):
                    h = rr // 4
                    qq = rr % 4
                    p = h // 2
                    rowbase = (h % 2) * 64
                    bank = py[p % 2]
                    if qq == 0:
                        gp = 8 * (c - n_warm) + p
                        if h % 2 == 0 and gp >= 2:
                            t.wait_ge(sYE, gp - 1)
                        if h == 0:
                            t.wait_ge(sQe, 24 * c + 23)
                        elif h == 2:
                            t.wait_ge(sQe, 24 * c + 24)
                        wb = (h % 2) * 64
                        t.matmul(bank[rowbase:rowbase + 64, :],
                                 wst_sb[wb:wb + 64, h * 64:(h + 1) * 64], qtile_ap(h),
                                 start=True, stop=False, skip_group_check=True)
                    for g in range(NGR):
                        GG = 256 * c + 4 * rr + g
                        t.wait_ge(sZ, GG + 1)
                        for t2 in range(2):
                            k = g * 2 + t2
                            m = qq * 8 + k
                            last = (qq == 3 and k == 7)
                            ins = t.matmul(bank[rowbase:rowbase + 64, :],
                                           dsel_sb[:, 64 - 2 * m:128 - 2 * m],
                                           uz[:, k * TCH:(k + 1) * TCH],
                                           start=False, stop=last, skip_group_check=True)
                            if t2 == 1:
                                ins.then_inc(sRD, 1)

                emit_vrep(0)
                for rr in range(1, NRR):
                    emit_vrep(rr)
                    if c >= n_warm:
                        emit_reduce(rr - 1)
                if c < n_warm:
                    continue
                co = c - n_warm
                emit_reduce(NRR - 1)
                t.wait_ge(sTPe, TPB(c) + 16)
                for pair in range(4):
                    for m in (2 * pair, 2 * pair + 1):
                        if pair == 0:
                            t.wait_ge(sYE, 8 * co + 7 + m)
                        if m >= 2:
                            t.wait_ge(sSG, 8 * co + m - 1)
                        bank = py[m % 2]
                        for k in range(KD):
                            ins = t.matmul(bank[:, :], wf_sb[:, k * D + m * 128: k * D + (m + 1) * 128],
                                           ffm[:, k * TCH:(k + 1) * TCH],
                                           start=(k == 0), stop=(k == KD - 1))
                            if k == KD - 1:
                                ins.then_inc(sFM, 1)
                    if pair == 0:
                        t.wait_ge(sYE, 8 * (co + 1))
                    for m in (2 * pair, 2 * pair + 1):
                        t.wait_ge(sSIL, 8 * co + m + 1)
                        bank = py[m % 2]
                        for k in range(KD):
                            ins = t.matmul(bank[:, :], wo_sb[:, k * D + m * 128: k * D + (m + 1) * 128],
                                           ybuf[:, k * TCH:(k + 1) * TCH],
                                           start=(k == 0), stop=(k == KD - 1))
                            if k == KD - 1:
                                ins.then_inc(sOM, 1)
                t.wait_ge(sSG, 8 * (co + 1))
                for tt in range(NT):
                    if tt >= 1:
                        t.wait_ge(sRES, 4 * co + tt)
                    for o8 in range(KD):
                        bank = pt[o8 // 4]
                        ins = t.transpose(bank[:, (o8 % 4) * 128:(o8 % 4 + 1) * 128],
                                          sgbuf[:, o8 * TCH + tt * 128: o8 * TCH + (tt + 1) * 128],
                                          idn_sb[:, :])
                        if o8 == KD - 1:
                            ins.then_inc(sSGT, 1)

    return nc


def host_weights(inputs):
    W = {}
    W["wq"] = np.ascontiguousarray(np.asarray(inputs["L0_attn_Wqkv_w"], np.float32).T)
    W["wqb"] = np.ascontiguousarray(np.asarray(inputs["L0_attn_Wqkv_b"], np.float32).reshape(24, 128).T)
    W["wo"] = np.ascontiguousarray(np.asarray(inputs["L0_attn_out_w"], np.float32).T)
    W["ob"] = np.ascontiguousarray(np.asarray(inputs["L0_attn_out_b"], np.float32).reshape(8, 128).T)
    W["wf"] = np.ascontiguousarray(np.asarray(inputs["L0_ffn_w_int8"], np.int32).T)
    W["fb"] = np.ascontiguousarray(np.asarray(inputs["L0_ffn_bias"], np.float32).reshape(8, 128).T)
    W["ws"] = np.ascontiguousarray(np.broadcast_to(
        np.asarray(inputs["L0_ffn_weight_scale"], np.float32).reshape(1, 1), (128, 1)))
    lam = np.asarray(inputs["L0_attn_Lambda_raw"], np.float32)
    gam = np.asarray(inputs["L0_attn_Gamma"], np.float32)
    W["lam"] = np.ascontiguousarray(lam.reshape(H, 32, 2, 64).transpose(2, 3, 0, 1).reshape(128, 512))
    W["gam"] = np.ascontiguousarray(gam.reshape(H, 32, 2, 64).transpose(2, 3, 0, 1).reshape(128, 512))
    wst = np.asarray(inputs["L0_attn_W_static"], np.float32)
    wstj = wst.transpose(2, 0, 1).reshape(64, H * 64)
    W["wst"] = np.ascontiguousarray(np.concatenate([wstj, wstj], 0))
    esel = np.zeros((64, 32 * 128), np.float32)
    for m in range(32):
        esel[2 * m, m * 128:m * 128 + 64] = 1.0
        esel[2 * m + 1, m * 128 + 64:m * 128 + 128] = 1.0
    W["esel"] = np.concatenate([esel, esel], 0)
    dsel = np.zeros((128, 128), np.float32)
    for p2 in range(128):
        dsel[p2, 64 + p2 // 64] = 1.0
    W["dsel"] = dsel
    W["idn"] = np.eye(128, dtype=np.float32)
    return W


N_WARM = 2
N_CHUNKS = 10
_CACHE = {}


def _get_nc():
    if "nc" not in _CACHE:
        _CACHE["nc"] = build_program(N_CHUNKS, N_WARM)
    return _CACHE["nc"]


def kernel(L0_attn_x, L0_attn_Wqkv_w, L0_attn_Wqkv_b, L0_attn_out_w, L0_attn_out_b,
           L0_attn_W_static, L0_attn_Lambda_raw, L0_attn_Gamma, L0_ffn_w_int8,
           L0_ffn_weight_scale, L0_ffn_bias):
    from concourse.bass_utils import run_bass_kernel_spmd
    inputs = dict(L0_attn_Wqkv_w=L0_attn_Wqkv_w, L0_attn_Wqkv_b=L0_attn_Wqkv_b,
                  L0_attn_out_w=L0_attn_out_w, L0_attn_out_b=L0_attn_out_b,
                  L0_attn_W_static=L0_attn_W_static, L0_attn_Lambda_raw=L0_attn_Lambda_raw,
                  L0_attn_Gamma=L0_attn_Gamma, L0_ffn_w_int8=L0_ffn_w_int8,
                  L0_ffn_weight_scale=L0_ffn_weight_scale, L0_ffn_bias=L0_ffn_bias)
    W = host_weights(inputs)
    x = np.asarray(L0_attn_x, np.float32)
    B, L, _ = x.shape
    TQ = L // 4
    WARM = N_WARM * TCH
    in_maps = []
    for c in range(8):
        b, q = c // 4, c % 4
        ks = np.full((128, N_CHUNKS), 0.125, np.float32)
        if q == 0:
            warm = x[b, 0:WARM]
            ks[:, :N_WARM] = 0.0
        else:
            warm = x[b, q * TQ - WARM: q * TQ]
        xc = np.ascontiguousarray(np.concatenate([warm, x[b, q * TQ:(q + 1) * TQ]], 0))
        in_maps.append(dict(x=xc, kscale=ks, **W))
    nc = _get_nc()
    res = run_bass_kernel_spmd(nc, in_maps, core_ids=list(range(8)))
    parts = [res.results[c]["out"] for c in range(8)]
    out = np.empty((B, L, D), np.float32)
    for c in range(8):
        b, q = c // 4, c % 4
        out[b, q * TQ:(q + 1) * TQ] = np.asarray(parts[c], np.float32)
    return out


# version 12
# speedup vs baseline: 1.0959x; 1.0348x over previous
import contextlib
import math
import numpy as np
import concourse.bass as bass
import concourse.mybir as mybir

F32 = mybir.dt.float32
BF16 = mybir.dt.bfloat16
I32 = mybir.dt.int32
ALU = mybir.AluOpType
AF = mybir.ActivationFunctionType
AX = mybir.AxisListType

H, DH, D, DQ = 16, 64, 1024, 3072
TCH = 512
NT = TCH // 128
KD = D // 128
NTL = 8
NRR = 64
NGR = 4
EPS_LN = 1e-5
EPS_RMS = 1e-6
EPS_ACT = 1e-5


def build_program(n_chunks, n_warm, debug_outputs=()):
    nc = bass.Bass("TRN2", target_bir_lowering=False, debug=False, detect_race_conditions=False)
    NC = n_chunks
    T_ALL = NC * TCH
    T_OUT = (NC - n_warm) * TCH

    def TPB(c):
        return 8 * c + 8 * max(c - n_warm, 0)

    x_in = nc.dram_tensor("x", [T_ALL, D], F32, kind="ExternalInput").ap()
    kscale_in = nc.dram_tensor("kscale", [128, NC], F32, kind="ExternalInput").ap()
    wq_in = nc.dram_tensor("wq", [D, DQ], F32, kind="ExternalInput").ap()
    wqb_in = nc.dram_tensor("wqb", [128, 24], F32, kind="ExternalInput").ap()
    wo_in = nc.dram_tensor("wo", [D, D], F32, kind="ExternalInput").ap()
    ob_in = nc.dram_tensor("ob", [128, 8], F32, kind="ExternalInput").ap()
    wf_in = nc.dram_tensor("wf", [D, D], I32, kind="ExternalInput").ap()
    fb_in = nc.dram_tensor("fb", [128, 8], F32, kind="ExternalInput").ap()
    ws_in = nc.dram_tensor("ws", [128, 1], F32, kind="ExternalInput").ap()
    lam_in = nc.dram_tensor("lam", [128, 512], F32, kind="ExternalInput").ap()
    gam_in = nc.dram_tensor("gam", [128, 512], F32, kind="ExternalInput").ap()
    wst_in = nc.dram_tensor("wst", [128, H * 64], F32, kind="ExternalInput").ap()
    esel_in = nc.dram_tensor("esel", [128, 32 * 128], F32, kind="ExternalInput").ap()
    dsel_in = nc.dram_tensor("dsel", [128, 128], F32, kind="ExternalInput").ap()
    idn_in = nc.dram_tensor("idn", [128, 128], F32, kind="ExternalInput").ap()
    out_dram = nc.dram_tensor("out", [T_OUT, D], F32, kind="ExternalOutput").ap()
    dbg = {}
    for name, shape in (("xn", [T_ALL, D]), ("qkv", [DQ, T_ALL]), ("y", [D, T_ALL]),
                        ("xq", [D, T_ALL]), ("s", [D, T_ALL])):
        if name in debug_outputs:
            dbg[name] = nc.dram_tensor("dbg_" + name, shape, BF16, kind="ExternalOutput").ap()

    with contextlib.ExitStack() as _st:
        def sb(name, shape, dt):
            return _st.enter_context(nc.sbuf_tensor(name, shape, dt))
        def psa(name):
            return _st.enter_context(nc.psum_tensor(name, [128, 512], F32))
        def sem(name):
            return _st.enter_context(nc.semaphore(name))

        wq_sb = sb("wq_sb", [128, KD * DQ], BF16)
        wo_sb = sb("wo_sb", [128, KD * D], BF16)
        wf_sb = sb("wf_sb", [128, KD * D], BF16)
        wqb_sb = sb("wqb_sb", [128, 24], F32)
        ob_sb = sb("ob_sb", [128, 8], F32)
        fb_sb = sb("fb_sb", [128, 8], F32)
        ws_sb = sb("ws_sb", [128, 1], F32)
        rr_sb = sb("rr_sb", [128, 512], F32)
        gam_sb = sb("gam_sb", [128, 512], F32)
        wst_sb = sb("wst_sb", [128, H * 64], BF16)
        esel_sb = sb("esel_sb", [128, 32 * 128], BF16)
        dsel_sb = sb("dsel_sb", [128, 128], BF16)
        idn_sb = sb("idn_sb", [128, 128], BF16)
        kscale_sb = sb("kscale_sb", [128, NC], F32)
        eps_sb = sb("eps_sb", [128, 2], F32)
        scr_sb = sb("scr_sb", [1, 4], F32)
        xstage = sb("xstage", [128, NT * D], F32)
        stats = sb("stats", [128, NT * 12], F32)
        cols = sb("cols", [128, 32], F32)
        xns = sb("xns", [128, 2 * D], F32)
        tmb = sb("tmb", [128, NT * D], BF16)
        ffm = sb("ffm", [128, KD * TCH], BF16)
        qkv_fm = sb("qkv_fm", [128, 3 * KD * TCH], BF16)
        kdup = sb("kdup", [128, 2 * TCH], BF16)
        qdup = sb("qdup", [128, 2 * TCH], BF16)
        gv = sb("gv", [128, NTL * TCH], BF16)
        uz = sb("uz", [128, NTL * TCH], BF16)
        scol = sb("scol", [128, 512], F32)
        stg32 = sb("stg32", [128, 512], I32)
        ybuf = sb("ybuf", [128, KD * TCH], BF16)
        gbuf = tmb
        sgbuf = sb("sgbuf", [128, KD * TCH], BF16)
        outstage = sb("outstage", [128, 2 * D], F32)
        pt0 = _st.enter_context(nc.psum_tensor("pt0", [128, 1024], BF16))
        pt1 = _st.enter_context(nc.psum_tensor("pt1", [128, 1024], BF16))
        py0, py1 = psa("py0"), psa("py1")
        pv0, pv1, pv2, pv3 = psa("pv0"), psa("pv1"), psa("pv2"), psa("pv3")
        sX, sW, sWc = sem("sX"), sem("sW"), sem("sWc")
        sW2, sW3 = sem("sW2"), sem("sW3")
        sWI, sLAM = sem("sWI"), sem("sLAM")
        sWS = [sem("sWS%d" % k) for k in range(8)]
        sXT = [sem("sXT%d" % k) for k in range(NT)]
        sQB = sem("sQB")
        sDUPb, sOUTb = sem("sDUPb"), sem("sOUTb")
        sST, sSQ, sLN, sQF = sem("sST"), sem("sSQ"), sem("sLN"), sem("sQF")
        sTPm, sTPe, sQm, sQe = sem("sTPm"), sem("sTPe"), sem("sQm"), sem("sQe")
        sDUP = sem("sDUP")
        sVR, sGV, sU, sSC = sem("sVR"), sem("sGV"), sem("sU"), sem("sSC")
        sZ, sRD, sYE = sem("sZ"), sem("sRD"), sem("sYE")
        sFM, sSIL, sOM, sSG = sem("sFM"), sem("sSIL"), sem("sOM"), sem("sSG")
        sSGT, sRES, sOUT, sDBG = sem("sSGT"), sem("sRES"), sem("sOUT"), sem("sDBG")
        sGD = sem("sGD")
        sGD2 = sem("sGD2")
        block = _st.enter_context(nc.Block())

        pv = [pv0, pv1, pv2, pv3]
        py = [py0, py1]
        pt = [pt0, pt1]

        floads = []
        for k in range(KD):
            for j in range(6):
                floads.append((wq_sb, k * DQ + j * 512,
                               wq_in[k * 128:(k + 1) * 128, j * 512:(j + 1) * 512], 512))
        for k in range(KD):
            for j in range(2):
                floads.append((wo_sb, k * D + j * 512,
                               wo_in[k * 128:(k + 1) * 128, j * 512:(j + 1) * 512], 512))
        for j in range(2):
            floads.append((wst_sb, j * 512, wst_in[:, j * 512:(j + 1) * 512], 512))
        for j in range(8):
            floads.append((esel_sb, j * 512, esel_in[:, j * 512:(j + 1) * 512], 512))
        floads.append((dsel_sb, 0, dsel_in[:, :], 128))
        floads.append((idn_sb, 0, idn_in[:, :], 128))
        NINT = 2 * KD
        NCONV = NINT + len(floads)

        def vtile_ap(h):
            m = 16 + h // 2
            return qkv_fm[(h % 2) * 64:(h % 2) * 64 + 64, m * TCH:(m + 1) * TCH]

        def qtile_ap(h):
            m = h // 2
            return qkv_fm[(h % 2) * 64:(h % 2) * 64 + 64, m * TCH:(m + 1) * TCH]

        def ktile_ap(h):
            m = 8 + h // 2
            return qkv_fm[(h % 2) * 64:(h % 2) * 64 + 64, m * TCH:(m + 1) * TCH]

        @block.sync
        def _(s):
            for j in range(NINT):
                if j >= 1:
                    s.wait_ge(sWc, j)
                k, jj = j // 2, j % 2
                s.dma_start(out=stg32[:, :],
                            in_=wf_in[k * 128:(k + 1) * 128, jj * 512:(jj + 1) * 512]
                            ).then_inc(sWI, 16)
            for i, (dst, off, src, w) in enumerate(floads):
                if i >= 8:
                    s.wait_ge(sWc, NINT + i - 7)
                slot = (i % 8) * 512
                s.dma_start(out=xstage[:, slot:slot + w],
                            in_=src).then_inc(sWS[i % 8], 16)
            s.dma_start(out=wqb_sb[:, :], in_=wqb_in).then_inc(sW3, 16)
            s.dma_start(out=ob_sb[:, :], in_=ob_in).then_inc(sW3, 16)
            s.dma_start(out=fb_sb[:, :], in_=fb_in).then_inc(sW3, 16)
            s.dma_start(out=ws_sb[:, :], in_=ws_in).then_inc(sW3, 16)
            s.dma_start(out=scol[:, :], in_=lam_in).then_inc(sW3, 16)
            s.dma_start(out=gam_sb[:, :], in_=gam_in).then_inc(sW3, 16)
            s.dma_start(out=kscale_sb[:, :], in_=kscale_in).then_inc(sW3, 16)
            for c in range(NC):
                if "xn" in dbg:
                    s.wait_ge(sLN, c + 1)
                    s.dma_start(out=dbg["xn"][c * TCH:(c + 1) * TCH, :]
                                .rearrange("(n p) d -> p n d", p=128), in_=tmb[:, :]).then_inc(sDBG, 16)
                for h in range(H):
                    gh = 16 * c + h
                    s.wait_ge(sQe, 24 * c + 16 + h // 2 + 1)
                    if gh >= 2:
                        s.wait_ge(sU, 16 * (gh - 1))
                        s.wait_ge(sZ, 16 * (gh - 1))
                    slot = (h % 2) * TCH
                    dsem = sDUP if h % 2 == 0 else sDUPb
                    s.dma_start(out=kdup[0:64, slot:slot + TCH], in_=ktile_ap(h)).then_inc(dsem, 16)
                    s.dma_start(out=kdup[64:128, slot:slot + TCH], in_=ktile_ap(h)).then_inc(dsem, 16)
                    s.dma_start(out=qdup[0:64, slot:slot + TCH], in_=qtile_ap(h)).then_inc(dsem, 16)
                    s.dma_start(out=qdup[64:128, slot:slot + TCH], in_=qtile_ap(h)).then_inc(dsem, 16)
                if "qkv" in dbg:
                    s.wait_ge(sQe, 24 * (c + 1))
                    s.dma_start(out=dbg["qkv"][:, c * TCH:(c + 1) * TCH]
                                .rearrange("(m p) t -> p m t", p=128), in_=qkv_fm[:, :]).then_inc(sDBG, 16)
                if "y" in dbg and c >= n_warm:
                    s.wait_ge(sYE, 8 * (c - n_warm + 1))
                    s.dma_start(out=dbg["y"][:, c * TCH:(c + 1) * TCH]
                                .rearrange("(m p) t -> p m t", p=128), in_=ybuf[:, :]).then_inc(sDBG, 16)
                if "xq" in dbg and c >= n_warm:
                    s.wait_ge(sTPe, TPB(c) + 16)
                    s.dma_start(out=dbg["xq"][:, c * TCH:(c + 1) * TCH]
                                .rearrange("(m p) t -> p m t", p=128), in_=ffm[:, :]).then_inc(sDBG, 16)
                if "s" in dbg and c >= n_warm:
                    s.wait_ge(sSG, 8 * (c - n_warm + 1))
                    s.dma_start(out=dbg["s"][:, c * TCH:(c + 1) * TCH]
                                .rearrange("(m p) t -> p m t", p=128), in_=sgbuf[:, :]).then_inc(sDBG, 16)
                if c >= n_warm:
                    co = c - n_warm
                    for tt in range(NT):
                        s.wait_ge(sRES, 4 * co + tt + 1)
                        osem = sOUT if tt % 2 == 0 else sOUTb
                        s.dma_start(out=out_dram[co * TCH + tt * 128: co * TCH + (tt + 1) * 128, :],
                                    in_=outstage[:, (tt % 2) * D:(tt % 2 + 1) * D]).then_inc(osem, 16)
            s.wait_ge(sOUT, 16 * (NC - n_warm) * 2)
            s.wait_ge(sOUTb, 16 * (NC - n_warm) * 2)

        @block.vector
        def _(v):
            v.memset(cols[:, :], 0.0)
            v.memset(eps_sb[:, 0:1], EPS_LN)
            v.memset(eps_sb[:, 1:2], EPS_RMS)
            for j in range(NINT):
                v.wait_ge(sWI, 16 * (j + 1))
                k, jj = j // 2, j % 2
                v.tensor_copy(wf_sb[:, k * D + jj * 512:k * D + (jj + 1) * 512],
                              stg32[:, :]).then_inc(sWc, 1)
            nseal = 0
            for i, (dst, off, src, w) in enumerate(floads):
                v.wait_ge(sWS[i % 8], 16 * (i // 8 + 1))
                slot = (i % 8) * 512
                ins = v.tensor_copy(dst[:, off:off + w], xstage[:, slot:slot + w])
                if w == 128:
                    nseal += 1
                    ins.then_inc(sGD, 1)
                    v.wait_ge(sGD, nseal)
                    v.engine_nop().then_inc(sWc, 1)
                else:
                    ins.then_inc(sWc, 1)
            v.wait_ge(sLAM, 1)
            v.memset(scol[:, :], 0.0)
            v.wait_ge(sW3, 16 * 7)

            GDBASE = [0]
            GDTOT = [nseal]
            for c in range(NC):
                GDBASE[0] = GDTOT[0]
                pass
                if dbg and c >= 1:
                    v.wait_ge(sDBG, 16 * len(dbg) * c)
                gd = [0]
                def seal(ins):
                    gd[0] += 1
                    ins.then_inc(sGD, 1)
                    v.wait_ge(sGD, GDBASE[0] + gd[0])
                def seal_sig(ins, sigsem, inc=1):
                    seal(ins)
                    v.engine_nop().then_inc(sigsem, inc)
                v.wait_ge(sOUT, 16 * 2 * max(c - n_warm, 0))
                v.wait_ge(sOUTb, 16 * 2 * max(c - n_warm, 0))
                for tt in range(NT):
                    xs = xstage[:, tt * D:(tt + 1) * D]
                    v.wait_ge(sXT[tt], 16 * (c + 1))
                    seal(v.reduce_sum(cols[:, 0 + tt:1 + tt], xs, AX.X))
                seal(v.tensor_scalar(cols[:, 0:NT], cols[:, 0:NT], 1.0 / D, None, ALU.mult))
                for tt in range(NT):
                    xs = xstage[:, tt * D:(tt + 1) * D]
                    xslot = xns[:, (tt % 2) * D:(tt % 2 + 1) * D]
                    oslot = outstage[:, (tt % 2) * D:(tt % 2 + 1) * D]
                    v.tensor_scalar(xslot, xs, cols[:, 0 + tt:1 + tt], None, ALU.subtract)
                    seal(v.tensor_tensor(oslot, xslot, xslot, ALU.mult))
                    seal(v.reduce_sum(cols[:, 4 + tt:5 + tt], oslot, AX.X))
                seal_sig(v.tensor_scalar(cols[:, 4:4 + NT], cols[:, 4:4 + NT], 1.0 / D, None, ALU.mult), sST)
                v.wait_ge(sSQ, 2 * c + 1)
                seal(v.reciprocal(cols[:, 12:12 + NT], cols[:, 8:8 + NT]))
                for tt in range(NT):
                    xs = xstage[:, tt * D:(tt + 1) * D]
                    xslot = xns[:, (tt % 2) * D:(tt % 2 + 1) * D]
                    v.tensor_scalar(xslot, xs, cols[:, 0 + tt:1 + tt],
                                    cols[:, 12 + tt:13 + tt], ALU.subtract, ALU.mult)
                    ins = v.tensor_copy(tmb[:, tt * D:(tt + 1) * D], xslot)
                    if tt == NT - 1:
                        seal_sig(ins, sLN)
                    v.reduce_max(cols[:, 16 + tt:17 + tt], xslot, AX.X,
                                 apply_absolute_value=True)
                seal_sig(v.tensor_scalar(stats[:, 0:8], wqb_sb[:, 8:16],
                                         kscale_sb[:, c:c + 1], None, ALU.mult), sQB)
                seal(v.tensor_tensor(cols[:, 20:20 + NT], cols[:, 12:12 + NT],
                                     cols[:, 12:12 + NT], ALU.mult))
                seal_sig(v.tensor_tensor(cols[:, 20:20 + NT], cols[:, 20:20 + NT],
                                         cols[:, 4:4 + NT], ALU.mult), sST)
                v.wait_ge(sSQ, 2 * c + 2)
                seal(v.reciprocal(cols[:, 28:28 + NT], cols[:, 8:8 + NT]))
                seal(v.tensor_tensor(cols[:, 16:16 + NT], cols[:, 16:16 + NT],
                                     cols[:, 28:28 + NT], ALU.mult))
                seal(v.tensor_scalar(cols[:, 16:16 + NT], cols[:, 16:16 + NT], EPS_ACT, None, ALU.max))
                seal(v.reciprocal(cols[:, 24:24 + NT], cols[:, 16:16 + NT]))
                seal(v.tensor_scalar(cols[:, 24:24 + NT], cols[:, 24:24 + NT], 127.0, None, ALU.mult))
                seal(v.tensor_tensor(cols[:, 28:28 + NT], cols[:, 28:28 + NT],
                                     cols[:, 12:12 + NT], ALU.mult))
                seal(v.tensor_tensor(cols[:, 28:28 + NT], cols[:, 28:28 + NT],
                                     cols[:, 24:24 + NT], ALU.mult))
                seal(v.tensor_scalar(cols[:, 16:16 + NT], cols[:, 16:16 + NT], ws_sb[:, 0:1], None, ALU.mult))
                seal(v.tensor_scalar(cols[:, 16:16 + NT], cols[:, 16:16 + NT], 1.0 / 127.0, None, ALU.mult))
                if "xn" in dbg:
                    v.wait_ge(sDBG, 16 * (len(dbg) * c + 1))
                if c >= n_warm:
                    v.wait_ge(sTPm, TPB(c) + 8)
                    for tt in range(NT):
                        xs = xstage[:, tt * D:(tt + 1) * D]
                        xslot = xns[:, (tt % 2) * D:(tt % 2 + 1) * D]
                        oslot = outstage[:, (tt % 2) * D:(tt % 2 + 1) * D]
                        v.tensor_scalar(xslot, xs, cols[:, 0 + tt:1 + tt],
                                        cols[:, 28 + tt:29 + tt], ALU.subtract, ALU.mult)
                        v.tensor_scalar(xslot, xslot, 128.5, None, ALU.add)
                        for kb in range(7, -1, -1):
                            v.tensor_scalar(oslot, xslot, float(1 << kb), float(1 << kb),
                                            ALU.is_ge, ALU.mult)
                            v.tensor_tensor(xslot, xslot, oslot, ALU.subtract)
                        v.tensor_scalar(oslot, xs, cols[:, 0 + tt:1 + tt],
                                        cols[:, 28 + tt:29 + tt], ALU.subtract, ALU.mult)
                        v.tensor_tensor(xslot, oslot, xslot, ALU.subtract)
                        v.tensor_scalar(xslot, xslot, 0.5, -128.0, ALU.add, ALU.max)
                        v.tensor_scalar(xslot, xslot, 127.0, cols[:, 16 + tt:17 + tt],
                                        ALU.min, ALU.mult)
                        ins = v.tensor_copy(tmb[:, tt * D:(tt + 1) * D], xslot)
                        if tt == NT - 1:
                            seal_sig(ins, sQF)
                for rr in range(NRR):
                    h = rr // 4
                    qq = rr % 4
                    qsl = qdup[:, (h % 2) * TCH:(h % 2 + 1) * TCH]
                    ksl = kdup[:, (h % 2) * TCH:(h % 2 + 1) * TCH]
                    for g in range(NGR):
                        GG = 256 * c + 4 * rr + g
                        v.wait_ge(sGV, GG + 1)
                        if g == 0 and rr % 4 == 0:
                            v.wait_ge(sDUP if h % 2 == 0 else sDUPb, 64 * (8 * c + h // 2 + 1))
                        for t2 in range(2):
                            k = g * 2 + t2
                            sl = slice(k * TCH, (k + 1) * TCH)
                            ins = v.tensor_tensor(uz[:, sl], gv[:, sl], ksl, ALU.mult)
                            if t2 == 1:
                                ins.then_inc(sU, 1)
                        for t2 in range(2):
                            k = g * 2 + t2
                            m = qq * 8 + k
                            ct = h * 32 + m
                            sl = slice(k * TCH, (k + 1) * TCH)
                            ins = v.tensor_tensor_scan(gv[:, sl],
                                                       rr_sb[:, ct:ct + 1].broadcast_to([128, TCH]),
                                                       uz[:, sl], scol[:, ct:ct + 1],
                                                       ALU.mult, ALU.add)
                            if c < n_warm and t2 == 1:
                                ins.then_inc(sZ, 1)
                        if c >= n_warm:
                            GGr = 256 * (c - n_warm) + 4 * rr + g
                            if GGr >= 4:
                                v.wait_ge(sRD, GGr - 3)
                            for t2 in range(2):
                                k = g * 2 + t2
                                sl = slice(k * TCH, (k + 1) * TCH)
                                ins = v.tensor_tensor(uz[:, sl], gv[:, sl], qsl, ALU.mult)
                                if t2 == 1:
                                    ins.then_inc(sZ, 1)
                    if c >= n_warm and rr >= 9 and (rr - 9) % 8 == 0:
                        p = (rr - 9) // 8
                        v.wait_ge(sRD, 256 * (c - n_warm) + 32 * (p + 1))
                        seal_sig(v.tensor_copy(ybuf[:, p * TCH:(p + 1) * TCH], py[p % 2][:, :]), sYE)
                if c >= n_warm:
                    co = c - n_warm
                    p = 7
                    v.wait_ge(sRD, 256 * co + 32 * (p + 1))
                    seal_sig(v.tensor_copy(ybuf[:, p * TCH:(p + 1) * TCH], py[p % 2][:, :]), sYE)
                    for m in range(8):
                        v.wait_ge(sOM, 8 * co + m + 1)
                        v.wait_ge(sSIL, 8 * co + m + 1)
                        seal_sig(v.scalar_tensor_tensor(sgbuf[:, m * TCH:(m + 1) * TCH], py[m % 2][:, :],
                                                        ob_sb[:, m:m + 1], gbuf[:, (m % 2) * TCH:(m % 2 + 1) * TCH],
                                                        ALU.add, ALU.mult), sSG)
                    for tt in range(NT):
                        v.wait_ge(sSGT, 4 * co + tt + 1)
                        if 2 * co + tt // 2 >= 1:
                            v.wait_ge(sOUT if tt % 2 == 0 else sOUTb, 16 * (2 * co + tt // 2))
                        oslot = outstage[:, (tt % 2) * D:(tt % 2 + 1) * D]
                        v.tensor_tensor(oslot[:, 0:512], pt0[:, 0:512],
                                        xstage[:, tt * D:tt * D + 512], ALU.add)
                        seal_sig(v.tensor_tensor(oslot[:, 512:1024], pt1[:, 0:512],
                                                 xstage[:, tt * D + 512:(tt + 1) * D], ALU.add), sRES)
                GDTOT[0] = GDBASE[0] + gd[0]

        @block.scalar
        def _(sc):
            gd2 = [0]
            def seal_sig2(ins, sigsem):
                gd2[0] += 1
                ins.then_inc(sGD2, 1)
                sc.wait_ge(sGD2, gd2[0])
                sc.mul(scr_sb[0:1, 0:1], scr_sb[0:1, 0:1], 1.0).then_inc(sigsem, 1)
            sc.wait_ge(sW3, 16 * 7)
            sc.copy(scr_sb[0:1, 0:1], eps_sb[0:1, 0:1])
            sc.activation(rr_sb[:, :], scol[:, :], AF.Sigmoid, scale=-1.0).then_inc(sLAM, 1)
            for c in range(NC):
                sc.wait_ge(sST, 2 * c + 1)
                seal_sig2(sc.activation(cols[:, 8:8 + NT], cols[:, 4:4 + NT], AF.Sqrt,
                                        bias=eps_sb[:, 0:1]), sSQ)
                sc.wait_ge(sST, 2 * c + 2)
                seal_sig2(sc.activation(cols[:, 8:8 + NT], cols[:, 20:20 + NT], AF.Sqrt,
                                        bias=eps_sb[:, 1:2]), sSQ)
                for g8 in range(KD):
                    sc.wait_ge(sTPm, TPB(c) + g8 + 1)
                    ins = sc.activation(ffm[:, g8 * TCH:(g8 + 1) * TCH],
                                        pt[g8 % 2][:, 0:512], AF.Copy)
                    if g8 == KD - 1:
                        seal_sig2(ins, sTPe)
                    else:
                        ins.then_inc(sTPe, 1)
                if c >= 1:
                    sc.wait_ge(sDUP, 512 * c)
                    sc.wait_ge(sDUPb, 512 * c)
                sc.wait_ge(sQB, c + 1)
                for m in range(24):
                    sc.wait_ge(sQm, 24 * c + m + 1)
                    bank = py[m % 2]
                    if 8 <= m < 16:
                        ins = sc.activation(qkv_fm[:, m * TCH:(m + 1) * TCH], bank[:, :],
                                            AF.Identity, scale=kscale_sb[:, c:c + 1],
                                            bias=stats[:, m - 8:m - 7])
                    else:
                        ins = sc.activation(qkv_fm[:, m * TCH:(m + 1) * TCH], bank[:, :],
                                            AF.Identity, bias=wqb_sb[:, m:m + 1])
                    if m >= 16:
                        seal_sig2(ins, sQe)
                    else:
                        ins.then_inc(sQe, 1)
                if c >= n_warm:
                    for g8 in range(KD):
                        sc.wait_ge(sTPm, TPB(c) + 8 + g8 + 1)
                        ins = sc.activation(ffm[:, g8 * TCH:(g8 + 1) * TCH],
                                            pt[g8 % 2][:, 0:512], AF.Copy)
                        if g8 == KD - 1:
                            seal_sig2(ins, sTPe)
                        else:
                            ins.then_inc(sTPe, 1)
                for rr in range(NRR):
                    h = rr // 4
                    qq = rr % 4
                    for g in range(NGR):
                        GG = 256 * c + 4 * rr + g
                        R = c * NRR + rr
                        sc.wait_ge(sVR, GG + 1)
                        if GG >= 4:
                            sc.wait_ge(sZ, GG - 3)
                            rp = (R - 1) % NRR
                            ct0 = (rp // 4) * 32 + (rp % 4) * 8 + 2 * g
                            k0 = 2 * g
                            sc.activation(scol[:, ct0:ct0 + 2],
                                          gv[:, k0 * TCH + TCH - 1:(k0 + 2) * TCH:TCH],
                                          AF.Copy)
                        for t2 in range(2):
                            k = g * 2 + t2
                            m = qq * 8 + k
                            ct = h * 32 + m
                            bank = pv[(g % 2) * 2 + t2]
                            ins = sc.activation(gv[:, k * TCH:(k + 1) * TCH], bank[:, :],
                                                AF.Copy, scale=gam_sb[:, ct:ct + 1])
                            if t2 == 1:
                                ins.then_inc(sGV, 1)
                if c >= n_warm:
                    co = c - n_warm
                    for m in range(8):
                        sc.wait_ge(sFM, 8 * co + m + 1)
                        if 8 * co + m >= 2:
                            sc.wait_ge(sSG, 8 * co + m - 1)
                        seal_sig2(sc.activation(gbuf[:, (m % 2) * TCH:(m % 2 + 1) * TCH], py[m % 2][:, :],
                                                AF.Silu, bias=fb_sb[:, m:m + 1]), sSIL)


        @block.gpsimd
        def _(g_):
            for c in range(NC):
                for tt in range(NT):
                    if c == 0 and tt == 0:
                        g_.wait_ge(sWc, NCONV)
                    if 1 <= c <= n_warm:
                        if tt == 0:
                            g_.wait_ge(sLN, c)
                    elif c >= 1:
                        g_.wait_ge(sRES, 4 * (c - 1 - n_warm) + tt + 1)
                    g_.dma_start(out=xstage[:, tt * D:(tt + 1) * D],
                                 in_=x_in[c * TCH + tt * 128: c * TCH + (tt + 1) * 128, :]
                                 ).then_inc(sXT[tt], 16)

        @block.tensor
        def _(t):
            t.wait_ge(sWc, NCONV)
            for c in range(NC):
                t.wait_ge(sLN, c + 1)
                if c > n_warm:
                    t.wait_ge(sRES, 4 * (c - n_warm))
                for g8 in range(KD):
                    if g8 >= 2:
                        t.wait_ge(sTPe, TPB(c) + g8 - 1)
                    bank = pt[g8 % 2]
                    for tt in range(NT):
                        ins = t.transpose(bank[:, tt * 128:(tt + 1) * 128],
                                          tmb[:, tt * D + g8 * 128: tt * D + (g8 + 1) * 128],
                                          idn_sb[:, :])
                        if tt == NT - 1:
                            ins.then_inc(sTPm, 1)
                t.wait_ge(sTPe, TPB(c) + 8)
                for m in range(24):
                    if m >= 2:
                        t.wait_ge(sQe, 24 * c + m - 1)
                    bank = py[m % 2]
                    for k in range(KD):
                        ins = t.matmul(bank[:, :], wq_sb[:, k * DQ + m * 128: k * DQ + (m + 1) * 128],
                                       ffm[:, k * TCH:(k + 1) * TCH],
                                       start=(k == 0), stop=(k == KD - 1))
                        if k == KD - 1:
                            ins.then_inc(sQm, 1)
                if c >= n_warm:
                    t.wait_ge(sQF, c - n_warm + 1)
                    for g8 in range(KD):
                        if g8 >= 2:
                            t.wait_ge(sTPe, TPB(c) + 8 + g8 - 1)
                        bank = pt[g8 % 2]
                        for tt in range(NT):
                            ins = t.transpose(bank[:, tt * 128:(tt + 1) * 128],
                                              tmb[:, tt * D + g8 * 128: tt * D + (g8 + 1) * 128],
                                              idn_sb[:, :])
                            if tt == NT - 1:
                                ins.then_inc(sTPm, 1)

                def emit_vrep(rr):
                    h = rr // 4
                    qq = rr % 4
                    if qq == 0:
                        t.wait_ge(sQe, 24 * c + 16 + h // 2 + 1)
                    for g in range(NGR):
                        GG = 256 * c + 4 * rr + g
                        if GG >= 2:
                            t.wait_ge(sGV, GG - 1)
                        for t2 in range(2):
                            k = g * 2 + t2
                            m = qq * 8 + k
                            bank = pv[(g % 2) * 2 + t2]
                            eb = (h % 2) * 64
                            ins = t.matmul(bank[:, :], esel_sb[eb:eb + 64, m * 128:(m + 1) * 128],
                                           vtile_ap(h), start=True, stop=True)
                            if t2 == 1:
                                ins.then_inc(sVR, 1)

                def emit_reduce(rr):
                    h = rr // 4
                    qq = rr % 4
                    p = h // 2
                    rowbase = (h % 2) * 64
                    bank = py[p % 2]
                    if qq == 0:
                        gp = 8 * (c - n_warm) + p
                        if h % 2 == 0 and gp >= 2:
                            t.wait_ge(sYE, gp - 1)
                        if h == 0:
                            t.wait_ge(sQe, 24 * c + 23)
                        elif h == 2:
                            t.wait_ge(sQe, 24 * c + 24)
                        wb = (h % 2) * 64
                        t.matmul(bank[rowbase:rowbase + 64, :],
                                 wst_sb[wb:wb + 64, h * 64:(h + 1) * 64], qtile_ap(h),
                                 start=True, stop=False, skip_group_check=True)
                    for g in range(NGR):
                        GG = 256 * c + 4 * rr + g
                        t.wait_ge(sZ, GG + 1)
                        for t2 in range(2):
                            k = g * 2 + t2
                            m = qq * 8 + k
                            last = (qq == 3 and k == 7)
                            ins = t.matmul(bank[rowbase:rowbase + 64, :],
                                           dsel_sb[:, 64 - 2 * m:128 - 2 * m],
                                           uz[:, k * TCH:(k + 1) * TCH],
                                           start=False, stop=last, skip_group_check=True)
                            if t2 == 1:
                                ins.then_inc(sRD, 1)

                emit_vrep(0)
                for rr in range(1, NRR):
                    emit_vrep(rr)
                    if c >= n_warm:
                        emit_reduce(rr - 1)
                if c < n_warm:
                    continue
                co = c - n_warm
                emit_reduce(NRR - 1)
                t.wait_ge(sTPe, TPB(c) + 16)
                for pair in range(4):
                    for m in (2 * pair, 2 * pair + 1):
                        if pair == 0:
                            t.wait_ge(sYE, 8 * co + 7 + m)
                        if m >= 2:
                            t.wait_ge(sSG, 8 * co + m - 1)
                        bank = py[m % 2]
                        for k in range(KD):
                            ins = t.matmul(bank[:, :], wf_sb[:, k * D + m * 128: k * D + (m + 1) * 128],
                                           ffm[:, k * TCH:(k + 1) * TCH],
                                           start=(k == 0), stop=(k == KD - 1))
                            if k == KD - 1:
                                ins.then_inc(sFM, 1)
                    if pair == 0:
                        t.wait_ge(sYE, 8 * (co + 1))
                    for m in (2 * pair, 2 * pair + 1):
                        t.wait_ge(sSIL, 8 * co + m + 1)
                        bank = py[m % 2]
                        for k in range(KD):
                            ins = t.matmul(bank[:, :], wo_sb[:, k * D + m * 128: k * D + (m + 1) * 128],
                                           ybuf[:, k * TCH:(k + 1) * TCH],
                                           start=(k == 0), stop=(k == KD - 1))
                            if k == KD - 1:
                                ins.then_inc(sOM, 1)
                t.wait_ge(sSG, 8 * (co + 1))
                for tt in range(NT):
                    if tt >= 1:
                        t.wait_ge(sRES, 4 * co + tt)
                    for o8 in range(KD):
                        bank = pt[o8 // 4]
                        ins = t.transpose(bank[:, (o8 % 4) * 128:(o8 % 4 + 1) * 128],
                                          sgbuf[:, o8 * TCH + tt * 128: o8 * TCH + (tt + 1) * 128],
                                          idn_sb[:, :])
                        if o8 == KD - 1:
                            ins.then_inc(sSGT, 1)

    return nc


def host_weights(inputs):
    W = {}
    W["wq"] = np.ascontiguousarray(np.asarray(inputs["L0_attn_Wqkv_w"], np.float32).T)
    W["wqb"] = np.ascontiguousarray(np.asarray(inputs["L0_attn_Wqkv_b"], np.float32).reshape(24, 128).T)
    W["wo"] = np.ascontiguousarray(np.asarray(inputs["L0_attn_out_w"], np.float32).T)
    W["ob"] = np.ascontiguousarray(np.asarray(inputs["L0_attn_out_b"], np.float32).reshape(8, 128).T)
    W["wf"] = np.ascontiguousarray(np.asarray(inputs["L0_ffn_w_int8"], np.int32).T)
    W["fb"] = np.ascontiguousarray(np.asarray(inputs["L0_ffn_bias"], np.float32).reshape(8, 128).T)
    W["ws"] = np.ascontiguousarray(np.broadcast_to(
        np.asarray(inputs["L0_ffn_weight_scale"], np.float32).reshape(1, 1), (128, 1)))
    lam = np.asarray(inputs["L0_attn_Lambda_raw"], np.float32)
    gam = np.asarray(inputs["L0_attn_Gamma"], np.float32)
    W["lam"] = np.ascontiguousarray(lam.reshape(H, 32, 2, 64).transpose(2, 3, 0, 1).reshape(128, 512))
    W["gam"] = np.ascontiguousarray(gam.reshape(H, 32, 2, 64).transpose(2, 3, 0, 1).reshape(128, 512))
    wst = np.asarray(inputs["L0_attn_W_static"], np.float32)
    wstj = wst.transpose(2, 0, 1).reshape(64, H * 64)
    W["wst"] = np.ascontiguousarray(np.concatenate([wstj, wstj], 0))
    esel = np.zeros((64, 32 * 128), np.float32)
    for m in range(32):
        esel[2 * m, m * 128:m * 128 + 64] = 1.0
        esel[2 * m + 1, m * 128 + 64:m * 128 + 128] = 1.0
    W["esel"] = np.concatenate([esel, esel], 0)
    dsel = np.zeros((128, 128), np.float32)
    for p2 in range(128):
        dsel[p2, 64 + p2 // 64] = 1.0
    W["dsel"] = dsel
    W["idn"] = np.eye(128, dtype=np.float32)
    return W


N_WARM = 2
N_CHUNKS = 10
_CACHE = {}


def _get_nc():
    if "nc" not in _CACHE:
        _CACHE["nc"] = build_program(N_CHUNKS, N_WARM)
    return _CACHE["nc"]


def kernel(L0_attn_x, L0_attn_Wqkv_w, L0_attn_Wqkv_b, L0_attn_out_w, L0_attn_out_b,
           L0_attn_W_static, L0_attn_Lambda_raw, L0_attn_Gamma, L0_ffn_w_int8,
           L0_ffn_weight_scale, L0_ffn_bias):
    from concourse.bass_utils import run_bass_kernel_spmd
    inputs = dict(L0_attn_Wqkv_w=L0_attn_Wqkv_w, L0_attn_Wqkv_b=L0_attn_Wqkv_b,
                  L0_attn_out_w=L0_attn_out_w, L0_attn_out_b=L0_attn_out_b,
                  L0_attn_W_static=L0_attn_W_static, L0_attn_Lambda_raw=L0_attn_Lambda_raw,
                  L0_attn_Gamma=L0_attn_Gamma, L0_ffn_w_int8=L0_ffn_w_int8,
                  L0_ffn_weight_scale=L0_ffn_weight_scale, L0_ffn_bias=L0_ffn_bias)
    W = host_weights(inputs)
    x = np.asarray(L0_attn_x, np.float32)
    B, L, _ = x.shape
    TQ = L // 4
    WARM = N_WARM * TCH
    in_maps = []
    for c in range(8):
        b, q = c // 4, c % 4
        ks = np.full((128, N_CHUNKS), 0.125, np.float32)
        if q == 0:
            warm = x[b, 0:WARM]
            ks[:, :N_WARM] = 0.0
        else:
            warm = x[b, q * TQ - WARM: q * TQ]
        xc = np.ascontiguousarray(np.concatenate([warm, x[b, q * TQ:(q + 1) * TQ]], 0))
        in_maps.append(dict(x=xc, kscale=ks, **W))
    nc = _get_nc()
    res = run_bass_kernel_spmd(nc, in_maps, core_ids=list(range(8)))
    parts = [res.results[c]["out"] for c in range(8)]
    out = np.empty((B, L, D), np.float32)
    for c in range(8):
        b, q = c // 4, c % 4
        out[b, q * TQ:(q + 1) * TQ] = np.asarray(parts[c], np.float32)
    return out


# version 13 (final)
# speedup vs baseline: 1.1033x; 1.0068x over previous
import contextlib
import math
import numpy as np
import concourse.bass as bass
import concourse.mybir as mybir

F32 = mybir.dt.float32
BF16 = mybir.dt.bfloat16
I32 = mybir.dt.int32
ALU = mybir.AluOpType
AF = mybir.ActivationFunctionType
AX = mybir.AxisListType

H, DH, D, DQ = 16, 64, 1024, 3072
TCH = 512
NT = TCH // 128
KD = D // 128
NTL = 8
NRR = 64
NGR = 4
EPS_LN = 1e-5
EPS_RMS = 1e-6
EPS_ACT = 1e-5


def build_program(n_chunks, n_warm, debug_outputs=()):
    nc = bass.Bass("TRN2", target_bir_lowering=False, debug=False, detect_race_conditions=False)
    NC = n_chunks
    T_ALL = NC * TCH
    T_OUT = (NC - n_warm) * TCH

    def TPB(c):
        return 8 * c + 8 * max(c - n_warm, 0)

    x_in = nc.dram_tensor("x", [T_ALL, D], F32, kind="ExternalInput").ap()
    kscale_in = nc.dram_tensor("kscale", [128, NC], F32, kind="ExternalInput").ap()
    wq_in = nc.dram_tensor("wq", [D, DQ], F32, kind="ExternalInput").ap()
    wqb_in = nc.dram_tensor("wqb", [128, 24], F32, kind="ExternalInput").ap()
    wo_in = nc.dram_tensor("wo", [D, D], F32, kind="ExternalInput").ap()
    ob_in = nc.dram_tensor("ob", [128, 8], F32, kind="ExternalInput").ap()
    wf_in = nc.dram_tensor("wf", [D, D], I32, kind="ExternalInput").ap()
    fb_in = nc.dram_tensor("fb", [128, 8], F32, kind="ExternalInput").ap()
    ws_in = nc.dram_tensor("ws", [128, 1], F32, kind="ExternalInput").ap()
    lam_in = nc.dram_tensor("lam", [128, 512], F32, kind="ExternalInput").ap()
    gam_in = nc.dram_tensor("gam", [128, 512], F32, kind="ExternalInput").ap()
    wst_in = nc.dram_tensor("wst", [128, H * 64], F32, kind="ExternalInput").ap()
    esel_in = nc.dram_tensor("esel", [128, 32 * 128], F32, kind="ExternalInput").ap()
    dsel_in = nc.dram_tensor("dsel", [128, 128], F32, kind="ExternalInput").ap()
    idn_in = nc.dram_tensor("idn", [128, 128], F32, kind="ExternalInput").ap()
    out_dram = nc.dram_tensor("out", [T_OUT, D], F32, kind="ExternalOutput").ap()
    dbg = {}
    for name, shape in (("xn", [T_ALL, D]), ("qkv", [DQ, T_ALL]), ("y", [D, T_ALL]),
                        ("xq", [D, T_ALL]), ("s", [D, T_ALL])):
        if name in debug_outputs:
            dbg[name] = nc.dram_tensor("dbg_" + name, shape, BF16, kind="ExternalOutput").ap()

    with contextlib.ExitStack() as _st:
        def sb(name, shape, dt):
            return _st.enter_context(nc.sbuf_tensor(name, shape, dt))
        def psa(name):
            return _st.enter_context(nc.psum_tensor(name, [128, 512], F32))
        def sem(name):
            return _st.enter_context(nc.semaphore(name))

        wq_sb = sb("wq_sb", [128, KD * DQ], BF16)
        wo_sb = sb("wo_sb", [128, KD * D], BF16)
        wf_sb = sb("wf_sb", [128, KD * D], BF16)
        wqb_sb = sb("wqb_sb", [128, 24], F32)
        ob_sb = sb("ob_sb", [128, 8], F32)
        fb_sb = sb("fb_sb", [128, 8], F32)
        ws_sb = sb("ws_sb", [128, 1], F32)
        rr_sb = sb("rr_sb", [128, 512], F32)
        gam_sb = sb("gam_sb", [128, 512], F32)
        wst_sb = sb("wst_sb", [128, H * 64], BF16)
        esel_sb = sb("esel_sb", [128, 32 * 128], BF16)
        dsel_sb = sb("dsel_sb", [128, 128], BF16)
        idn_sb = sb("idn_sb", [128, 128], BF16)
        kscale_sb = sb("kscale_sb", [128, NC], F32)
        eps_sb = sb("eps_sb", [128, 2], F32)
        scr_sb = sb("scr_sb", [1, 4], F32)
        xstage = sb("xstage", [128, NT * D], F32)
        stats = sb("stats", [128, NT * 12], F32)
        cols = sb("cols", [128, 32], F32)
        xns = sb("xns", [128, 2 * D], F32)
        tmb = sb("tmb", [128, NT * D], BF16)
        ffm = sb("ffm", [128, KD * TCH], BF16)
        qkv_fm = sb("qkv_fm", [128, 3 * KD * TCH], BF16)
        kdup = sb("kdup", [128, 2 * TCH], BF16)
        qdup = sb("qdup", [128, 2 * TCH], BF16)
        gv = sb("gv", [128, NTL * TCH], BF16)
        uz = sb("uz", [128, NTL * TCH], BF16)
        scol = sb("scol", [128, 512], F32)
        stg32 = sb("stg32", [128, 512], I32)
        ybuf = sb("ybuf", [128, KD * TCH], BF16)
        gbuf = tmb
        sgbuf = sb("sgbuf", [128, KD * TCH], BF16)
        outstage = sb("outstage", [128, 2 * D], F32)
        pt0 = _st.enter_context(nc.psum_tensor("pt0", [128, 1024], BF16))
        pt1 = _st.enter_context(nc.psum_tensor("pt1", [128, 1024], BF16))
        py0, py1 = psa("py0"), psa("py1")
        pv0, pv1, pv2, pv3 = psa("pv0"), psa("pv1"), psa("pv2"), psa("pv3")
        sX, sW, sWc = sem("sX"), sem("sW"), sem("sWc")
        sW2, sW3 = sem("sW2"), sem("sW3")
        sWI, sLAM = sem("sWI"), sem("sLAM")
        sWS = [sem("sWS%d" % k) for k in range(8)]
        sXT = [sem("sXT%d" % k) for k in range(NT)]
        sQB = sem("sQB")
        sDUPb, sOUTb = sem("sDUPb"), sem("sOUTb")
        sST, sSQ, sLN, sQF = sem("sST"), sem("sSQ"), sem("sLN"), sem("sQF")
        sTPm, sTPe, sQm, sQe = sem("sTPm"), sem("sTPe"), sem("sQm"), sem("sQe")
        sDUP = sem("sDUP")
        sVR, sGV, sU, sSC = sem("sVR"), sem("sGV"), sem("sU"), sem("sSC")
        sZ, sRD, sYE = sem("sZ"), sem("sRD"), sem("sYE")
        sFM, sSIL, sOM, sSG = sem("sFM"), sem("sSIL"), sem("sOM"), sem("sSG")
        sSGT, sRES, sOUT, sDBG = sem("sSGT"), sem("sRES"), sem("sOUT"), sem("sDBG")
        sGD = sem("sGD")
        sGD2 = sem("sGD2")
        block = _st.enter_context(nc.Block())

        pv = [pv0, pv1, pv2, pv3]
        py = [py0, py1]
        pt = [pt0, pt1]

        floads = []
        for k in range(KD):
            for j in range(6):
                floads.append((wq_sb, k * DQ + j * 512,
                               wq_in[k * 128:(k + 1) * 128, j * 512:(j + 1) * 512], 512))
        for k in range(KD):
            for j in range(2):
                floads.append((wo_sb, k * D + j * 512,
                               wo_in[k * 128:(k + 1) * 128, j * 512:(j + 1) * 512], 512))
        for j in range(2):
            floads.append((wst_sb, j * 512, wst_in[:, j * 512:(j + 1) * 512], 512))
        for j in range(8):
            floads.append((esel_sb, j * 512, esel_in[:, j * 512:(j + 1) * 512], 512))
        floads.append((dsel_sb, 0, dsel_in[:, :], 128))
        floads.append((idn_sb, 0, idn_in[:, :], 128))
        NINT = 2 * KD
        NCONV = NINT + len(floads)

        def vtile_ap(h):
            m = 16 + h // 2
            return qkv_fm[(h % 2) * 64:(h % 2) * 64 + 64, m * TCH:(m + 1) * TCH]

        def qtile_ap(h):
            m = h // 2
            return qkv_fm[(h % 2) * 64:(h % 2) * 64 + 64, m * TCH:(m + 1) * TCH]

        def ktile_ap(h):
            m = 8 + h // 2
            return qkv_fm[(h % 2) * 64:(h % 2) * 64 + 64, m * TCH:(m + 1) * TCH]

        @block.sync
        def _(s):
            for j in range(NINT):
                if j >= 1:
                    s.wait_ge(sWc, j)
                k, jj = j // 2, j % 2
                s.dma_start(out=stg32[:, :],
                            in_=wf_in[k * 128:(k + 1) * 128, jj * 512:(jj + 1) * 512]
                            ).then_inc(sWI, 16)
            for i, (dst, off, src, w) in enumerate(floads):
                if i >= 8:
                    s.wait_ge(sWc, NINT + i - 7)
                slot = (i % 8) * 512
                s.dma_start(out=xstage[:, slot:slot + w],
                            in_=src).then_inc(sWS[i % 8], 16)
            s.dma_start(out=wqb_sb[:, :], in_=wqb_in).then_inc(sW3, 16)
            s.dma_start(out=ob_sb[:, :], in_=ob_in).then_inc(sW3, 16)
            s.dma_start(out=fb_sb[:, :], in_=fb_in).then_inc(sW3, 16)
            s.dma_start(out=ws_sb[:, :], in_=ws_in).then_inc(sW3, 16)
            s.dma_start(out=scol[:, :], in_=lam_in).then_inc(sW3, 16)
            s.dma_start(out=gam_sb[:, :], in_=gam_in).then_inc(sW3, 16)
            s.dma_start(out=kscale_sb[:, :], in_=kscale_in).then_inc(sW3, 16)
            for c in range(NC):
                if "xn" in dbg:
                    s.wait_ge(sLN, c + 1)
                    s.dma_start(out=dbg["xn"][c * TCH:(c + 1) * TCH, :]
                                .rearrange("(n p) d -> p n d", p=128), in_=tmb[:, :]).then_inc(sDBG, 16)
                for h in range(H):
                    gh = 16 * c + h
                    s.wait_ge(sQe, 24 * c + 16 + h // 2 + 1)
                    if gh >= 2:
                        s.wait_ge(sU, 16 * (gh - 1))
                        s.wait_ge(sZ, 16 * (gh - 1))
                    slot = (h % 2) * TCH
                    dsem = sDUP if h % 2 == 0 else sDUPb
                    s.dma_start(out=kdup[0:64, slot:slot + TCH], in_=ktile_ap(h)).then_inc(dsem, 16)
                    s.dma_start(out=kdup[64:128, slot:slot + TCH], in_=ktile_ap(h)).then_inc(dsem, 16)
                    s.dma_start(out=qdup[0:64, slot:slot + TCH], in_=qtile_ap(h)).then_inc(dsem, 16)
                    s.dma_start(out=qdup[64:128, slot:slot + TCH], in_=qtile_ap(h)).then_inc(dsem, 16)
                if "qkv" in dbg:
                    s.wait_ge(sQe, 24 * (c + 1))
                    s.dma_start(out=dbg["qkv"][:, c * TCH:(c + 1) * TCH]
                                .rearrange("(m p) t -> p m t", p=128), in_=qkv_fm[:, :]).then_inc(sDBG, 16)
                if "y" in dbg and c >= n_warm:
                    s.wait_ge(sYE, 8 * (c - n_warm + 1))
                    s.dma_start(out=dbg["y"][:, c * TCH:(c + 1) * TCH]
                                .rearrange("(m p) t -> p m t", p=128), in_=ybuf[:, :]).then_inc(sDBG, 16)
                if "xq" in dbg and c >= n_warm:
                    s.wait_ge(sTPe, TPB(c) + 16)
                    s.dma_start(out=dbg["xq"][:, c * TCH:(c + 1) * TCH]
                                .rearrange("(m p) t -> p m t", p=128), in_=ffm[:, :]).then_inc(sDBG, 16)
                if "s" in dbg and c >= n_warm:
                    s.wait_ge(sSG, 8 * (c - n_warm + 1))
                    s.dma_start(out=dbg["s"][:, c * TCH:(c + 1) * TCH]
                                .rearrange("(m p) t -> p m t", p=128), in_=sgbuf[:, :]).then_inc(sDBG, 16)
                if c >= n_warm:
                    co = c - n_warm
                    for tt in range(NT):
                        s.wait_ge(sRES, 4 * co + tt + 1)
                        osem = sOUT if tt % 2 == 0 else sOUTb
                        s.dma_start(out=out_dram[co * TCH + tt * 128: co * TCH + (tt + 1) * 128, :],
                                    in_=outstage[:, (tt % 2) * D:(tt % 2 + 1) * D]).then_inc(osem, 16)
            s.wait_ge(sOUT, 16 * (NC - n_warm) * 2)
            s.wait_ge(sOUTb, 16 * (NC - n_warm) * 2)

        @block.vector
        def _(v):
            v.memset(cols[:, :], 0.0)
            v.memset(eps_sb[:, 0:1], EPS_LN)
            v.memset(eps_sb[:, 1:2], EPS_RMS)
            for j in range(NINT):
                v.wait_ge(sWI, 16 * (j + 1))
                k, jj = j // 2, j % 2
                v.tensor_copy(wf_sb[:, k * D + jj * 512:k * D + (jj + 1) * 512],
                              stg32[:, :]).then_inc(sWc, 1)
            nseal = 0
            for i, (dst, off, src, w) in enumerate(floads):
                v.wait_ge(sWS[i % 8], 16 * (i // 8 + 1))
                slot = (i % 8) * 512
                ins = v.tensor_copy(dst[:, off:off + w], xstage[:, slot:slot + w])
                if w == 128:
                    nseal += 1
                    ins.then_inc(sGD, 1)
                    v.wait_ge(sGD, nseal)
                    v.engine_nop().then_inc(sWc, 1)
                else:
                    ins.then_inc(sWc, 1)
            v.wait_ge(sLAM, 1)
            v.memset(scol[:, :], 0.0)
            v.wait_ge(sW3, 16 * 7)

            GDBASE = [0]
            GDTOT = [nseal]
            for c in range(NC):
                GDBASE[0] = GDTOT[0]
                pass
                if dbg and c >= 1:
                    v.wait_ge(sDBG, 16 * len(dbg) * c)
                gd = [0]
                def seal(ins):
                    gd[0] += 1
                    ins.then_inc(sGD, 1)
                    v.wait_ge(sGD, GDBASE[0] + gd[0])
                def seal_sig(ins, sigsem, inc=1):
                    seal(ins)
                    v.engine_nop().then_inc(sigsem, inc)
                v.wait_ge(sOUT, 16 * 2 * max(c - n_warm, 0))
                v.wait_ge(sOUTb, 16 * 2 * max(c - n_warm, 0))
                for tt in range(NT):
                    xs = xstage[:, tt * D:(tt + 1) * D]
                    v.wait_ge(sXT[tt], 16 * (c + 1))
                    seal(v.reduce_sum(cols[:, 0 + tt:1 + tt], xs, AX.X))
                seal(v.tensor_scalar(cols[:, 0:NT], cols[:, 0:NT], 1.0 / D, None, ALU.mult))
                for tt in range(NT):
                    xs = xstage[:, tt * D:(tt + 1) * D]
                    xslot = xns[:, (tt % 2) * D:(tt % 2 + 1) * D]
                    oslot = outstage[:, (tt % 2) * D:(tt % 2 + 1) * D]
                    v.tensor_scalar(xslot, xs, cols[:, 0 + tt:1 + tt], None, ALU.subtract)
                    seal(v.tensor_tensor(oslot, xslot, xslot, ALU.mult))
                    seal(v.reduce_sum(cols[:, 4 + tt:5 + tt], oslot, AX.X))
                seal_sig(v.tensor_scalar(cols[:, 4:4 + NT], cols[:, 4:4 + NT], 1.0 / D, None, ALU.mult), sST)
                v.wait_ge(sSQ, 2 * c + 1)
                seal(v.reciprocal(cols[:, 12:12 + NT], cols[:, 8:8 + NT]))
                for tt in range(NT):
                    xs = xstage[:, tt * D:(tt + 1) * D]
                    xslot = xns[:, (tt % 2) * D:(tt % 2 + 1) * D]
                    v.tensor_scalar(xslot, xs, cols[:, 0 + tt:1 + tt],
                                    cols[:, 12 + tt:13 + tt], ALU.subtract, ALU.mult)
                    ins = v.tensor_copy(tmb[:, tt * D:(tt + 1) * D], xslot)
                    if tt == NT - 1:
                        seal_sig(ins, sLN)
                    v.reduce_max(cols[:, 16 + tt:17 + tt], xslot, AX.X,
                                 apply_absolute_value=True)
                seal_sig(v.tensor_scalar(stats[:, 0:8], wqb_sb[:, 8:16],
                                         kscale_sb[:, c:c + 1], None, ALU.mult), sQB)
                seal(v.tensor_tensor(cols[:, 20:20 + NT], cols[:, 12:12 + NT],
                                     cols[:, 12:12 + NT], ALU.mult))
                seal_sig(v.tensor_tensor(cols[:, 20:20 + NT], cols[:, 20:20 + NT],
                                         cols[:, 4:4 + NT], ALU.mult), sST)
                v.wait_ge(sSQ, 2 * c + 2)
                seal(v.reciprocal(cols[:, 28:28 + NT], cols[:, 8:8 + NT]))
                seal(v.tensor_tensor(cols[:, 16:16 + NT], cols[:, 16:16 + NT],
                                     cols[:, 28:28 + NT], ALU.mult))
                seal(v.tensor_scalar(cols[:, 16:16 + NT], cols[:, 16:16 + NT], EPS_ACT, None, ALU.max))
                seal(v.reciprocal(cols[:, 24:24 + NT], cols[:, 16:16 + NT]))
                seal(v.tensor_scalar(cols[:, 24:24 + NT], cols[:, 24:24 + NT], 127.0, None, ALU.mult))
                seal(v.tensor_tensor(cols[:, 28:28 + NT], cols[:, 28:28 + NT],
                                     cols[:, 12:12 + NT], ALU.mult))
                seal(v.tensor_tensor(cols[:, 28:28 + NT], cols[:, 28:28 + NT],
                                     cols[:, 24:24 + NT], ALU.mult))
                seal(v.tensor_scalar(cols[:, 16:16 + NT], cols[:, 16:16 + NT], ws_sb[:, 0:1], None, ALU.mult))
                seal(v.tensor_scalar(cols[:, 16:16 + NT], cols[:, 16:16 + NT], 1.0 / 127.0, None, ALU.mult))
                if "xn" in dbg:
                    v.wait_ge(sDBG, 16 * (len(dbg) * c + 1))
                if c >= n_warm:
                    v.wait_ge(sTPm, TPB(c) + 8)
                    for tt in range(NT):
                        xs = xstage[:, tt * D:(tt + 1) * D]
                        xslot = xns[:, (tt % 2) * D:(tt % 2 + 1) * D]
                        oslot = outstage[:, (tt % 2) * D:(tt % 2 + 1) * D]
                        v.tensor_scalar(xslot, xs, cols[:, 0 + tt:1 + tt],
                                        cols[:, 28 + tt:29 + tt], ALU.subtract, ALU.mult)
                        v.tensor_scalar(xslot, xslot, 128.5, None, ALU.add)
                        for kb in range(7, -1, -1):
                            v.tensor_scalar(oslot, xslot, float(1 << kb), float(1 << kb),
                                            ALU.is_ge, ALU.mult)
                            v.tensor_tensor(xslot, xslot, oslot, ALU.subtract)
                        v.tensor_scalar(oslot, xs, cols[:, 0 + tt:1 + tt],
                                        cols[:, 28 + tt:29 + tt], ALU.subtract, ALU.mult)
                        v.tensor_tensor(xslot, oslot, xslot, ALU.subtract)
                        v.tensor_scalar(xslot, xslot, 0.5, -128.0, ALU.add, ALU.max)
                        v.tensor_scalar(xslot, xslot, 127.0, cols[:, 16 + tt:17 + tt],
                                        ALU.min, ALU.mult)
                        ins = v.tensor_copy(tmb[:, tt * D:(tt + 1) * D], xslot)
                        if tt == NT - 1:
                            seal_sig(ins, sQF)
                for rr in range(NRR):
                    h = rr // 4
                    qq = rr % 4
                    qsl = qdup[:, (h % 2) * TCH:(h % 2 + 1) * TCH]
                    ksl = kdup[:, (h % 2) * TCH:(h % 2 + 1) * TCH]
                    for g in range(NGR):
                        GG = 256 * c + 4 * rr + g
                        v.wait_ge(sGV, GG + 1)
                        if g == 0 and rr % 4 == 0:
                            v.wait_ge(sDUP if h % 2 == 0 else sDUPb, 64 * (8 * c + h // 2 + 1))
                        for t2 in range(2):
                            k = g * 2 + t2
                            sl = slice(k * TCH, (k + 1) * TCH)
                            ins = v.tensor_tensor(uz[:, sl], gv[:, sl], ksl, ALU.mult)
                            if t2 == 1:
                                ins.then_inc(sU, 1)
                        for t2 in range(2):
                            k = g * 2 + t2
                            m = qq * 8 + k
                            ct = h * 32 + m
                            sl = slice(k * TCH, (k + 1) * TCH)
                            ins = v.tensor_tensor_scan(gv[:, sl],
                                                       rr_sb[:, ct:ct + 1].broadcast_to([128, TCH]),
                                                       uz[:, sl], scol[:, ct:ct + 1],
                                                       ALU.mult, ALU.add)
                            if c < n_warm and t2 == 1:
                                ins.then_inc(sZ, 1)
                        if c >= n_warm:
                            GGr = 256 * (c - n_warm) + 4 * rr + g
                            if GGr >= 4:
                                v.wait_ge(sRD, GGr - 3)
                            for t2 in range(2):
                                k = g * 2 + t2
                                sl = slice(k * TCH, (k + 1) * TCH)
                                ins = v.tensor_tensor(uz[:, sl], gv[:, sl], qsl, ALU.mult)
                                if t2 == 1:
                                    ins.then_inc(sZ, 1)
                    if c >= n_warm and rr >= 9 and (rr - 9) % 8 == 0:
                        p = (rr - 9) // 8
                        v.wait_ge(sRD, 256 * (c - n_warm) + 32 * (p + 1))
                        seal_sig(v.tensor_copy(ybuf[:, p * TCH:(p + 1) * TCH], py[p % 2][:, :]), sYE)
                if c >= n_warm:
                    co = c - n_warm
                    p = 7
                    v.wait_ge(sRD, 256 * co + 32 * (p + 1))
                    seal_sig(v.tensor_copy(ybuf[:, p * TCH:(p + 1) * TCH], py[p % 2][:, :]), sYE)
                    for m in range(8):
                        v.wait_ge(sOM, 8 * co + m + 1)
                        v.wait_ge(sSIL, 8 * co + m + 1)
                        seal_sig(v.scalar_tensor_tensor(sgbuf[:, m * TCH:(m + 1) * TCH], py[m % 2][:, :],
                                                        ob_sb[:, m:m + 1], gbuf[:, (m % 2) * TCH:(m % 2 + 1) * TCH],
                                                        ALU.add, ALU.mult), sSG)
                    for tt in range(NT):
                        v.wait_ge(sSGT, 4 * co + tt + 1)
                        if 2 * co + tt // 2 >= 1:
                            v.wait_ge(sOUT if tt % 2 == 0 else sOUTb, 16 * (2 * co + tt // 2))
                        oslot = outstage[:, (tt % 2) * D:(tt % 2 + 1) * D]
                        v.tensor_tensor(oslot[:, 0:512], pt0[:, 0:512],
                                        xstage[:, tt * D:tt * D + 512], ALU.add)
                        seal_sig(v.tensor_tensor(oslot[:, 512:1024], pt1[:, 0:512],
                                                 xstage[:, tt * D + 512:(tt + 1) * D], ALU.add), sRES)
                GDTOT[0] = GDBASE[0] + gd[0]

        @block.scalar
        def _(sc):
            gd2 = [0]
            def seal_sig2(ins, sigsem):
                gd2[0] += 1
                ins.then_inc(sGD2, 1)
                sc.wait_ge(sGD2, gd2[0])
                sc.mul(scr_sb[0:1, 0:1], scr_sb[0:1, 0:1], 1.0).then_inc(sigsem, 1)
            sc.wait_ge(sW3, 16 * 7)
            sc.copy(scr_sb[0:1, 0:1], eps_sb[0:1, 0:1])
            sc.activation(rr_sb[:, :], scol[:, :], AF.Sigmoid, scale=-1.0).then_inc(sLAM, 1)
            for c in range(NC):
                sc.wait_ge(sST, 2 * c + 1)
                seal_sig2(sc.activation(cols[:, 8:8 + NT], cols[:, 4:4 + NT], AF.Sqrt,
                                        bias=eps_sb[:, 0:1]), sSQ)
                sc.wait_ge(sST, 2 * c + 2)
                seal_sig2(sc.activation(cols[:, 8:8 + NT], cols[:, 20:20 + NT], AF.Sqrt,
                                        bias=eps_sb[:, 1:2]), sSQ)
                for g8 in range(KD):
                    sc.wait_ge(sTPm, TPB(c) + g8 + 1)
                    ins = sc.activation(ffm[:, g8 * TCH:(g8 + 1) * TCH],
                                        pt[g8 % 2][:, 0:512], AF.Copy)
                    if g8 == KD - 1:
                        seal_sig2(ins, sTPe)
                    else:
                        ins.then_inc(sTPe, 1)
                if c >= 1:
                    sc.wait_ge(sDUP, 512 * c)
                    sc.wait_ge(sDUPb, 512 * c)
                sc.wait_ge(sQB, c + 1)
                for m in range(24):
                    sc.wait_ge(sQm, 24 * c + m + 1)
                    bank = py[m % 2]
                    if 8 <= m < 16:
                        ins = sc.activation(qkv_fm[:, m * TCH:(m + 1) * TCH], bank[:, :],
                                            AF.Identity, scale=kscale_sb[:, c:c + 1],
                                            bias=stats[:, m - 8:m - 7])
                    else:
                        ins = sc.activation(qkv_fm[:, m * TCH:(m + 1) * TCH], bank[:, :],
                                            AF.Identity, bias=wqb_sb[:, m:m + 1])
                    if m >= 16:
                        seal_sig2(ins, sQe)
                    else:
                        ins.then_inc(sQe, 1)
                def emit_xq_evacs():
                    for g8 in range(KD):
                        sc.wait_ge(sTPm, TPB(c) + 8 + g8 + 1)
                        ins = sc.activation(ffm[:, g8 * TCH:(g8 + 1) * TCH],
                                            pt[g8 % 2][:, 0:512], AF.Copy)
                        if g8 == KD - 1:
                            seal_sig2(ins, sTPe)
                        else:
                            ins.then_inc(sTPe, 1)
                for rr in range(NRR):
                    if rr == 2 and c >= n_warm:
                        emit_xq_evacs()
                    h = rr // 4
                    qq = rr % 4
                    for g in range(NGR):
                        GG = 256 * c + 4 * rr + g
                        R = c * NRR + rr
                        sc.wait_ge(sVR, GG + 1)
                        if GG >= 4:
                            sc.wait_ge(sZ, GG - 3)
                            rp = (R - 1) % NRR
                            ct0 = (rp // 4) * 32 + (rp % 4) * 8 + 2 * g
                            k0 = 2 * g
                            sc.activation(scol[:, ct0:ct0 + 2],
                                          gv[:, k0 * TCH + TCH - 1:(k0 + 2) * TCH:TCH],
                                          AF.Copy)
                        for t2 in range(2):
                            k = g * 2 + t2
                            m = qq * 8 + k
                            ct = h * 32 + m
                            bank = pv[(g % 2) * 2 + t2]
                            ins = sc.activation(gv[:, k * TCH:(k + 1) * TCH], bank[:, :],
                                                AF.Copy, scale=gam_sb[:, ct:ct + 1])
                            if t2 == 1:
                                ins.then_inc(sGV, 1)
                if c >= n_warm:
                    co = c - n_warm
                    for m in range(8):
                        sc.wait_ge(sFM, 8 * co + m + 1)
                        if 8 * co + m >= 2:
                            sc.wait_ge(sSG, 8 * co + m - 1)
                        seal_sig2(sc.activation(gbuf[:, (m % 2) * TCH:(m % 2 + 1) * TCH], py[m % 2][:, :],
                                                AF.Silu, bias=fb_sb[:, m:m + 1]), sSIL)


        @block.gpsimd
        def _(g_):
            for c in range(NC):
                for tt in range(NT):
                    if c == 0 and tt == 0:
                        g_.wait_ge(sWc, NCONV)
                    if 1 <= c <= n_warm:
                        if tt == 0:
                            g_.wait_ge(sLN, c)
                    elif c >= 1:
                        g_.wait_ge(sRES, 4 * (c - 1 - n_warm) + tt + 1)
                    g_.dma_start(out=xstage[:, tt * D:(tt + 1) * D],
                                 in_=x_in[c * TCH + tt * 128: c * TCH + (tt + 1) * 128, :]
                                 ).then_inc(sXT[tt], 16)

        @block.tensor
        def _(t):
            t.wait_ge(sWc, NCONV)
            for c in range(NC):
                t.wait_ge(sLN, c + 1)
                if c > n_warm:
                    t.wait_ge(sRES, 4 * (c - n_warm))
                for g8 in range(KD):
                    if g8 >= 2:
                        t.wait_ge(sTPe, TPB(c) + g8 - 1)
                    bank = pt[g8 % 2]
                    for tt in range(NT):
                        ins = t.transpose(bank[:, tt * 128:(tt + 1) * 128],
                                          tmb[:, tt * D + g8 * 128: tt * D + (g8 + 1) * 128],
                                          idn_sb[:, :])
                        if tt == NT - 1:
                            ins.then_inc(sTPm, 1)
                t.wait_ge(sTPe, TPB(c) + 8)
                for m in range(24):
                    if m >= 2:
                        t.wait_ge(sQe, 24 * c + m - 1)
                    bank = py[m % 2]
                    for k in range(KD):
                        ins = t.matmul(bank[:, :], wq_sb[:, k * DQ + m * 128: k * DQ + (m + 1) * 128],
                                       ffm[:, k * TCH:(k + 1) * TCH],
                                       start=(k == 0), stop=(k == KD - 1))
                        if k == KD - 1:
                            ins.then_inc(sQm, 1)
                def emit_xqT():
                    t.wait_ge(sQF, c - n_warm + 1)
                    for g8 in range(KD):
                        if g8 >= 2:
                            t.wait_ge(sTPe, TPB(c) + 8 + g8 - 1)
                        bank = pt[g8 % 2]
                        for tt in range(NT):
                            ins = t.transpose(bank[:, tt * 128:(tt + 1) * 128],
                                              tmb[:, tt * D + g8 * 128: tt * D + (g8 + 1) * 128],
                                              idn_sb[:, :])
                            if tt == NT - 1:
                                ins.then_inc(sTPm, 1)

                def emit_vrep(rr):
                    h = rr // 4
                    qq = rr % 4
                    if qq == 0:
                        t.wait_ge(sQe, 24 * c + 16 + h // 2 + 1)
                    for g in range(NGR):
                        GG = 256 * c + 4 * rr + g
                        if GG >= 2:
                            t.wait_ge(sGV, GG - 1)
                        for t2 in range(2):
                            k = g * 2 + t2
                            m = qq * 8 + k
                            bank = pv[(g % 2) * 2 + t2]
                            eb = (h % 2) * 64
                            ins = t.matmul(bank[:, :], esel_sb[eb:eb + 64, m * 128:(m + 1) * 128],
                                           vtile_ap(h), start=True, stop=True)
                            if t2 == 1:
                                ins.then_inc(sVR, 1)

                def emit_reduce(rr):
                    h = rr // 4
                    qq = rr % 4
                    p = h // 2
                    rowbase = (h % 2) * 64
                    bank = py[p % 2]
                    if qq == 0:
                        gp = 8 * (c - n_warm) + p
                        if h % 2 == 0 and gp >= 2:
                            t.wait_ge(sYE, gp - 1)
                        if h == 0:
                            t.wait_ge(sQe, 24 * c + 23)
                        elif h == 2:
                            t.wait_ge(sQe, 24 * c + 24)
                        wb = (h % 2) * 64
                        t.matmul(bank[rowbase:rowbase + 64, :],
                                 wst_sb[wb:wb + 64, h * 64:(h + 1) * 64], qtile_ap(h),
                                 start=True, stop=False, skip_group_check=True)
                    for g in range(NGR):
                        GG = 256 * c + 4 * rr + g
                        t.wait_ge(sZ, GG + 1)
                        for t2 in range(2):
                            k = g * 2 + t2
                            m = qq * 8 + k
                            last = (qq == 3 and k == 7)
                            ins = t.matmul(bank[rowbase:rowbase + 64, :],
                                           dsel_sb[:, 64 - 2 * m:128 - 2 * m],
                                           uz[:, k * TCH:(k + 1) * TCH],
                                           start=False, stop=last, skip_group_check=True)
                            if t2 == 1:
                                ins.then_inc(sRD, 1)

                emit_vrep(0)
                for rr in range(1, NRR):
                    if rr == 2 and c >= n_warm:
                        emit_xqT()
                    emit_vrep(rr)
                    if c >= n_warm:
                        emit_reduce(rr - 1)
                if c < n_warm:
                    continue
                co = c - n_warm
                emit_reduce(NRR - 1)
                t.wait_ge(sTPe, TPB(c) + 16)
                for pair in range(4):
                    for m in (2 * pair, 2 * pair + 1):
                        if pair == 0:
                            t.wait_ge(sYE, 8 * co + 7 + m)
                        if m >= 2:
                            t.wait_ge(sSG, 8 * co + m - 1)
                        bank = py[m % 2]
                        for k in range(KD):
                            ins = t.matmul(bank[:, :], wf_sb[:, k * D + m * 128: k * D + (m + 1) * 128],
                                           ffm[:, k * TCH:(k + 1) * TCH],
                                           start=(k == 0), stop=(k == KD - 1))
                            if k == KD - 1:
                                ins.then_inc(sFM, 1)
                    if pair == 0:
                        t.wait_ge(sYE, 8 * (co + 1))
                    for m in (2 * pair, 2 * pair + 1):
                        t.wait_ge(sSIL, 8 * co + m + 1)
                        bank = py[m % 2]
                        for k in range(KD):
                            ins = t.matmul(bank[:, :], wo_sb[:, k * D + m * 128: k * D + (m + 1) * 128],
                                           ybuf[:, k * TCH:(k + 1) * TCH],
                                           start=(k == 0), stop=(k == KD - 1))
                            if k == KD - 1:
                                ins.then_inc(sOM, 1)
                t.wait_ge(sSG, 8 * (co + 1))
                for tt in range(NT):
                    if tt >= 1:
                        t.wait_ge(sRES, 4 * co + tt)
                    for o8 in range(KD):
                        bank = pt[o8 // 4]
                        ins = t.transpose(bank[:, (o8 % 4) * 128:(o8 % 4 + 1) * 128],
                                          sgbuf[:, o8 * TCH + tt * 128: o8 * TCH + (tt + 1) * 128],
                                          idn_sb[:, :])
                        if o8 == KD - 1:
                            ins.then_inc(sSGT, 1)

    return nc


def host_weights(inputs):
    W = {}
    W["wq"] = np.ascontiguousarray(np.asarray(inputs["L0_attn_Wqkv_w"], np.float32).T)
    W["wqb"] = np.ascontiguousarray(np.asarray(inputs["L0_attn_Wqkv_b"], np.float32).reshape(24, 128).T)
    W["wo"] = np.ascontiguousarray(np.asarray(inputs["L0_attn_out_w"], np.float32).T)
    W["ob"] = np.ascontiguousarray(np.asarray(inputs["L0_attn_out_b"], np.float32).reshape(8, 128).T)
    W["wf"] = np.ascontiguousarray(np.asarray(inputs["L0_ffn_w_int8"], np.int32).T)
    W["fb"] = np.ascontiguousarray(np.asarray(inputs["L0_ffn_bias"], np.float32).reshape(8, 128).T)
    W["ws"] = np.ascontiguousarray(np.broadcast_to(
        np.asarray(inputs["L0_ffn_weight_scale"], np.float32).reshape(1, 1), (128, 1)))
    lam = np.asarray(inputs["L0_attn_Lambda_raw"], np.float32)
    gam = np.asarray(inputs["L0_attn_Gamma"], np.float32)
    W["lam"] = np.ascontiguousarray(lam.reshape(H, 32, 2, 64).transpose(2, 3, 0, 1).reshape(128, 512))
    W["gam"] = np.ascontiguousarray(gam.reshape(H, 32, 2, 64).transpose(2, 3, 0, 1).reshape(128, 512))
    wst = np.asarray(inputs["L0_attn_W_static"], np.float32)
    wstj = wst.transpose(2, 0, 1).reshape(64, H * 64)
    W["wst"] = np.ascontiguousarray(np.concatenate([wstj, wstj], 0))
    esel = np.zeros((64, 32 * 128), np.float32)
    for m in range(32):
        esel[2 * m, m * 128:m * 128 + 64] = 1.0
        esel[2 * m + 1, m * 128 + 64:m * 128 + 128] = 1.0
    W["esel"] = np.concatenate([esel, esel], 0)
    dsel = np.zeros((128, 128), np.float32)
    for p2 in range(128):
        dsel[p2, 64 + p2 // 64] = 1.0
    W["dsel"] = dsel
    W["idn"] = np.eye(128, dtype=np.float32)
    return W


N_WARM = 2
N_CHUNKS = 10
_CACHE = {}


def _get_nc():
    if "nc" not in _CACHE:
        _CACHE["nc"] = build_program(N_CHUNKS, N_WARM)
    return _CACHE["nc"]


def kernel(L0_attn_x, L0_attn_Wqkv_w, L0_attn_Wqkv_b, L0_attn_out_w, L0_attn_out_b,
           L0_attn_W_static, L0_attn_Lambda_raw, L0_attn_Gamma, L0_ffn_w_int8,
           L0_ffn_weight_scale, L0_ffn_bias):
    from concourse.bass_utils import run_bass_kernel_spmd
    inputs = dict(L0_attn_Wqkv_w=L0_attn_Wqkv_w, L0_attn_Wqkv_b=L0_attn_Wqkv_b,
                  L0_attn_out_w=L0_attn_out_w, L0_attn_out_b=L0_attn_out_b,
                  L0_attn_W_static=L0_attn_W_static, L0_attn_Lambda_raw=L0_attn_Lambda_raw,
                  L0_attn_Gamma=L0_attn_Gamma, L0_ffn_w_int8=L0_ffn_w_int8,
                  L0_ffn_weight_scale=L0_ffn_weight_scale, L0_ffn_bias=L0_ffn_bias)
    W = host_weights(inputs)
    x = np.asarray(L0_attn_x, np.float32)
    B, L, _ = x.shape
    TQ = L // 4
    WARM = N_WARM * TCH
    in_maps = []
    for c in range(8):
        b, q = c // 4, c % 4
        ks = np.full((128, N_CHUNKS), 0.125, np.float32)
        if q == 0:
            warm = x[b, 0:WARM]
            ks[:, :N_WARM] = 0.0
        else:
            warm = x[b, q * TQ - WARM: q * TQ]
        xc = np.ascontiguousarray(np.concatenate([warm, x[b, q * TQ:(q + 1) * TQ]], 0))
        in_maps.append(dict(x=xc, kscale=ks, **W))
    nc = _get_nc()
    res = run_bass_kernel_spmd(nc, in_maps, core_ids=list(range(8)))
    parts = [res.results[c]["out"] for c in range(8)]
    out = np.empty((B, L, D), np.float32)
    for c in range(8):
        b, q = c // 4, c % 4
        out[b, q * TQ:(q + 1) * TQ] = np.asarray(parts[c], np.float32)
    return out
